# version 09
# baseline (speedup 1.0000x reference)
import numpy as np
import ml_dtypes
import concourse.bass as bass
import concourse.mybir as mybir
from concourse.bass_utils import run_bass_kernel_spmd

F32 = mybir.dt.float32
BF16 = mybir.dt.bfloat16
AF = mybir.ActivationFunctionType
ALU = mybir.AluOpType
AX = mybir.AxisListType.X

NCORES = 8
B_, S_, C = 4, 8192, 1024
T_FULL = S_ // 2
N = 512
CC = C // 128
JC = 32
DI = 2048
JI = DI // 128
DS = 64


def build_nc(T=T_FULL, stage=3, dbg=False):
    NG = T // N
    nc = bass.Bass("TRN2", target_bir_lowering=False,
                   detect_race_conditions=False)

    d_xT = nc.dram_tensor("xT", [C, T], F32, kind="ExternalInput")
    d_win = nc.dram_tensor("win", [128, JC * CC * 128], F32, kind="ExternalInput")
    d_wdt = nc.dram_tensor("wdt", [128, JI * CC * 128], F32, kind="ExternalInput")
    d_wxp = nc.dram_tensor("wxp", [128, JI * DS], F32, kind="ExternalInput")
    d_wout = nc.dram_tensor("wout", [128, JI * CC * 128], F32, kind="ExternalInput")
    d_wr = nc.dram_tensor("wr", [128, CC * CC * 128], F32, kind="ExternalInput")
    d_wk = nc.dram_tensor("wk", [128, CC * CC * 128], F32, kind="ExternalInput")
    d_wv = nc.dram_tensor("wv", [128, CC * CC * 128], F32, kind="ExternalInput")
    d_bin = nc.dram_tensor("bin", [128, JC], F32, kind="ExternalInput")
    d_bdt = nc.dram_tensor("bdt", [128, CC], F32, kind="ExternalInput")
    d_bout = nc.dram_tensor("bout", [128, CC], F32, kind="ExternalInput")
    d_bxp = nc.dram_tensor("bxp", [DS, 1], F32, kind="ExternalInput")
    d_alog = nc.dram_tensor("alog", [DS, 1], F32, kind="ExternalInput")
    d_dvec = nc.dram_tensor("dvec", [128, CC], F32, kind="ExternalInput")
    d_mur = nc.dram_tensor("mur", [128, CC], F32, kind="ExternalInput")
    d_muk = nc.dram_tensor("muk", [128, CC], F32, kind="ExternalInput")
    d_xbT = nc.dram_tensor("xbT", [128, CC], F32, kind="ExternalInput")
    d_hmask = nc.dram_tensor("hmask", [128, 1], F32, kind="ExternalInput")
    d_rmask = nc.dram_tensor("rmask", [128, NCORES * 9], F32, kind="ExternalInput")
    d_out = nc.dram_tensor("outT", [C, T], F32, kind="ExternalOutput")
    if dbg:
        d_dbg1 = nc.dram_tensor("dbg1", [128, CC * N], F32, kind="ExternalOutput")
        d_dbg2 = nc.dram_tensor("dbg2", [128, CC * N], F32, kind="ExternalOutput")
        d_dbg3 = nc.dram_tensor("dbg3", [128, CC * N], F32, kind="ExternalOutput")
        d_dbg4 = nc.dram_tensor("dbg4", [128, CC * N], F32, kind="ExternalOutput")
    d_spill = nc.dram_tensor("spill", [C, T], BF16, kind="Internal")
    d_ccin = nc.dram_tensor("cc_in", [128, 9], F32, kind="Internal")
    d_ccout = nc.dram_tensor("cc_out", [NCORES * 128, 9], F32, kind="Internal")

    BASE = ((nc.sbuf_base + 31) // 32) * 32
    TOP = nc.sbuf_top
    cur = [BASE]

    def sb(name, shape, dtype, at=None):
        isz = 2 if dtype == BF16 else 4
        nb = int(np.prod(shape[1:])) * isz
        nb = ((nb + 31) // 32) * 32
        if at is None:
            off = cur[0]
            cur[0] += nb
        else:
            off = BASE_W0[0] + at
        assert off + nb <= TOP, (name, off, nb, TOP)
        return nc.alloc_sbuf_tensor_at(name, list(shape), dtype, offset=off)

    BASE_W0 = [0]
    s_ssum = sb("ssum", [1, T], BF16)
    s_G = sb("G", [DS, T], BF16)
    s_s0 = sb("s0", [DS, N], F32)
    s_s1 = sb("s1", [DS, N], F32)
    s_a = sb("a", [DS, N], F32)
    s_bx = sb("bx", [DS, N], F32)
    s_sres0 = sb("sres0", [128, N], BF16)
    s_sres1 = sb("sres1", [128, N], BF16)
    s_dsb = sb("dsb", [1, N], F32)
    s_bin = sb("v_bin", [128, JC], F32)
    s_bdt = sb("v_bdt", [128, CC], F32)
    s_bout = sb("v_bout", [128, CC], F32)
    s_bxp = sb("v_bxp", [DS, 1], F32)
    s_alog = sb("v_alog", [DS, 1], F32)
    s_A = sb("v_A", [DS, 1], F32)
    s_dp1 = sb("v_dp1", [128, CC], F32)
    s_mur = sb("v_mur", [128, CC], F32)
    s_muk = sb("v_muk", [128, CC], F32)
    s_xbT = sb("v_xbT", [128, CC], F32)
    s_hmask = sb("v_hmask", [128, 1], F32)
    s_ones1 = sb("ones1", [1, 128], F32)
    s_ones1b = sb("ones1b", [1, 128], BF16)
    s_ones1_64 = sb("ones1_64", [1, DS], F32)
    s_ones64c = sb("ones64c", [DS, 1], F32)
    s_onescb = sb("onescb", [128, 1], BF16)
    s_onescf = sb("onescf", [128, 1], F32)
    s_c1 = sb("c_one", [128, 1], F32)
    s_ceps = sb("c_eps", [1, 1], F32)
    s_cm05 = sb("c_m05", [1, 1], F32)
    s_junk = sb("junk", [128, 128], F32)
    s_junks = sb("junks", [128, 128], F32)
    s_pay = sb("pay", [128, 9], F32)
    s_gath = sb("gath", [128, NCORES * 9], F32)
    s_rmask = sb("rmaskT", [128, NCORES * 9], F32)
    s_sel = sb("sel", [128, 9], F32)
    s_hinb = sb("hinb", [DS, 1], BF16)
    s_ub = sb("ub", [128, CC], F32)
    s_prevc = sb("prevc", [128, CC], F32)
    s_slastc = sb("slastc", [128, 1], F32)
    s_stt2 = sb("stt2", [1, 2], F32)
    BASE_W0[0] = ((cur[0] + 31) // 32) * 32

    s_win = sb("w_in", [128, JC * CC * 128], BF16, at=0)
    s_wdt = sb("w_dt", [128, JI * CC * 128], BF16, at=65536)
    s_wxp = sb("w_xp", [128, JI * DS], BF16, at=98304)
    s_wout = sb("w_out", [128, JI * CC * 128], BF16, at=100352)
    s_xa0 = sb("xa0", [128, CC * N], BF16, at=133120)
    s_xa1 = sb("xa1", [128, CC * N], BF16, at=141312)
    s_xin = sb("xin", [128, JI * N], BF16, at=149504)
    s_del0 = sb("del0", [128, N], BF16, at=165888)
    s_del1 = sb("del1", [128, N], BF16, at=166912)
    s_sp0 = sb("sp0", [128, 4 * N], BF16, at=167936)
    s_sp1 = sb("sp1", [128, 4 * N], BF16, at=172032)
    A_END = 176128
    s_wr = sb("w_r", [128, CC * CC * 128], BF16, at=0)
    s_wk = sb("w_k", [128, CC * CC * 128], BF16, at=16384)
    s_wv = sb("w_v", [128, CC * CC * 128], BF16, at=32768)
    s_xb0 = sb("xb0", [128, CC * N], F32, at=49152)
    s_xb1 = sb("xb1", [128, CC * N], F32, at=65536)
    s_mmb = sb("mmb", [128, CC * N], F32, at=81920)
    s_x1 = sb("x1", [128, CC * N], BF16, at=98304)
    s_u2 = sb("u2", [128, CC * N], BF16, at=106496)
    s_kact = sb("kact", [128, CC * N], BF16, at=114688)
    s_r = sb("r", [128, CC * N], BF16, at=122880)
    s_usq1 = sb("usq1", [128, CC * N], BF16, at=131072)
    s_xrf = sb("xrf", [128, CC * N], BF16, at=139264)
    s_xkf = sb("xkf", [128, CC * N], BF16, at=147456)
    s_ost0 = sb("ost0", [128, N], F32, at=155648)
    s_ost1 = sb("ost1", [128, N], F32, at=157696)
    s_stat = sb("stat", [1, 8 * N], F32, at=159744)
    s_gt0 = sb("gt0", [128, N], BF16, at=176128)
    s_gt1 = sb("gt1", [128, N], BF16, at=177184)
    B_END = 178240
    assert BASE_W0[0] + max(A_END, B_END) <= TOP, (BASE_W0[0], TOP)

    pb = [nc.alloc_psum_tensor(f"pb{i}", [128, N], F32) for i in range(8)]

    semnames = [
        "dw", "dxa", "dxb", "dmb", "dso", "dod", "dsm", "dcc",
        "tpa", "tpd", "tpx", "tpm", "tdb", "tss", "tpo", "tgx", "t1",
        "tcr", "tB", "tst", "dpay", "dcol", "dgat",
        "sea", "sdt", "ssa", "sso", "srs", "pfc",
        "vin", "vz", "vd", "vsc", "vss", "vbx", "vpay", "vcor", "vbnd",
        "vB", "vub", "vst", "vmx", "vmxr", "vmxk", "vod", "vdbg", "ddbg", "sbc",
        "vgt",
    ]
    sem = {n: nc.alloc_semaphore("sem_" + n) for n in semnames}
    rg = [list(range(NCORES))]

    TB_PS, TB_LN1S, TB_LN1BM, TB_LN1BR = {}, {}, {}, {}
    TB_WR, TB_WK, TB_WV = {}, {}, {}
    TB_LN2S, TB_LN2BM, TB_LN2BR = {}, {}, {}
    cnt = 2
    cnt += 1; TB_PS[0] = cnt
    cnt += 1; TB_LN1S[0] = cnt
    cnt += 1; TB_LN1BM[0] = cnt
    cnt += 1; TB_LN1BR[0] = cnt
    for g in range(NG):
        if g < NG - 1:
            cnt += 1; TB_PS[g + 1] = cnt
        TB_WR[g] = {}
        for dc in range(CC):
            cnt += 1; TB_WR[g][dc] = cnt
        if g >= 1:
            cnt += 1; TB_LN2S[g - 1] = cnt
            cnt += 1; TB_LN2BM[g - 1] = cnt
            cnt += 1; TB_LN2BR[g - 1] = cnt
        if g < NG - 1:
            cnt += 1; TB_LN1S[g + 1] = cnt
            cnt += 1; TB_LN1BM[g + 1] = cnt
            cnt += 1; TB_LN1BR[g + 1] = cnt
        TB_WK[g] = {}
        for dc in range(CC):
            cnt += 1; TB_WK[g][dc] = cnt
        TB_WV[g] = {}
        for cc in range(CC):
            cnt += 1; TB_WV[g][cc] = cnt
    cnt += 1; TB_LN2S[NG - 1] = cnt
    cnt += 1; TB_LN2BM[NG - 1] = cnt
    cnt += 1; TB_LN2BR[NG - 1] = cnt

    VB_N, VUB1, VUB2, VST1, VST2, VMXR, VMXK = {}, {}, {}, {}, {}, {}, {}
    vb = vub = vst = vmxr = vmxk = 0
    vb += 1; VB_N[0] = vb
    vub += 1; VUB1[0] = vub
    vst += 2; VST1[0] = vst
    vmxr += 1; VMXR[0] = vmxr
    vmxk += 1; VMXK[0] = vmxk
    for g in range(NG):
        if g < NG - 1:
            vb += 1; VB_N[g + 1] = vb
            vub += 1; VUB1[g + 1] = vub
            vst += 2; VST1[g + 1] = vst
            vmxr += 1; VMXR[g + 1] = vmxr
            vmxk += 1; VMXK[g + 1] = vmxk
        vub += 1; VUB2[g] = vub
        vst += 2; VST2[g] = vst
    SRS1, SRS2, SBC1, SBC2 = {}, {}, {}, {}
    srs = sbc = 0
    srs += 1; SRS1[0] = srs
    sbc += 2; SBC1[0] = sbc
    for g in range(NG):
        if g >= 1:
            srs += 1; SRS2[g - 1] = srs
            sbc += 2; SBC2[g - 1] = sbc
        if g < NG - 1:
            srs += 1; SRS1[g + 1] = srs
            sbc += 2; SBC1[g + 1] = sbc
    srs += 1; SRS2[NG - 1] = srs
    sbc += 2; SBC2[NG - 1] = sbc

    with nc.Block() as block:

        @block.gpsimd
        def _(g):
            def xa_load(gi):
                buf = s_xa0 if gi % 2 == 0 else s_xa1
                src = d_xT.ap().rearrange("(c p) t -> p c t", c=CC, p=128)
                src = src[:, :, gi * N:(gi + 1) * N]
                dst = buf[:].rearrange("p (c t) -> p c t", c=CC)
                g.dma_start(dst, src).then_inc(sem["dxa"], 16)

            def mmb_load(gi):
                src = d_spill.ap().rearrange("(c p) t -> p c t", c=CC, p=128)
                src = src[:, :, gi * N:(gi + 1) * N]
                dst = s_mmb[:].rearrange("p (c t) -> p c t", c=CC)
                g.dma_start(dst, src).then_inc(sem["dmb"], 16)

            xa_load(0)
            WQ = JC * CC * 128
            for q in range(4):
                sl = slice(q * WQ // 4, (q + 1) * WQ // 4)
                g.dma_start(s_win[:, sl], d_win[:, sl]).then_inc(sem["dw"], 16)
            if NG > 1:
                xa_load(1)
            WD = JI * CC * 128
            for q in range(2):
                sl = slice(q * WD // 2, (q + 1) * WD // 2)
                g.dma_start(s_wdt[:, sl], d_wdt[:, sl]).then_inc(sem["dw"], 16)
            g.dma_start(s_wxp[:], d_wxp[:]).then_inc(sem["dw"], 16)
            for q in range(2):
                sl = slice(q * WD // 2, (q + 1) * WD // 2)
                g.dma_start(s_wout[:, sl], d_wout[:, sl]).then_inc(sem["dw"], 16)
            for gi in range(2, NG):
                g.wait_ge(sem["tpa"], JC * (gi - 1))
                xa_load(gi)
            g.wait_ge(sem["tpa"], JC * NG)
            g.dma_start(s_wr[:], d_wr[:]).then_inc(sem["dw"], 16)
            g.dma_start(s_wk[:], d_wk[:]).then_inc(sem["dw"], 16)
            g.dma_start(s_wv[:], d_wv[:]).then_inc(sem["dw"], 16)
            if stage < 2:
                return
            if stage >= 3:
                g.wait_ge(sem["tpd"], CC * NG)
                g.wait_ge(sem["dso"], 16 * 2)
                mmb_load(0)
            g.wait_ge(sem["dpay"], 16)
            g.collective_compute(
                "AllGather", ALU.bypass, replica_groups=rg,
                ins=[d_ccin.ap().opt()], outs=[d_ccout.ap().opt()],
            ).then_inc(sem["dcol"])
            g.wait_ge(sem["dcol"], 1)
            src = d_ccout.ap().rearrange("(r p) c -> p r c", r=NCORES, p=128)
            dst = s_gath[:].rearrange("p (r c) -> p r c", r=NCORES)
            g.dma_start(dst, src).then_inc(sem["dgat"], 16)
            if stage < 3 and stage not in (4, 5, 7, 8, 9, 10):
                return
            for gi in range(1, NG):
                g.wait_ge(sem["dso"], 16 * 2 * (gi + 1))
                g.wait_ge(sem["vB"], gi)
                mmb_load(gi)
            if dbg:
                g.wait_ge(sem["vdbg"], 1)
                g.dma_start(d_dbg1[:], s_x1[:]).then_inc(sem["ddbg"], 16)
                g.wait_ge(sem["vdbg"], 3)
                g.dma_start(d_dbg3[:], s_r[:]).then_inc(sem["ddbg"], 16)
                g.dma_start(d_dbg4[:], s_kact[:]).then_inc(sem["ddbg"], 16)
                g.wait_ge(sem["vdbg"], 4)
                g.dma_start(d_dbg2[:], s_u2[:]).then_inc(sem["ddbg"], 16)

        @block.sync
        def _(sy):
            nsm = 0
            for dst, src in [(s_bin, d_bin), (s_bdt, d_bdt), (s_bout, d_bout),
                             (s_bxp, d_bxp), (s_alog, d_alog), (s_dp1, d_dvec),
                             (s_mur, d_mur), (s_muk, d_muk), (s_xbT, d_xbT),
                             (s_hmask, d_hmask), (s_rmask, d_rmask)]:
                sy.dma_start(dst[:], src[:]).then_inc(sem["dsm"], 16)
                nsm += 1
            for gi in range(NG):
                for h in range(2):
                    sy.wait_ge(sem["sso"], gi * CC + (h + 1) * 4)
                    buf = s_sp0 if h == 0 else s_sp1
                    src = buf[:].rearrange("p (c t) -> p c t", c=4)
                    dst = d_spill.ap().rearrange("(c p) t -> p c t", c=CC, p=128)
                    dst = dst[:, h * 4:(h + 1) * 4, gi * N:(gi + 1) * N]
                    sy.dma_start(dst, src).then_inc(sem["dso"], 16)

            def xb_load(gi):
                buf = s_xb0 if gi % 2 == 0 else s_xb1
                src = d_xT.ap().rearrange("(c p) t -> p c t", c=CC, p=128)
                src = src[:, :, gi * N:(gi + 1) * N]
                dst = buf[:].rearrange("p (c t) -> p c t", c=CC)
                sy.dma_start(dst, src).then_inc(sem["dxb"], 16)

            if stage >= 3:
                sy.wait_ge(sem["tpa"], JC * NG)
                xb_load(0)
                if NG > 1:
                    sy.wait_ge(sem["tpd"], CC * NG)
                    xb_load(1)
            if stage >= 2:
                sy.wait_ge(sem["vpay"], 1)
                sy.dma_start(d_ccin[:], s_pay[:]).then_inc(sem["dpay"], 16)
            if stage in (4, 5, 7, 8, 9, 10):
                sy.wait_ge(sem["vod"], NG)
                sy.dma_start(d_out[0:128, 0:128], s_junk[:]).then_inc(sem["dod"], 16)
                sy.wait_ge(sem["dod"], 16)
            elif stage >= 3:
                nod = 0
                for gi in range(NG):
                    if gi + 2 < NG:
                        sy.wait_ge(sem["vB"], gi + 1)
                        xb_load(gi + 2)
                    for cc in range(CC):
                        sy.wait_ge(sem["vod"], gi * CC + cc + 1)
                        buf = s_ost0 if cc % 2 == 0 else s_ost1
                        dst = d_out[cc * 128:(cc + 1) * 128, gi * N:(gi + 1) * N]
                        sy.dma_start(dst, buf[:]).then_inc(sem["dod"], 16)
                        nod += 1
                sy.wait_ge(sem["dod"], 16 * nod)
            else:
                if stage == 1:
                    sy.wait_ge(sem["vss"], NG)
                else:
                    sy.wait_ge(sem["vbnd"], 3)
                sy.wait_ge(sem["sso"], CC * NG)
                sy.dma_start(d_out[0:128, 0:128], s_junk[:]).then_inc(sem["dod"], 16)
                sy.wait_ge(sem["dod"], 16)
            sy.wait_ge(sem["dso"], 16 * 2 * NG)

        @block.tensor
        def _(t):
            def win_ap(cc, jc):
                o = (jc * CC + cc) * 128
                return s_win[:, o:o + 128]

            def wdt_ap(jc, dc):
                o = (jc * CC + dc) * 128
                return s_wdt[:, o:o + 128]

            def wout_ap(jc, dc):
                o = (jc * CC + dc) * 128
                return s_wout[:, o:o + 128]

            def wffn_ap(w, cc, dc):
                o = (cc * CC + dc) * 128
                return w[:, o:o + 128]

            t.wait_ge(sem["dsm"], 16 * 11)
            t.wait_ge(sem["vin"], 1)
            for gi in range(NG):
                xa = s_xa0 if gi % 2 == 0 else s_xa1
                t.wait_ge(sem["dxa"], 16 * (gi + 1))
                def win_jc(jc):
                    i = gi * JC + jc
                    if gi == 0:
                        t.wait_ge(sem["dw"], 16 * (jc // 8 + 1))
                    if i >= 2:
                        t.wait_ge(sem["sea"], i - 1)
                    for cc in range(CC):
                        mm = t.matmul(pb[jc % 2][:], win_ap(cc, jc),
                                      xa[:, cc * N:(cc + 1) * N],
                                      start=(cc == 0), stop=(cc == CC - 1))
                    mm.then_inc(sem["tpa"], 1)

                for jc in range(JI):
                    win_jc(jc)

                def mean_mm(dc):
                    t.wait_ge(sem["sdt"], gi * CC + dc + 1)
                    if dc == 0:
                        t.wait_ge(sem["vss"], gi)
                    dl = s_del0 if dc % 2 == 0 else s_del1
                    mm = t.matmul(pb[5][0:1, :], s_onescb[:], dl[:],
                                  start=(dc == 0), stop=(dc == CC - 1))
                    mm.then_inc(sem["tpm"], 1)

                if gi == 0:
                    t.wait_ge(sem["dw"], 16 * 6)
                t.wait_ge(sem["sea"], gi * JC + JI)
                for dc in range(CC):
                    j = gi * CC + dc
                    if j >= 2:
                        t.wait_ge(sem["sdt"], j - 1)
                    for jc in range(JI):
                        mm = t.matmul(pb[2 + dc % 2][:], wdt_ap(jc, dc),
                                      s_xin[:, jc * N:(jc + 1) * N],
                                      start=(jc == 0), stop=(jc == JI - 1))
                    mm.then_inc(sem["tpd"], 1)
                    if dc >= 2:
                        mean_mm(dc - 2)
                mean_mm(CC - 2)
                mean_mm(CC - 1)
                if gi == 0:
                    t.wait_ge(sem["dw"], 16 * 7)
                t.wait_ge(sem["vbx"], gi)
                for jc in range(JI):
                    mm = t.matmul(pb[4][0:DS, :], s_wxp[:, jc * DS:(jc + 1) * DS],
                                  s_xin[:, jc * N:(jc + 1) * N],
                                  start=(jc == 0), stop=(jc == JI - 1))
                mm.then_inc(sem["tpx"], 1)
                for jc in range(JI, JC):
                    win_jc(jc)
                t.wait_ge(sem["vd"], gi + 1)
                t.wait_ge(sem["ssa"], gi + 1)
                t.matmul(pb[5][0:DS, :], s_ones1_64[:], s_dsb[:],
                         start=True, stop=True).then_inc(sem["tdb"], 1)
                if gi == 0:
                    t.wait_ge(sem["dw"], 16 * 9)
                t.wait_ge(sem["vz"], (gi + 1) * JI)
                for dc in range(CC):
                    j = gi * CC + dc
                    if j >= 2:
                        t.wait_ge(sem["sso"], j - 1)
                    for jc in range(JI):
                        mm = t.matmul(pb[6 + dc % 2][:], wout_ap(jc, dc),
                                      s_xin[:, jc * N:(jc + 1) * N],
                                      start=(jc == 0), stop=(jc == JI - 1))
                    mm.then_inc(sem["tpo"], 1)
                t.wait_ge(sem["vsc"], gi + 1)
                t.wait_ge(sem["vss"], gi)
                sbf = s_s0 if gi % 2 == 0 else s_s1
                t.matmul(pb[5][0:1, :], s_ones64c[:], sbf[:],
                         start=True, stop=True).then_inc(sem["tss"], 1)

            if stage < 2:
                return
            t.wait_ge(sem["vss"], NG)
            t.matmul(pb[5][:, 0:1], s_ones1b[:], s_ssum[0:1, T - 1:T],
                     start=True, stop=True).then_inc(sem["tss"], 1)
            t.wait_ge(sem["vpay"], 2)
            for gi in range(NG):
                if gi >= 2:
                    t.wait_ge(sem["vcor"], gi - 1)
                mm = t.matmul(pb[6 + gi % 2][0:1, :], s_hinb[:],
                              s_G[:, gi * N:(gi + 1) * N], start=True, stop=True)
                mm.then_inc(sem["tcr"], 1)
            t.wait_ge(sem["vbnd"], 1)
            t.matmul(pb[5][0:1, 0:2 * CC], s_onescf[:], s_junk[:, 0:2 * CC],
                     start=True, stop=True).then_inc(sem["tB"], 1)
            t.wait_ge(sem["ssa"], NG + 2)
            t.matmul(pb[5][:, 2:4], s_ones1[:], s_stt2[:],
                     start=True, stop=True).then_inc(sem["tB"], 1)

            if stage < 3:
                return

            def t_ps(g):
                if g == 0:
                    t.wait_ge(sem["vcor"], NG)
                    t.wait_ge(sem["dw"], 16 * 12)
                t.wait_ge(sem["vB"], g)
                t.matmul(pb[0][:], s_ones1b[:], s_ssum[0:1, g * N:(g + 1) * N],
                         start=True, stop=True).then_inc(sem["tB"], 1)

            def t_ln_stats(g, mean_src, sq_src, mb, qb, vub_n, vst_n, sbc_n):
                t.wait_ge(sem["vub"], vub_n)
                if vst_n:
                    t.wait_ge(sem["vst"], vst_n)
                if sbc_n:
                    t.wait_ge(sem["sbc"], sbc_n)
                for cc in range(CC):
                    sl = slice(cc * N, (cc + 1) * N)
                    t.matmul(mb[0:1, :], s_onescb[:], mean_src[:, sl],
                             start=(cc == 0), stop=(cc == CC - 1))
                for cc in range(CC):
                    sl = slice(cc * N, (cc + 1) * N)
                    mm = t.matmul(qb[0:1, :], s_onescb[:], sq_src[:, sl],
                                  start=(cc == 0), stop=(cc == CC - 1))
                mm.then_inc(sem["tB"], 1)

            def t_ln_bcast(mb, qb, base, vst_n, srs_n):
                t.wait_ge(sem["vst"], vst_n)
                t.matmul(mb[:], s_ones1[:], s_stat[0:1, base:base + N],
                         start=True, stop=True).then_inc(sem["tB"], 1)
                t.wait_ge(sem["srs"], srs_n)
                t.matmul(qb[:], s_ones1[:], s_stat[0:1, base + N:base + 2 * N],
                         start=True, stop=True).then_inc(sem["tB"], 1)

            def t_ln1(g):
                t_ln_stats(g, s_x1, s_usq1, pb[1], pb[2], VUB1[g],
                           VST1[g - 1] if g >= 1 else 0,
                           SBC1[g - 1] if g >= 1 else 0)
                t_ln_bcast(pb[1], pb[2], 0, VST1[g], SRS1[g])

            def t_ln2(g):
                t_ln_stats(g, s_u2, s_usq1, pb[3], pb[4], VUB2[g],
                           VST2[g - 1] if g >= 1 else 0,
                           SBC2[g - 1] if g >= 1 else 0)
                t_ln_bcast(pb[3], pb[4], 4 * N, VST2[g], SRS2[g])

            t_ps(0)
            t_ln1(0)
            for gi in range(NG):
                if gi < NG - 1:
                    t_ps(gi + 1)
                t.wait_ge(sem["vmxr"], gi + 1)
                fi = 0
                for dc in range(CC):
                    if fi < 2:
                        if 8 * gi + fi - 1 > 0:
                            t.wait_ge(sem["vgt"], 8 * gi + fi - 1)
                    else:
                        t.wait_ge(sem["pfc"], 16 * gi + fi - 1)
                    for cc in range(CC):
                        mm = t.matmul(pb[5 + fi % 2][:], wffn_ap(s_wr, cc, dc),
                                      s_xrf[:, cc * N:(cc + 1) * N],
                                      start=(cc == 0), stop=(cc == CC - 1))
                    mm.then_inc(sem["tB"], 1)
                    fi += 1
                if gi >= 1:
                    t_ln2(gi - 1)
                if gi < NG - 1:
                    t_ln1(gi + 1)
                t.wait_ge(sem["vmxk"], gi + 1)
                for dc in range(CC):
                    t.wait_ge(sem["pfc"], 16 * gi + fi - 1)
                    for cc in range(CC):
                        mm = t.matmul(pb[5 + fi % 2][:], wffn_ap(s_wk, cc, dc),
                                      s_xkf[:, cc * N:(cc + 1) * N],
                                      start=(cc == 0), stop=(cc == CC - 1))
                    mm.then_inc(sem["tB"], 1)
                    fi += 1
                for cc in range(CC):
                    if cc == 0:
                        t.wait_ge(sem["pfc"], 16 * gi + 16)
                    elif cc == 1:
                        pass
                    else:
                        t.wait_ge(sem["vgt"], 8 * gi + fi - 17)
                    for dc in range(CC):
                        mm = t.matmul(pb[5 + fi % 2][:], wffn_ap(s_wv, dc, cc),
                                      s_kact[:, dc * N:(dc + 1) * N],
                                      start=(dc == 0), stop=(dc == CC - 1))
                    mm.then_inc(sem["tB"], 1)
                    fi += 1
            t_ln2(NG - 1)

        @block.scalar
        def _(s):
            s.wait_ge(sem["dsm"], 16 * 11)
            s.activation(s_A[:], s_alog[:], AF.Exp)
            s.memzero(s_junks[:])
            s.memzero(s_junks[:]).then_inc(sem["ssa"], 1)
            for gi in range(NG):
                for jc in range(JI):
                    s.wait_ge(sem["tpa"], gi * JC + jc + 1)
                    if gi > 0 and jc == 0:
                        s.wait_ge(sem["tpo"], gi * CC)
                    s.activation(s_xin[:, jc * N:(jc + 1) * N], pb[jc % 2][:],
                                 AF.Silu, bias=s_bin[:, jc:jc + 1]
                                 ).then_inc(sem["sea"], 1)
                for dc in range(CC):
                    j = gi * CC + dc
                    s.wait_ge(sem["tpd"], j + 1)
                    if j >= 2:
                        s.wait_ge(sem["tpm"], j - 1)
                    dl = s_del0 if dc % 2 == 0 else s_del1
                    s.activation(dl[:], pb[2 + dc % 2][:], AF.Exp,
                                 bias=s_bdt[:, dc:dc + 1])
                    s.activation(dl[:], dl[:], AF.Ln,
                                 bias=s_c1[:]).then_inc(sem["sdt"], 1)
                for jc in range(JI, JC):
                    s.wait_ge(sem["tpa"], gi * JC + jc + 1)
                    if gi * JI + (jc - JI) >= 2:
                        s.wait_ge(sem["vz"], gi * JI + (jc - JI) - 1)
                    dst = (s_sres0 if jc % 2 == 0 else s_sres1)[:]
                    s.activation(dst, pb[jc % 2][:], AF.Silu,
                                 bias=s_bin[:, jc:jc + 1]).then_inc(sem["sea"], 1)
                s.wait_ge(sem["tdb"], gi + 1)
                s.wait_ge(sem["vsc"], gi)
                s.activation(s_a[:], pb[5][0:DS, :], AF.Exp,
                             scale=s_A[:]).then_inc(sem["ssa"], 1)
                for dc in range(CC):
                    s.wait_ge(sem["tpo"], gi * CC + dc + 1)
                    if gi > 0:
                        h = 0 if dc < 4 else 1
                        s.wait_ge(sem["dso"], 16 * (2 * (gi - 1) + h + 1))
                    buf = s_sp0 if dc < 4 else s_sp1
                    s.activation(buf[:, (dc % 4) * N:(dc % 4 + 1) * N],
                                 pb[6 + dc % 2][:], AF.Copy).then_inc(sem["sso"], 1)
            if stage < 2:
                return
            s.wait_ge(sem["vbnd"], 2)
            s.activation(s_junk[0:1, 24:25], s_stt2[0:1, 1:2], AF.Ln,
                         bias=s_ceps[0:1, 0:1])
            s.memzero(s_junks[:])
            s.memzero(s_junks[:])
            s.activation(s_stt2[0:1, 1:2], s_junk[0:1, 24:25], AF.Exp,
                         scale=s_cm05[0:1, 0:1])
            s.memzero(s_junks[:])
            s.memzero(s_junks[:]).then_inc(sem["ssa"], 1)
            if stage < 3:
                return

            def s_rstd(vst_n, base):
                s.wait_ge(sem["vst"], vst_n)
                s.activation(s_stat[0:1, base + 3 * N:base + 4 * N],
                             s_stat[0:1, base + 2 * N:base + 3 * N],
                             AF.Ln, bias=s_ceps[0:1, 0:1])
                s.activation(s_stat[0:1, base + N:base + 2 * N],
                             s_stat[0:1, base + 3 * N:base + 4 * N],
                             AF.Exp, scale=s_cm05[0:1, 0:1])
                s.memzero(s_junks[:])
                s.memzero(s_junks[:]).then_inc(sem["srs"], 1)

            def s_evac(tb_m, tb_r, mb, qb, dst_m, dst_r):
                s.wait_ge(sem["tB"], tb_m)
                s.activation(dst_m[:], mb[:], AF.Copy)
                s.wait_ge(sem["tB"], tb_r)
                s.activation(dst_r[:], qb[:], AF.Copy)
                s.memzero(s_junks[:])
                s.memzero(s_junks[:]).then_inc(sem["sbc"], 2)

            s_rstd(VST1[0], 0)
            s_evac(TB_LN1BM[0], TB_LN1BR[0], pb[1], pb[2], s_sres0, s_sres1)
            for gi in range(NG):
                for dc in range(CC):
                    s.wait_ge(sem["tB"], TB_WR[gi][dc])
                    s.activation(s_r[:, dc * N:(dc + 1) * N], pb[5 + dc % 2][:],
                                 AF.Sigmoid).then_inc(sem["pfc"], 1)
                if gi >= 1:
                    s_rstd(VST2[gi - 1], 4 * N)
                    s_evac(TB_LN2BM[gi - 1], TB_LN2BR[gi - 1], pb[3], pb[4],
                           s_gt0, s_gt1)
                if gi < NG - 1:
                    s_rstd(VST1[gi + 1], 0)
                    s_evac(TB_LN1BM[gi + 1], TB_LN1BR[gi + 1], pb[1], pb[2],
                           s_sres0, s_sres1)
                for dc in range(CC):
                    s.wait_ge(sem["tB"], TB_WK[gi][dc])
                    ka = s_kact[:, dc * N:(dc + 1) * N]
                    s.activation(ka, pb[5 + dc % 2][:], AF.Relu)
                    s.activation(ka, ka, AF.Square).then_inc(sem["pfc"], 1)
            s_rstd(VST2[NG - 1], 4 * N)
            s_evac(TB_LN2BM[NG - 1], TB_LN2BR[NG - 1], pb[3], pb[4],
                   s_gt0, s_gt1)

        @block.vector
        def _(v):
            def pad():
                v.memset(s_junk[:, 64:96], 0.0)

            v.wait_ge(sem["dsm"], 16 * 11)
            v.memset(s_ones1[:], 1.0)
            v.memset(s_ones1b[:], 1.0)
            v.memset(s_ones1_64[:], 1.0)
            v.memset(s_ones64c[:], 1.0)
            v.memset(s_onescb[:], 1.0)
            v.memset(s_onescf[:], 1.0)
            v.memset(s_c1[:], 1.0)
            v.memset(s_ceps[:], 1e-5)
            v.memset(s_cm05[:], -0.5)
            v.memset(s_pay[:], 0.0)
            v.memset(s_junk[:], 0.0)
            v.wait_ge(sem["ssa"], 1)
            v.tensor_scalar(s_A[:], s_A[:], -1.0, None, ALU.mult)
            v.tensor_scalar(s_dp1[:], s_dp1[:], 1.0, None, ALU.add)
            pad()
            v.engine_nop().then_inc(sem["vin"], 1)

            for gi in range(NG):
                v.wait_ge(sem["tpx"], gi + 1)
                v.wait_ge(sem["tpd"], CC * (gi + 1))
                for jc in range(JI):
                    v.wait_ge(sem["sea"], gi * JC + JI + jc + 1)
                    sr = s_sres0 if jc % 2 == 0 else s_sres1
                    v.tensor_mul(s_xin[:, jc * N:(jc + 1) * N],
                                 s_xin[:, jc * N:(jc + 1) * N], sr[:]
                                 ).then_inc(sem["vz"], 1)
                v.wait_ge(sem["tpm"], CC * (gi + 1))
                v.tensor_scalar(s_dsb[:], pb[5][0:1, :], 1.0 / C, None,
                                ALU.mult)
                pad(); pad()
                v.engine_nop().then_inc(sem["vd"], 1)
                v.wait_ge(sem["tpx"], gi + 1)
                v.tensor_scalar(s_bx[:], pb[4][0:DS, :], s_bxp[:], None,
                                ALU.add).then_inc(sem["vbx"], 1)
                v.wait_ge(sem["ssa"], gi + 2)
                sbf = s_s0 if gi % 2 == 0 else s_s1
                pbf = s_s1 if gi % 2 == 0 else s_s0
                if gi >= 2:
                    v.wait_ge(sem["tss"], gi - 1)
                init_s = 0.0 if gi == 0 else pbf[:, N - 1:N]
                init_g = 1.0 if gi == 0 else s_G[:, gi * N - 1:gi * N]
                v.tensor_tensor_scan(sbf[:], s_a[:], s_bx[:], init_s,
                                     ALU.mult, ALU.add)
                v.tensor_tensor_scan(s_G[:, gi * N:(gi + 1) * N], s_a[:],
                                     s_bx[:], init_g, ALU.mult, ALU.bypass)
                v.engine_nop().then_inc(sem["vsc"], 1)
                v.wait_ge(sem["tss"], gi + 1)
                v.tensor_copy(s_ssum[0:1, gi * N:(gi + 1) * N], pb[5][0:1, :])
                pad()
                v.engine_nop().then_inc(sem["vss"], 1)

            if stage < 2:
                return
            sbf7 = s_s0 if (NG - 1) % 2 == 0 else s_s1
            v.wait_ge(sem["sso"], CC * NG)
            v.tensor_copy(s_pay[0:DS, 0:1], sbf7[:, N - 1:N])
            v.wait_ge(sem["tss"], NG + 1)
            v.tensor_copy(s_slastc[:], pb[5][:, 0:1])
            pad(); pad()
            for cc in range(CC):
                sp = s_sp0 if cc < 4 else s_sp1
                col = (cc % 4) * N + (N - 1)
                v.tensor_scalar(s_pay[:, 1 + cc:2 + cc], sp[:, col:col + 1],
                                s_slastc[:], None, ALU.mult)
            pad(); pad(); pad()
            v.engine_nop().then_inc(sem["vpay"], 1)
            v.wait_ge(sem["dgat"], 16)
            v.tensor_mul(s_gath[:], s_gath[:], s_rmask[:])
            pad(); pad()
            half = NCORES * 9 // 2
            v.tensor_add(s_gath[:, 0:half], s_gath[:, 0:half], s_gath[:, half:])
            pad(); pad()
            v.tensor_add(s_gath[:, 0:18], s_gath[:, 0:18], s_gath[:, 18:36])
            pad(); pad()
            v.tensor_add(s_sel[:, 0:9], s_gath[:, 0:9], s_gath[:, 9:18])
            pad(); pad()
            v.tensor_copy(s_hinb[:], s_sel[0:DS, 0:1])
            pad(); pad()
            v.engine_nop().then_inc(sem["vpay"], 1)
            for gi in range(NG):
                v.wait_ge(sem["tcr"], gi + 1)
                sl = slice(gi * N, (gi + 1) * N)
                if gi < NG - 1:
                    v.tensor_add(s_ssum[0:1, sl], s_ssum[0:1, sl],
                                 pb[6 + gi % 2][0:1, :]).then_inc(sem["vcor"], 1)
                else:
                    v.tensor_add(s_ssum[0:1, sl], s_ssum[0:1, sl],
                                 pb[6 + gi % 2][0:1, :])
                    pad(); pad()
                    v.engine_nop().then_inc(sem["vcor"], 1)
            v.tensor_mul(s_ub[:], s_xbT[:], s_dp1[:])
            pad(); pad()
            v.tensor_add(s_ub[:], s_ub[:], s_bout[:])
            pad(); pad()
            v.tensor_add(s_ub[:], s_ub[:], s_sel[:, 1:9])
            pad(); pad()
            v.tensor_copy(s_junk[:, 0:CC], s_ub[:])
            v.tensor_mul(s_junk[:, CC:2 * CC], s_ub[:], s_ub[:])
            pad(); pad()
            v.engine_nop().then_inc(sem["vbnd"], 1)
            v.wait_ge(sem["tB"], 1)
            v.reduce_sum(s_stt2[0:1, 0:1], pb[5][0:1, 0:CC], axis=AX)
            v.reduce_sum(s_stt2[0:1, 1:2], pb[5][0:1, CC:2 * CC], axis=AX)
            pad(); pad()
            v.tensor_scalar(s_stt2[0:1, 0:2], s_stt2[0:1, 0:2], 1.0 / C, None,
                            ALU.mult)
            pad(); pad()
            v.tensor_mul(s_junk[0:1, 16:17], s_stt2[0:1, 0:1], s_stt2[0:1, 0:1])
            pad(); pad()
            v.tensor_sub(s_stt2[0:1, 1:2], s_stt2[0:1, 1:2], s_junk[0:1, 16:17])
            pad(); pad()
            v.engine_nop().then_inc(sem["vbnd"], 1)
            v.wait_ge(sem["ssa"], NG + 2)
            v.wait_ge(sem["tB"], 2)
            v.tensor_copy(s_junk[:, 32:34], pb[5][:, 2:4])
            pad(); pad()
            v.tensor_scalar(s_ub[:], s_ub[:], s_junk[:, 32:33],
                            s_junk[:, 33:34], ALU.subtract, ALU.mult)
            pad(); pad()
            v.tensor_scalar(s_prevc[:], s_ub[:], s_hmask[:], None, ALU.mult)
            pad(); pad()
            v.engine_nop().then_inc(sem["vbnd"], 1)

            if stage < 3:
                return

            def v_u_steps(g):
                xb = s_xb0 if g % 2 == 0 else s_xb1
                v.wait_ge(sem["dxb"], 16 * (g + 1))
                v.wait_ge(sem["dmb"], 16 * (g + 1))
                v.wait_ge(sem["tB"], TB_PS[g])
                for cc in range(CC):
                    sl = slice(cc * N, (cc + 1) * N)
                    v.tensor_mul(s_mmb[:, sl], s_mmb[:, sl], pb[0][:])
                for cc in range(CC):
                    sl = slice(cc * N, (cc + 1) * N)
                    v.scalar_tensor_tensor(s_x1[:, sl], xb[:, sl],
                                           s_dp1[:, cc:cc + 1], s_mmb[:, sl],
                                           ALU.mult, ALU.add)
                for cc in range(CC):
                    sl = slice(cc * N, (cc + 1) * N)
                    v.tensor_scalar(s_x1[:, sl], s_x1[:, sl],
                                    s_bout[:, cc:cc + 1], None, ALU.add)
                v.engine_nop().then_inc(sem["vB"], 1)

            def v_usq(src, dst):
                for cc in range(CC):
                    sl = slice(cc * N, (cc + 1) * N)
                    v.tensor_mul(dst[:, sl], src[:, sl], src[:, sl])
                v.engine_nop().then_inc(sem["vub"], 1)

            def v_ln_vals(tb_n, base, mbank, qbank):
                v.wait_ge(sem["tB"], tb_n)
                v.tensor_scalar(s_stat[0:1, base:base + N], mbank[0:1, :],
                                1.0 / C, None, ALU.mult)
                v.tensor_scalar(s_stat[0:1, base + 2 * N:base + 3 * N],
                                qbank[0:1, :], 1.0 / C, None, ALU.mult)
                v.tensor_mul(s_stat[0:1, base + 3 * N:base + 4 * N],
                             s_stat[0:1, base:base + N],
                             s_stat[0:1, base:base + N])
                v.tensor_sub(s_stat[0:1, base + 2 * N:base + 3 * N],
                             s_stat[0:1, base + 2 * N:base + 3 * N],
                             s_stat[0:1, base + 3 * N:base + 4 * N])
                pad(); pad()
                v.engine_nop().then_inc(sem["vst"], 2)

            def v_apply(g):
                v.wait_ge(sem["sbc"], SBC1[g])
                for cc in range(CC):
                    sl = slice(cc * N, (cc + 1) * N)
                    v.tensor_sub(s_x1[:, sl], s_x1[:, sl], s_sres0[:])
                for cc in range(CC):
                    sl = slice(cc * N, (cc + 1) * N)
                    v.tensor_mul(s_x1[:, sl], s_x1[:, sl], s_sres1[:])

            def v_mixr(g):
                for cc in range(CC):
                    o = cc * N
                    v.tensor_sub(s_xkf[:, o + 1:o + N], s_x1[:, o + 1:o + N],
                                 s_x1[:, o:o + N - 1])
                for cc in range(CC):
                    o = cc * N
                    v.tensor_sub(s_xkf[:, o:o + 1], s_x1[:, o:o + 1],
                                 s_prevc[:, cc:cc + 1])
                for cc in range(CC):
                    o = cc * N
                    v.scalar_tensor_tensor(s_xrf[:, o + 1:o + N],
                                           s_xkf[:, o + 1:o + N],
                                           s_mur[:, cc:cc + 1],
                                           s_x1[:, o:o + N - 1],
                                           ALU.mult, ALU.add)
                for cc in range(CC):
                    o = cc * N
                    v.scalar_tensor_tensor(s_xrf[:, o:o + 1], s_xkf[:, o:o + 1],
                                           s_mur[:, cc:cc + 1],
                                           s_prevc[:, cc:cc + 1],
                                           ALU.mult, ALU.add)
                pad()
                v.engine_nop().then_inc(sem["vmxr"], 1)

            def v_mixk(g):
                for cc in range(CC):
                    o = cc * N
                    v.scalar_tensor_tensor(s_xkf[:, o:o + 1], s_xkf[:, o:o + 1],
                                           s_muk[:, cc:cc + 1],
                                           s_prevc[:, cc:cc + 1],
                                           ALU.mult, ALU.add)
                for cc in range(CC):
                    o = cc * N
                    v.scalar_tensor_tensor(s_xkf[:, o + 1:o + N],
                                           s_xkf[:, o + 1:o + N],
                                           s_muk[:, cc:cc + 1],
                                           s_x1[:, o:o + N - 1],
                                           ALU.mult, ALU.add)
                for cc in range(CC):
                    o = cc * N
                    v.tensor_copy(s_prevc[:, cc:cc + 1], s_x1[:, o + N - 1:o + N])
                pad()
                v.engine_nop().then_inc(sem["vmxk"], 1)

            v.wait_ge(sem["dw"], 16 * 12)
            v_u_steps(0)
            v_usq(s_x1, s_usq1)
            v_ln_vals(TB_LN1S[0], 0, pb[1], pb[2])
            v_apply(0)
            if dbg:
                pad()
                v.engine_nop().then_inc(sem["vdbg"], 1)
                v.wait_ge(sem["ddbg"], 16)
            v_mixr(0)
            v_mixk(0)
            for gi in range(NG):
                for cc in range(CC):
                    sl = slice(cc * N, (cc + 1) * N)
                    v.tensor_copy(s_u2[:, sl], s_x1[:, sl])
                if gi < NG - 1:
                    v_u_steps(gi + 1)
                    v_usq(s_x1, s_usq1)
                    v_ln_vals(TB_LN1S[gi + 1], 0, pb[1], pb[2])
                    v_apply(gi + 1)
                    v.wait_ge(sem["tB"], TB_WK[gi][CC - 1])
                    v_mixr(gi + 1)
                    v_mixk(gi + 1)
                if dbg and gi == 0:
                    v.wait_ge(sem["pfc"], 16)
                    pad()
                    v.engine_nop().then_inc(sem["vdbg"], 2)
                    v.wait_ge(sem["ddbg"], 48)
                v.wait_ge(sem["sbc"], SBC2[gi - 1] if gi >= 1 else 0)
                for cc in range(CC):
                    sl = slice(cc * N, (cc + 1) * N)
                    gt = s_gt0 if cc % 2 == 0 else s_gt1
                    v.wait_ge(sem["tB"], TB_WV[gi][cc])
                    v.tensor_mul(gt[:], s_r[:, sl], pb[5 + (16 + cc) % 2][:])
                    v.tensor_add(s_u2[:, sl], s_u2[:, sl], gt[:]
                                 ).then_inc(sem["vgt"], 1)
                if dbg and gi == 0:
                    pad()
                    v.engine_nop().then_inc(sem["vdbg"], 1)
                    v.wait_ge(sem["ddbg"], 64)
                v_usq(s_u2, s_usq1)
                v_ln_vals(TB_LN2S[gi], 4 * N, pb[3], pb[4])
                v.wait_ge(sem["sbc"], SBC2[gi])
                for cc in range(CC):
                    k = gi * CC + cc
                    if k >= 2:
                        v.wait_ge(sem["dod"], 16 * (k - 1))
                    sl = slice(cc * N, (cc + 1) * N)
                    ost = s_ost0 if cc % 2 == 0 else s_ost1
                    v.tensor_sub(ost[:], s_u2[:, sl], s_gt0[:])
                    v.tensor_mul(ost[:], ost[:], s_gt1[:]).then_inc(sem["vod"], 1)

    nc.finalize()
    return nc


_NC_CACHE = {}


def _get_nc():
    if "nc" not in _NC_CACHE:
        _NC_CACHE["nc"] = build_nc()
    return _NC_CACHE["nc"]


def _chunk_weight(w, kchunks):
    K, M = w.shape
    assert K == kchunks * 128
    return np.ascontiguousarray(
        w.reshape(kchunks, 128, M).transpose(1, 0, 2).reshape(128, kchunks * M))


def _col_tile(vec, ncol):
    return np.ascontiguousarray(vec.reshape(ncol, 128).T)


def stage_inputs(L0_attn_x, L0_attn_W_in, L0_attn_b_in, L0_attn_W_xp,
                 L0_attn_b_xp, L0_attn_W_dt, L0_attn_b_dt, L0_attn_W_out,
                 L0_attn_b_out, L0_attn_A_log, L0_attn_D, L0_ffn_W_r,
                 L0_ffn_W_k, L0_ffn_W_v, L0_ffn_mu_r, L0_ffn_mu_k, T=T_FULL):
    f = np.float32
    x = np.asarray(L0_attn_x, f)
    w_in = np.asarray(L0_attn_W_in, f)
    win = np.ascontiguousarray(
        w_in.reshape(CC, 128, JC, 128).transpose(1, 2, 0, 3).reshape(128, -1))
    wdt = _chunk_weight(np.asarray(L0_attn_W_dt, f), JI)
    wxp = _chunk_weight(np.asarray(L0_attn_W_xp, f), JI)
    wout = _chunk_weight(np.asarray(L0_attn_W_out, f), JI)
    wr = _chunk_weight(np.ascontiguousarray(np.asarray(L0_ffn_W_r, f).T), CC)
    wk = _chunk_weight(np.ascontiguousarray(np.asarray(L0_ffn_W_k, f).T), CC)
    wv = _chunk_weight(np.ascontiguousarray(np.asarray(L0_ffn_W_v, f).T), CC)
    bin_t = _col_tile(np.asarray(L0_attn_b_in, f), JC)
    bdt_t = _col_tile(np.asarray(L0_attn_b_dt, f), CC)
    bout_t = _col_tile(np.asarray(L0_attn_b_out, f), CC)
    bxp_t = np.ascontiguousarray(np.asarray(L0_attn_b_xp, f).reshape(DS, 1))
    alog_t = np.ascontiguousarray(np.asarray(L0_attn_A_log, f).reshape(DS, 1))
    dvec_t = _col_tile(np.asarray(L0_attn_D, f), CC)
    mur_t = _col_tile(np.asarray(L0_ffn_mu_r, f), CC)
    muk_t = _col_tile(np.asarray(L0_ffn_mu_k, f), CC)

    shared = dict(win=win, wdt=wdt, wxp=wxp, wout=wout, wr=wr, wk=wk, wv=wv,
                  bin=bin_t, bdt=bdt_t, bout=bout_t, bxp=bxp_t, alog=alog_t,
                  dvec=dvec_t, mur=mur_t, muk=muk_t)

    in_maps = []
    for c in range(NCORES):
        b, h = c // 2, c % 2
        xT = np.ascontiguousarray(x[b, h * T:(h + 1) * T, :].T)
        if h == 1:
            xbT = _col_tile(x[b, T - 1, :], CC)
            hmask = np.ones((128, 1), f)
            rmask = np.zeros((128, NCORES * 9), f)
            rmask[:, (c - 1) * 9:c * 9] = 1.0
        else:
            xbT = np.zeros((128, CC), f)
            hmask = np.zeros((128, 1), f)
            rmask = np.zeros((128, NCORES * 9), f)
        in_maps.append(dict(shared, xT=xT, xbT=xbT, hmask=hmask, rmask=rmask))
    return in_maps


def kernel(**inputs):
    in_maps = stage_inputs(**inputs)
    nc = _get_nc()
    res = run_bass_kernel_spmd(nc, in_maps, core_ids=list(range(NCORES)))
    f = np.float32
    out = np.empty((B_, S_, C), f)
    for c in range(NCORES):
        b, h = c // 2, c % 2
        out[b, h * T_FULL:(h + 1) * T_FULL, :] = np.asarray(
            res.results[c]["outT"], f).T
    return out


# version 11
# speedup vs baseline: 1.0065x; 1.0065x over previous
import numpy as np
import ml_dtypes
import concourse.bass as bass
import concourse.mybir as mybir
from concourse.bass_utils import run_bass_kernel_spmd

F32 = mybir.dt.float32
BF16 = mybir.dt.bfloat16
AF = mybir.ActivationFunctionType
ALU = mybir.AluOpType
AX = mybir.AxisListType.X

NCORES = 8
B_, S_, C = 4, 8192, 1024
T_FULL = S_ // 2
N = 512
CC = C // 128
JC = 32
DI = 2048
JI = DI // 128
DS = 64


def build_nc(T=T_FULL, stage=3, dbg=False):
    NG = T // N
    nc = bass.Bass("TRN2", target_bir_lowering=False,
                   detect_race_conditions=False)

    d_xT = nc.dram_tensor("xT", [C, T], F32, kind="ExternalInput")
    d_win = nc.dram_tensor("win", [128, JC * CC * 128], F32, kind="ExternalInput")
    d_wdt = nc.dram_tensor("wdt", [128, JI * CC * 128], F32, kind="ExternalInput")
    d_wxp = nc.dram_tensor("wxp", [128, JI * DS], F32, kind="ExternalInput")
    d_wout = nc.dram_tensor("wout", [128, JI * CC * 128], F32, kind="ExternalInput")
    d_wr = nc.dram_tensor("wr", [128, CC * CC * 128], F32, kind="ExternalInput")
    d_wk = nc.dram_tensor("wk", [128, CC * CC * 128], F32, kind="ExternalInput")
    d_wv = nc.dram_tensor("wv", [128, CC * CC * 128], F32, kind="ExternalInput")
    d_bin = nc.dram_tensor("bin", [128, JC], F32, kind="ExternalInput")
    d_bdt = nc.dram_tensor("bdt", [128, CC], F32, kind="ExternalInput")
    d_bout = nc.dram_tensor("bout", [128, CC], F32, kind="ExternalInput")
    d_bxp = nc.dram_tensor("bxp", [DS, 1], F32, kind="ExternalInput")
    d_alog = nc.dram_tensor("alog", [DS, 1], F32, kind="ExternalInput")
    d_dvec = nc.dram_tensor("dvec", [128, CC], F32, kind="ExternalInput")
    d_mur = nc.dram_tensor("mur", [128, CC], F32, kind="ExternalInput")
    d_muk = nc.dram_tensor("muk", [128, CC], F32, kind="ExternalInput")
    d_xbT = nc.dram_tensor("xbT", [128, CC], F32, kind="ExternalInput")
    d_hmask = nc.dram_tensor("hmask", [128, 1], F32, kind="ExternalInput")
    d_rmask = nc.dram_tensor("rmask", [128, NCORES * 9], F32, kind="ExternalInput")
    d_out = nc.dram_tensor("outT", [C, T], F32, kind="ExternalOutput")
    if dbg:
        d_dbg1 = nc.dram_tensor("dbg1", [128, CC * N], F32, kind="ExternalOutput")
        d_dbg2 = nc.dram_tensor("dbg2", [128, CC * N], F32, kind="ExternalOutput")
        d_dbg3 = nc.dram_tensor("dbg3", [128, CC * N], F32, kind="ExternalOutput")
        d_dbg4 = nc.dram_tensor("dbg4", [128, CC * N], F32, kind="ExternalOutput")
    d_spill = nc.dram_tensor("spill", [C, T], BF16, kind="Internal")
    d_ccin = nc.dram_tensor("cc_in", [128, 9], F32, kind="Internal")
    d_ccout = nc.dram_tensor("cc_out", [NCORES * 128, 9], F32, kind="Internal")

    BASE = ((nc.sbuf_base + 31) // 32) * 32
    TOP = nc.sbuf_top
    cur = [BASE]

    def sb(name, shape, dtype, at=None):
        isz = 2 if dtype == BF16 else 4
        nb = int(np.prod(shape[1:])) * isz
        nb = ((nb + 31) // 32) * 32
        if at is None:
            off = cur[0]
            cur[0] += nb
        else:
            off = BASE_W0[0] + at
        assert off + nb <= TOP, (name, off, nb, TOP)
        return nc.alloc_sbuf_tensor_at(name, list(shape), dtype, offset=off)

    BASE_W0 = [0]
    s_ssum = sb("ssum", [1, T], BF16)
    s_G = sb("G", [DS, T], BF16)
    s_s0 = sb("s0", [DS, N], F32)
    s_s1 = sb("s1", [DS, N], F32)
    s_a = sb("a", [DS, N], F32)
    s_bx = sb("bx", [DS, N], F32)
    s_sres0 = sb("sres0", [128, N], BF16)
    s_sres1 = sb("sres1", [128, N], BF16)
    s_dsb = sb("dsb", [1, N], F32)
    s_bin = sb("v_bin", [128, JC], F32)
    s_bdt = sb("v_bdt", [128, CC], F32)
    s_bout = sb("v_bout", [128, CC], F32)
    s_bxp = sb("v_bxp", [DS, 1], F32)
    s_alog = sb("v_alog", [DS, 1], F32)
    s_A = sb("v_A", [DS, 1], F32)
    s_dp1 = sb("v_dp1", [128, CC], F32)
    s_mur = sb("v_mur", [128, CC], F32)
    s_muk = sb("v_muk", [128, CC], F32)
    s_xbT = sb("v_xbT", [128, CC], F32)
    s_hmask = sb("v_hmask", [128, 1], F32)
    s_ones1 = sb("ones1", [1, 128], F32)
    s_ones1b = sb("ones1b", [1, 128], BF16)
    s_ones1_64 = sb("ones1_64", [1, DS], F32)
    s_ones64c = sb("ones64c", [DS, 1], F32)
    s_onescb = sb("onescb", [128, 1], BF16)
    s_onescf = sb("onescf", [128, 1], F32)
    s_c1 = sb("c_one", [128, 1], F32)
    s_ceps = sb("c_eps", [1, 1], F32)
    s_cm05 = sb("c_m05", [1, 1], F32)
    s_junk = sb("junk", [128, 128], F32)
    s_junks = sb("junks", [128, 128], F32)
    s_pay = sb("pay", [128, 9], F32)
    s_gath = sb("gath", [128, NCORES * 9], F32)
    s_rmask = sb("rmaskT", [128, NCORES * 9], F32)
    s_sel = sb("sel", [128, 9], F32)
    s_hinb = sb("hinb", [DS, 1], BF16)
    s_ub = sb("ub", [128, CC], F32)
    s_prevc = sb("prevc", [128, CC], F32)
    s_slastc = sb("slastc", [128, 1], F32)
    s_stt2 = sb("stt2", [1, 2], F32)
    BASE_W0[0] = ((cur[0] + 31) // 32) * 32

    s_win = sb("w_in", [128, JC * CC * 128], BF16, at=0)
    s_wdt = sb("w_dt", [128, JI * CC * 128], BF16, at=65536)
    s_wxp = sb("w_xp", [128, JI * DS], BF16, at=98304)
    s_wout = sb("w_out", [128, JI * CC * 128], BF16, at=100352)
    s_xa0 = sb("xa0", [128, CC * N], BF16, at=133120)
    s_xa1 = sb("xa1", [128, CC * N], BF16, at=141312)
    s_xin = sb("xin", [128, JI * N], BF16, at=149504)
    s_del0 = sb("del0", [128, N], BF16, at=165888)
    s_del1 = sb("del1", [128, N], BF16, at=166912)
    s_sp0 = sb("sp0", [128, 4 * N], BF16, at=167936)
    s_sp1 = sb("sp1", [128, 4 * N], BF16, at=172032)
    A_END = 176128
    s_wr = sb("w_r", [128, CC * CC * 128], BF16, at=0)
    s_wk = sb("w_k", [128, CC * CC * 128], BF16, at=16384)
    s_wv = sb("w_v", [128, CC * CC * 128], BF16, at=32768)
    s_xb0 = sb("xb0", [128, CC * N], F32, at=49152)
    s_xb1 = sb("xb1", [128, CC * N], F32, at=65536)
    s_mmb = sb("mmb", [128, CC * N], F32, at=81920)
    s_x1 = sb("x1", [128, CC * N], BF16, at=98304)
    s_u2 = sb("u2", [128, CC * N], BF16, at=106496)
    s_kact = sb("kact", [128, CC * N], BF16, at=114688)
    s_r = sb("r", [128, CC * N], BF16, at=122880)
    s_usq1 = sb("usq1", [128, CC * N], BF16, at=131072)
    s_xrf = sb("xrf", [128, CC * N], BF16, at=139264)
    s_xkf = sb("xkf", [128, CC * N], BF16, at=147456)
    s_ost0 = sb("ost0", [128, N], F32, at=155648)
    s_ost1 = sb("ost1", [128, N], F32, at=157696)
    s_stat = sb("stat", [1, 6 * N], F32, at=159744)
    s_ost2 = sb("ost2", [128, N], F32, at=172032)
    s_ost3 = sb("ost3", [128, N], F32, at=174080)
    s_gt0 = sb("gt0", [128, N], BF16, at=176128)
    s_gt1 = sb("gt1", [128, N], BF16, at=177184)
    B_END = 178240
    assert BASE_W0[0] + max(A_END, B_END) <= TOP, (BASE_W0[0], TOP)

    pb = [nc.alloc_psum_tensor(f"pb{i}", [128, N], F32) for i in range(8)]

    semnames = [
        "dw", "dxa", "dxb", "dmb", "dso", "dod", "dsm", "dcc",
        "tpa", "tpd", "tpx", "tpm", "tdb", "tss", "tpo", "tgx", "t1",
        "tcr", "tB", "tst", "dpay", "dcol", "dgat",
        "sea", "sdt", "ssa", "sso", "srs", "pfc",
        "vin", "vz", "vd", "vsc", "vss", "vbx", "vpay", "vcor", "vbnd",
        "vB", "vub", "vst", "vmx", "vmxr", "vmxk", "vod", "vdbg", "ddbg", "sbc",
        "vgt", "vu2",
    ]
    sem = {n: nc.alloc_semaphore("sem_" + n) for n in semnames}
    rg = [list(range(NCORES))]

    TB_PS, TB_LN1S, TB_LN1BM, TB_LN1BR = {}, {}, {}, {}
    TB_WR, TB_WK, TB_WV = {}, {}, {}
    TB_LN2S, TB_LN2BM, TB_LN2BR = {}, {}, {}
    cnt = 2
    cnt += 1; TB_PS[0] = cnt
    cnt += 1; TB_LN1S[0] = cnt
    cnt += 1; TB_LN1BM[0] = cnt
    cnt += 1; TB_LN1BR[0] = cnt
    for g in range(NG):
        if g < NG - 1:
            cnt += 1; TB_PS[g + 1] = cnt
        TB_WR[g] = {}
        for dc in range(CC):
            cnt += 1; TB_WR[g][dc] = cnt
        if g >= 1:
            cnt += 1; TB_LN2S[g - 1] = cnt
            cnt += 1; TB_LN2BM[g - 1] = cnt
            cnt += 1; TB_LN2BR[g - 1] = cnt
        if g < NG - 1:
            cnt += 1; TB_LN1S[g + 1] = cnt
            cnt += 1; TB_LN1BM[g + 1] = cnt
            cnt += 1; TB_LN1BR[g + 1] = cnt
        TB_WK[g] = {}
        for dc in range(CC):
            cnt += 1; TB_WK[g][dc] = cnt
        TB_WV[g] = {}
        for cc in range(CC):
            cnt += 1; TB_WV[g][cc] = cnt
    cnt += 1; TB_LN2S[NG - 1] = cnt
    cnt += 1; TB_LN2BM[NG - 1] = cnt
    cnt += 1; TB_LN2BR[NG - 1] = cnt

    VB_N, VUB1, VUB2, VST1, VST2, VMXR, VMXK = {}, {}, {}, {}, {}, {}, {}
    vb = vub = vst = vmxr = vmxk = 0
    vb += 1; VB_N[0] = vb
    vub += 1; VUB1[0] = vub
    vst += 2; VST1[0] = vst
    vmxr += 1; VMXR[0] = vmxr
    vmxk += 1; VMXK[0] = vmxk
    for g in range(NG):
        if g < NG - 1:
            vb += 1; VB_N[g + 1] = vb
            vub += 1; VUB1[g + 1] = vub
            vst += 2; VST1[g + 1] = vst
            vmxr += 1; VMXR[g + 1] = vmxr
            vmxk += 1; VMXK[g + 1] = vmxk
        VUB2[g] = g + 1
        vst += 2; VST2[g] = vst
    SRS1, SRS2, SBC1, SBC2 = {}, {}, {}, {}
    srs = sbc = 0
    srs += 1; SRS1[0] = srs
    sbc += 2; SBC1[0] = sbc
    for g in range(NG):
        if g >= 1:
            srs += 1; SRS2[g - 1] = srs
            sbc += 2; SBC2[g - 1] = sbc
        if g < NG - 1:
            srs += 1; SRS1[g + 1] = srs
            sbc += 2; SBC1[g + 1] = sbc
    srs += 1; SRS2[NG - 1] = srs
    sbc += 2; SBC2[NG - 1] = sbc

    with nc.Block() as block:

        @block.gpsimd
        def _(g):
            def xa_load(gi):
                buf = s_xa0 if gi % 2 == 0 else s_xa1
                src = d_xT.ap().rearrange("(c p) t -> p c t", c=CC, p=128)
                src = src[:, :, gi * N:(gi + 1) * N]
                dst = buf[:].rearrange("p (c t) -> p c t", c=CC)
                g.dma_start(dst, src).then_inc(sem["dxa"], 16)

            def mmb_load(gi):
                src = d_spill.ap().rearrange("(c p) t -> p c t", c=CC, p=128)
                src = src[:, :, gi * N:(gi + 1) * N]
                dst = s_mmb[:].rearrange("p (c t) -> p c t", c=CC)
                g.dma_start(dst, src).then_inc(sem["dmb"], 16)

            xa_load(0)
            WQ = JC * CC * 128
            for q in range(4):
                sl = slice(q * WQ // 4, (q + 1) * WQ // 4)
                g.dma_start(s_win[:, sl], d_win[:, sl]).then_inc(sem["dw"], 16)
            if NG > 1:
                xa_load(1)
            WD = JI * CC * 128
            for q in range(2):
                sl = slice(q * WD // 2, (q + 1) * WD // 2)
                g.dma_start(s_wdt[:, sl], d_wdt[:, sl]).then_inc(sem["dw"], 16)
            g.dma_start(s_wxp[:], d_wxp[:]).then_inc(sem["dw"], 16)
            for q in range(2):
                sl = slice(q * WD // 2, (q + 1) * WD // 2)
                g.dma_start(s_wout[:, sl], d_wout[:, sl]).then_inc(sem["dw"], 16)
            for gi in range(2, NG):
                g.wait_ge(sem["tpa"], JC * (gi - 1))
                xa_load(gi)
            g.wait_ge(sem["tpa"], JC * NG)
            g.dma_start(s_wr[:], d_wr[:]).then_inc(sem["dw"], 16)
            g.dma_start(s_wk[:], d_wk[:]).then_inc(sem["dw"], 16)
            g.dma_start(s_wv[:], d_wv[:]).then_inc(sem["dw"], 16)
            if stage < 2:
                return
            if stage >= 3:
                g.wait_ge(sem["tpd"], CC * NG)
                g.wait_ge(sem["dso"], 16 * 2)
                mmb_load(0)
            g.wait_ge(sem["dpay"], 16)
            g.collective_compute(
                "AllGather", ALU.bypass, replica_groups=rg,
                ins=[d_ccin.ap().opt()], outs=[d_ccout.ap().opt()],
            ).then_inc(sem["dcol"])
            g.wait_ge(sem["dcol"], 1)
            src = d_ccout.ap().rearrange("(r p) c -> p r c", r=NCORES, p=128)
            dst = s_gath[:].rearrange("p (r c) -> p r c", r=NCORES)
            g.dma_start(dst, src).then_inc(sem["dgat"], 16)
            if stage < 3 and stage not in (4, 5, 7, 8, 9, 10):
                return
            for gi in range(1, NG):
                g.wait_ge(sem["dso"], 16 * 2 * (gi + 1))
                g.wait_ge(sem["vB"], gi)
                mmb_load(gi)
            if dbg:
                g.wait_ge(sem["vdbg"], 1)
                g.dma_start(d_dbg1[:], s_x1[:]).then_inc(sem["ddbg"], 16)
                g.wait_ge(sem["vdbg"], 3)
                g.dma_start(d_dbg3[:], s_r[:]).then_inc(sem["ddbg"], 16)
                g.dma_start(d_dbg4[:], s_kact[:]).then_inc(sem["ddbg"], 16)
                g.wait_ge(sem["vdbg"], 4)
                g.dma_start(d_dbg2[:], s_u2[:]).then_inc(sem["ddbg"], 16)

        @block.sync
        def _(sy):
            nsm = 0
            for dst, src in [(s_bin, d_bin), (s_bdt, d_bdt), (s_bout, d_bout),
                             (s_bxp, d_bxp), (s_alog, d_alog), (s_dp1, d_dvec),
                             (s_mur, d_mur), (s_muk, d_muk), (s_xbT, d_xbT),
                             (s_hmask, d_hmask), (s_rmask, d_rmask)]:
                sy.dma_start(dst[:], src[:]).then_inc(sem["dsm"], 16)
                nsm += 1
            for gi in range(NG):
                for h in range(2):
                    sy.wait_ge(sem["sso"], gi * CC + (h + 1) * 4)
                    buf = s_sp0 if h == 0 else s_sp1
                    src = buf[:].rearrange("p (c t) -> p c t", c=4)
                    dst = d_spill.ap().rearrange("(c p) t -> p c t", c=CC, p=128)
                    dst = dst[:, h * 4:(h + 1) * 4, gi * N:(gi + 1) * N]
                    sy.dma_start(dst, src).then_inc(sem["dso"], 16)

            def xb_load(gi):
                buf = s_xb0 if gi % 2 == 0 else s_xb1
                src = d_xT.ap().rearrange("(c p) t -> p c t", c=CC, p=128)
                src = src[:, :, gi * N:(gi + 1) * N]
                dst = buf[:].rearrange("p (c t) -> p c t", c=CC)
                sy.dma_start(dst, src).then_inc(sem["dxb"], 16)

            if stage >= 3:
                sy.wait_ge(sem["tpa"], JC * NG)
                xb_load(0)
                if NG > 1:
                    sy.wait_ge(sem["tpd"], CC * NG)
                    xb_load(1)
            if stage >= 2:
                sy.wait_ge(sem["vpay"], 1)
                sy.dma_start(d_ccin[:], s_pay[:]).then_inc(sem["dpay"], 16)
            if stage in (4, 5, 7, 8, 9, 10):
                sy.wait_ge(sem["vod"], NG)
                sy.dma_start(d_out[0:128, 0:128], s_junk[:]).then_inc(sem["dod"], 16)
                sy.wait_ge(sem["dod"], 16)
            elif stage >= 3:
                nod = 0
                for gi in range(NG):
                    if gi + 2 < NG:
                        sy.wait_ge(sem["vB"], gi + 1)
                        xb_load(gi + 2)
                    for cc in range(CC):
                        sy.wait_ge(sem["vod"], gi * CC + cc + 1)
                        buf = (s_ost0, s_ost1, s_ost2, s_ost3)[cc % 4]
                        dst = d_out[cc * 128:(cc + 1) * 128, gi * N:(gi + 1) * N]
                        sy.dma_start(dst, buf[:]).then_inc(sem["dod"], 16)
                        nod += 1
                sy.wait_ge(sem["dod"], 16 * nod)
            else:
                if stage == 1:
                    sy.wait_ge(sem["vss"], NG)
                else:
                    sy.wait_ge(sem["vbnd"], 3)
                sy.wait_ge(sem["sso"], CC * NG)
                sy.dma_start(d_out[0:128, 0:128], s_junk[:]).then_inc(sem["dod"], 16)
                sy.wait_ge(sem["dod"], 16)
            sy.wait_ge(sem["dso"], 16 * 2 * NG)

        @block.tensor
        def _(t):
            def win_ap(cc, jc):
                o = (jc * CC + cc) * 128
                return s_win[:, o:o + 128]

            def wdt_ap(jc, dc):
                o = (jc * CC + dc) * 128
                return s_wdt[:, o:o + 128]

            def wout_ap(jc, dc):
                o = (jc * CC + dc) * 128
                return s_wout[:, o:o + 128]

            def wffn_ap(w, cc, dc):
                o = (cc * CC + dc) * 128
                return w[:, o:o + 128]

            t.wait_ge(sem["dsm"], 16 * 11)
            t.wait_ge(sem["vin"], 1)
            for gi in range(NG):
                xa = s_xa0 if gi % 2 == 0 else s_xa1
                t.wait_ge(sem["dxa"], 16 * (gi + 1))
                def win_jc(jc):
                    i = gi * JC + jc
                    if gi == 0:
                        t.wait_ge(sem["dw"], 16 * (jc // 8 + 1))
                    if i >= 2:
                        t.wait_ge(sem["sea"], i - 1)
                    for cc in range(CC):
                        mm = t.matmul(pb[jc % 2][:], win_ap(cc, jc),
                                      xa[:, cc * N:(cc + 1) * N],
                                      start=(cc == 0), stop=(cc == CC - 1))
                    mm.then_inc(sem["tpa"], 1)

                for jc in range(JI):
                    win_jc(jc)

                def mean_mm(dc):
                    t.wait_ge(sem["sdt"], gi * CC + dc + 1)
                    if dc == 0:
                        t.wait_ge(sem["vss"], gi)
                    dl = s_del0 if dc % 2 == 0 else s_del1
                    mm = t.matmul(pb[5][0:1, :], s_onescb[:], dl[:],
                                  start=(dc == 0), stop=(dc == CC - 1))
                    mm.then_inc(sem["tpm"], 1)

                if gi == 0:
                    t.wait_ge(sem["dw"], 16 * 6)
                t.wait_ge(sem["sea"], gi * JC + JI)
                for dc in range(CC):
                    j = gi * CC + dc
                    if j >= 2:
                        t.wait_ge(sem["sdt"], j - 1)
                    for jc in range(JI):
                        mm = t.matmul(pb[2 + dc % 2][:], wdt_ap(jc, dc),
                                      s_xin[:, jc * N:(jc + 1) * N],
                                      start=(jc == 0), stop=(jc == JI - 1))
                    mm.then_inc(sem["tpd"], 1)
                    if dc >= 2:
                        mean_mm(dc - 2)
                mean_mm(CC - 2)
                mean_mm(CC - 1)
                if gi == 0:
                    t.wait_ge(sem["dw"], 16 * 7)
                t.wait_ge(sem["vbx"], gi)
                for jc in range(JI):
                    mm = t.matmul(pb[4][0:DS, :], s_wxp[:, jc * DS:(jc + 1) * DS],
                                  s_xin[:, jc * N:(jc + 1) * N],
                                  start=(jc == 0), stop=(jc == JI - 1))
                mm.then_inc(sem["tpx"], 1)
                for jc in range(JI, JC):
                    win_jc(jc)
                t.wait_ge(sem["vd"], gi + 1)
                t.wait_ge(sem["ssa"], gi + 1)
                t.matmul(pb[5][0:DS, :], s_ones1_64[:], s_dsb[:],
                         start=True, stop=True).then_inc(sem["tdb"], 1)
                if gi == 0:
                    t.wait_ge(sem["dw"], 16 * 9)
                t.wait_ge(sem["vz"], (gi + 1) * JI)
                for dc in range(CC):
                    j = gi * CC + dc
                    if j >= 2:
                        t.wait_ge(sem["sso"], j - 1)
                    for jc in range(JI):
                        mm = t.matmul(pb[6 + dc % 2][:], wout_ap(jc, dc),
                                      s_xin[:, jc * N:(jc + 1) * N],
                                      start=(jc == 0), stop=(jc == JI - 1))
                    mm.then_inc(sem["tpo"], 1)
                t.wait_ge(sem["vsc"], gi + 1)
                t.wait_ge(sem["vss"], gi)
                sbf = s_s0 if gi % 2 == 0 else s_s1
                t.matmul(pb[5][0:1, :], s_ones64c[:], sbf[:],
                         start=True, stop=True).then_inc(sem["tss"], 1)

            if stage < 2:
                return
            t.wait_ge(sem["vss"], NG)
            t.matmul(pb[5][:, 0:1], s_ones1b[:], s_ssum[0:1, T - 1:T],
                     start=True, stop=True).then_inc(sem["tss"], 1)
            t.wait_ge(sem["vpay"], 2)
            for gi in range(NG):
                if gi >= 2:
                    t.wait_ge(sem["vcor"], gi - 1)
                mm = t.matmul(pb[6 + gi % 2][0:1, :], s_hinb[:],
                              s_G[:, gi * N:(gi + 1) * N], start=True, stop=True)
                mm.then_inc(sem["tcr"], 1)
            t.wait_ge(sem["vbnd"], 1)
            t.matmul(pb[5][0:1, 0:2 * CC], s_onescf[:], s_junk[:, 0:2 * CC],
                     start=True, stop=True).then_inc(sem["tB"], 1)
            t.wait_ge(sem["ssa"], NG + 2)
            t.matmul(pb[5][:, 2:4], s_ones1[:], s_stt2[:],
                     start=True, stop=True).then_inc(sem["tB"], 1)

            if stage < 3:
                return

            def t_ps(g):
                if g == 0:
                    t.wait_ge(sem["vcor"], NG)
                    t.wait_ge(sem["dw"], 16 * 12)
                t.wait_ge(sem["vB"], g)
                t.matmul(pb[0][:], s_ones1b[:], s_ssum[0:1, g * N:(g + 1) * N],
                         start=True, stop=True).then_inc(sem["tB"], 1)

            def t_ln_stats(g, mean_src, sq_src, mb, qb, vub_n, vst_n, sbc_n,
                           usem="vub"):
                t.wait_ge(sem[usem], vub_n)
                if vst_n:
                    t.wait_ge(sem["vst"], vst_n)
                if sbc_n:
                    t.wait_ge(sem["sbc"], sbc_n)
                for cc in range(CC):
                    sl = slice(cc * N, (cc + 1) * N)
                    t.matmul(mb[0:1, :], s_onescb[:], mean_src[:, sl],
                             start=(cc == 0), stop=(cc == CC - 1))
                for cc in range(CC):
                    sl = slice(cc * N, (cc + 1) * N)
                    mm = t.matmul(qb[0:1, :], s_onescb[:], sq_src[:, sl],
                                  start=(cc == 0), stop=(cc == CC - 1))
                mm.then_inc(sem["tB"], 1)

            def t_ln_bcast(mb, qb, base, vst_n, srs_n):
                t.wait_ge(sem["vst"], vst_n)
                t.matmul(mb[:], s_ones1[:], s_stat[0:1, base:base + N],
                         start=True, stop=True).then_inc(sem["tB"], 1)
                t.wait_ge(sem["srs"], srs_n)
                t.matmul(qb[:], s_ones1[:], s_stat[0:1, base + N:base + 2 * N],
                         start=True, stop=True).then_inc(sem["tB"], 1)

            def t_ln1(g):
                t_ln_stats(g, s_x1, s_usq1, pb[1], pb[2], VUB1[g],
                           VST1[g - 1] if g >= 1 else 0,
                           SBC1[g - 1] if g >= 1 else 0)
                t_ln_bcast(pb[1], pb[2], 0, VST1[g], SRS1[g])

            def t_ln2(g):
                t_ln_stats(g, s_u2, s_usq1, pb[3], pb[4], g + 1,
                           VST2[g - 1] if g >= 1 else 0,
                           SBC2[g - 1] if g >= 1 else 0, usem="vu2")
                t_ln_bcast(pb[3], pb[4], 3 * N, VST2[g], SRS2[g])

            t_ps(0)
            t_ln1(0)
            for gi in range(NG):
                if gi < NG - 1:
                    t_ps(gi + 1)
                t.wait_ge(sem["vmxr"], gi + 1)
                fi = 0
                for dc in range(CC):
                    if fi < 2:
                        if 8 * gi + fi - 1 > 0:
                            t.wait_ge(sem["vgt"], 8 * gi + fi - 1)
                    else:
                        t.wait_ge(sem["pfc"], 16 * gi + fi - 1)
                    for cc in range(CC):
                        mm = t.matmul(pb[5 + fi % 2][:], wffn_ap(s_wr, cc, dc),
                                      s_xrf[:, cc * N:(cc + 1) * N],
                                      start=(cc == 0), stop=(cc == CC - 1))
                    mm.then_inc(sem["tB"], 1)
                    fi += 1
                if gi >= 1:
                    t_ln2(gi - 1)
                if gi < NG - 1:
                    t_ln1(gi + 1)
                t.wait_ge(sem["vmxk"], gi + 1)
                for dc in range(CC):
                    t.wait_ge(sem["pfc"], 16 * gi + fi - 1)
                    for cc in range(CC):
                        mm = t.matmul(pb[5 + fi % 2][:], wffn_ap(s_wk, cc, dc),
                                      s_xkf[:, cc * N:(cc + 1) * N],
                                      start=(cc == 0), stop=(cc == CC - 1))
                    mm.then_inc(sem["tB"], 1)
                    fi += 1
                for cc in range(CC):
                    if cc == 0:
                        t.wait_ge(sem["pfc"], 16 * gi + 16)
                    elif cc == 1:
                        pass
                    else:
                        t.wait_ge(sem["vgt"], 8 * gi + fi - 17)
                    for dc in range(CC):
                        mm = t.matmul(pb[5 + fi % 2][:], wffn_ap(s_wv, dc, cc),
                                      s_kact[:, dc * N:(dc + 1) * N],
                                      start=(dc == 0), stop=(dc == CC - 1))
                    mm.then_inc(sem["tB"], 1)
                    fi += 1
            t_ln2(NG - 1)

        @block.scalar
        def _(s):
            s.wait_ge(sem["dsm"], 16 * 11)
            s.activation(s_A[:], s_alog[:], AF.Exp)
            s.memzero(s_junks[:])
            s.memzero(s_junks[:]).then_inc(sem["ssa"], 1)
            for gi in range(NG):
                for jc in range(JI):
                    s.wait_ge(sem["tpa"], gi * JC + jc + 1)
                    if gi > 0 and jc == 0:
                        s.wait_ge(sem["tpo"], gi * CC)
                    s.activation(s_xin[:, jc * N:(jc + 1) * N], pb[jc % 2][:],
                                 AF.Silu, bias=s_bin[:, jc:jc + 1]
                                 ).then_inc(sem["sea"], 1)
                for dc in range(CC):
                    j = gi * CC + dc
                    s.wait_ge(sem["tpd"], j + 1)
                    if j >= 2:
                        s.wait_ge(sem["tpm"], j - 1)
                    dl = s_del0 if dc % 2 == 0 else s_del1
                    s.activation(dl[:], pb[2 + dc % 2][:], AF.Exp,
                                 bias=s_bdt[:, dc:dc + 1])
                    s.activation(dl[:], dl[:], AF.Ln,
                                 bias=s_c1[:]).then_inc(sem["sdt"], 1)
                for jc in range(JI, JC):
                    s.wait_ge(sem["tpa"], gi * JC + jc + 1)
                    if gi * JI + (jc - JI) >= 2:
                        s.wait_ge(sem["vz"], gi * JI + (jc - JI) - 1)
                    dst = (s_sres0 if jc % 2 == 0 else s_sres1)[:]
                    s.activation(dst, pb[jc % 2][:], AF.Silu,
                                 bias=s_bin[:, jc:jc + 1]).then_inc(sem["sea"], 1)
                s.wait_ge(sem["tdb"], gi + 1)
                s.wait_ge(sem["vsc"], gi)
                s.activation(s_a[:], pb[5][0:DS, :], AF.Exp,
                             scale=s_A[:]).then_inc(sem["ssa"], 1)
                for dc in range(CC):
                    s.wait_ge(sem["tpo"], gi * CC + dc + 1)
                    if gi > 0:
                        h = 0 if dc < 4 else 1
                        s.wait_ge(sem["dso"], 16 * (2 * (gi - 1) + h + 1))
                    buf = s_sp0 if dc < 4 else s_sp1
                    s.activation(buf[:, (dc % 4) * N:(dc % 4 + 1) * N],
                                 pb[6 + dc % 2][:], AF.Copy).then_inc(sem["sso"], 1)
            if stage < 2:
                return
            s.wait_ge(sem["vbnd"], 2)
            s.activation(s_junk[0:1, 24:25], s_stt2[0:1, 1:2], AF.Ln,
                         bias=s_ceps[0:1, 0:1])
            s.memzero(s_junks[:])
            s.memzero(s_junks[:])
            s.activation(s_stt2[0:1, 1:2], s_junk[0:1, 24:25], AF.Exp,
                         scale=s_cm05[0:1, 0:1])
            s.memzero(s_junks[:])
            s.memzero(s_junks[:]).then_inc(sem["ssa"], 1)
            if stage < 3:
                return

            def s_rstd(vst_n, base):
                s.wait_ge(sem["vst"], vst_n)
                s.activation(s_stat[0:1, base + 2 * N:base + 3 * N],
                             s_stat[0:1, base + 2 * N:base + 3 * N],
                             AF.Ln, bias=s_ceps[0:1, 0:1])
                s.activation(s_stat[0:1, base + N:base + 2 * N],
                             s_stat[0:1, base + 2 * N:base + 3 * N],
                             AF.Exp, scale=s_cm05[0:1, 0:1])
                s.memzero(s_junks[:])
                s.memzero(s_junks[:]).then_inc(sem["srs"], 1)

            def s_evac(tb_m, tb_r, mb, qb, dst_m, dst_r):
                s.wait_ge(sem["tB"], tb_m)
                s.activation(dst_m[:], mb[:], AF.Copy)
                s.wait_ge(sem["tB"], tb_r)
                s.activation(dst_r[:], qb[:], AF.Copy)
                s.memzero(s_junks[:])
                s.memzero(s_junks[:]).then_inc(sem["sbc"], 2)

            s_rstd(VST1[0], 0)
            s_evac(TB_LN1BM[0], TB_LN1BR[0], pb[1], pb[2], s_sres0, s_sres1)
            for gi in range(NG):
                for dc in range(CC):
                    s.wait_ge(sem["tB"], TB_WR[gi][dc])
                    s.activation(s_r[:, dc * N:(dc + 1) * N], pb[5 + dc % 2][:],
                                 AF.Sigmoid).then_inc(sem["pfc"], 1)
                if gi >= 1:
                    s_rstd(VST2[gi - 1], 3 * N)
                    s_evac(TB_LN2BM[gi - 1], TB_LN2BR[gi - 1], pb[3], pb[4],
                           s_gt0, s_gt1)
                if gi < NG - 1:
                    s_rstd(VST1[gi + 1], 0)
                    s_evac(TB_LN1BM[gi + 1], TB_LN1BR[gi + 1], pb[1], pb[2],
                           s_sres0, s_sres1)
                for dc in range(CC):
                    s.wait_ge(sem["tB"], TB_WK[gi][dc])
                    ka = s_kact[:, dc * N:(dc + 1) * N]
                    s.activation(ka, pb[5 + dc % 2][:], AF.Relu)
                    s.activation(ka, ka, AF.Square).then_inc(sem["pfc"], 1)
                s.wait_ge(sem["vgt"], 8 * gi + 8)
                for dc in range(CC):
                    sl = slice(dc * N, (dc + 1) * N)
                    s.activation(s_usq1[:, sl], s_u2[:, sl], AF.Square)
                s.memzero(s_junks[:])
                s.memzero(s_junks[:]).then_inc(sem["vu2"], 1)
            s_rstd(VST2[NG - 1], 3 * N)
            s_evac(TB_LN2BM[NG - 1], TB_LN2BR[NG - 1], pb[3], pb[4],
                   s_gt0, s_gt1)

        @block.vector
        def _(v):
            def pad():
                v.memset(s_junk[:, 64:96], 0.0)

            v.wait_ge(sem["dsm"], 16 * 11)
            v.memset(s_ones1[:], 1.0)
            v.memset(s_ones1b[:], 1.0)
            v.memset(s_ones1_64[:], 1.0)
            v.memset(s_ones64c[:], 1.0)
            v.memset(s_onescb[:], 1.0)
            v.memset(s_onescf[:], 1.0)
            v.memset(s_c1[:], 1.0)
            v.memset(s_ceps[:], 1e-5)
            v.memset(s_cm05[:], -0.5)
            v.memset(s_pay[:], 0.0)
            v.memset(s_junk[:], 0.0)
            v.wait_ge(sem["ssa"], 1)
            v.tensor_scalar(s_A[:], s_A[:], -1.0, None, ALU.mult)
            v.tensor_scalar(s_dp1[:], s_dp1[:], 1.0, None, ALU.add)
            pad()
            v.engine_nop().then_inc(sem["vin"], 1)

            for gi in range(NG):
                v.wait_ge(sem["tpx"], gi + 1)
                v.wait_ge(sem["tpd"], CC * (gi + 1))
                for jc in range(JI):
                    v.wait_ge(sem["sea"], gi * JC + JI + jc + 1)
                    sr = s_sres0 if jc % 2 == 0 else s_sres1
                    v.tensor_mul(s_xin[:, jc * N:(jc + 1) * N],
                                 s_xin[:, jc * N:(jc + 1) * N], sr[:]
                                 ).then_inc(sem["vz"], 1)
                v.wait_ge(sem["tpm"], CC * (gi + 1))
                v.tensor_scalar(s_dsb[:], pb[5][0:1, :], 1.0 / C, None,
                                ALU.mult)
                pad(); pad()
                v.engine_nop().then_inc(sem["vd"], 1)
                v.wait_ge(sem["tpx"], gi + 1)
                v.tensor_scalar(s_bx[:], pb[4][0:DS, :], s_bxp[:], None,
                                ALU.add).then_inc(sem["vbx"], 1)
                v.wait_ge(sem["ssa"], gi + 2)
                sbf = s_s0 if gi % 2 == 0 else s_s1
                pbf = s_s1 if gi % 2 == 0 else s_s0
                if gi >= 2:
                    v.wait_ge(sem["tss"], gi - 1)
                init_s = 0.0 if gi == 0 else pbf[:, N - 1:N]
                init_g = 1.0 if gi == 0 else s_G[:, gi * N - 1:gi * N]
                v.tensor_tensor_scan(sbf[:], s_a[:], s_bx[:], init_s,
                                     ALU.mult, ALU.add)
                v.tensor_tensor_scan(s_G[:, gi * N:(gi + 1) * N], s_a[:],
                                     s_bx[:], init_g, ALU.mult, ALU.bypass)
                v.engine_nop().then_inc(sem["vsc"], 1)
                v.wait_ge(sem["tss"], gi + 1)
                v.tensor_copy(s_ssum[0:1, gi * N:(gi + 1) * N], pb[5][0:1, :])
                pad()
                v.engine_nop().then_inc(sem["vss"], 1)

            if stage < 2:
                return
            sbf7 = s_s0 if (NG - 1) % 2 == 0 else s_s1
            v.wait_ge(sem["sso"], CC * NG)
            v.tensor_copy(s_pay[0:DS, 0:1], sbf7[:, N - 1:N])
            v.wait_ge(sem["tss"], NG + 1)
            v.tensor_copy(s_slastc[:], pb[5][:, 0:1])
            pad(); pad()
            for cc in range(CC):
                sp = s_sp0 if cc < 4 else s_sp1
                col = (cc % 4) * N + (N - 1)
                v.tensor_scalar(s_pay[:, 1 + cc:2 + cc], sp[:, col:col + 1],
                                s_slastc[:], None, ALU.mult)
            pad(); pad(); pad()
            v.engine_nop().then_inc(sem["vpay"], 1)
            v.wait_ge(sem["dgat"], 16)
            v.tensor_mul(s_gath[:], s_gath[:], s_rmask[:])
            pad(); pad()
            half = NCORES * 9 // 2
            v.tensor_add(s_gath[:, 0:half], s_gath[:, 0:half], s_gath[:, half:])
            pad(); pad()
            v.tensor_add(s_gath[:, 0:18], s_gath[:, 0:18], s_gath[:, 18:36])
            pad(); pad()
            v.tensor_add(s_sel[:, 0:9], s_gath[:, 0:9], s_gath[:, 9:18])
            pad(); pad()
            v.tensor_copy(s_hinb[:], s_sel[0:DS, 0:1])
            pad(); pad()
            v.engine_nop().then_inc(sem["vpay"], 1)
            for gi in range(NG):
                v.wait_ge(sem["tcr"], gi + 1)
                sl = slice(gi * N, (gi + 1) * N)
                if gi < NG - 1:
                    v.tensor_add(s_ssum[0:1, sl], s_ssum[0:1, sl],
                                 pb[6 + gi % 2][0:1, :]).then_inc(sem["vcor"], 1)
                else:
                    v.tensor_add(s_ssum[0:1, sl], s_ssum[0:1, sl],
                                 pb[6 + gi % 2][0:1, :])
                    pad(); pad()
                    v.engine_nop().then_inc(sem["vcor"], 1)
            v.tensor_mul(s_ub[:], s_xbT[:], s_dp1[:])
            pad(); pad()
            v.tensor_add(s_ub[:], s_ub[:], s_bout[:])
            pad(); pad()
            v.tensor_add(s_ub[:], s_ub[:], s_sel[:, 1:9])
            pad(); pad()
            v.tensor_copy(s_junk[:, 0:CC], s_ub[:])
            v.tensor_mul(s_junk[:, CC:2 * CC], s_ub[:], s_ub[:])
            pad(); pad()
            v.engine_nop().then_inc(sem["vbnd"], 1)
            v.wait_ge(sem["tB"], 1)
            v.reduce_sum(s_stt2[0:1, 0:1], pb[5][0:1, 0:CC], axis=AX)
            v.reduce_sum(s_stt2[0:1, 1:2], pb[5][0:1, CC:2 * CC], axis=AX)
            pad(); pad()
            v.tensor_scalar(s_stt2[0:1, 0:2], s_stt2[0:1, 0:2], 1.0 / C, None,
                            ALU.mult)
            pad(); pad()
            v.tensor_mul(s_junk[0:1, 16:17], s_stt2[0:1, 0:1], s_stt2[0:1, 0:1])
            pad(); pad()
            v.tensor_sub(s_stt2[0:1, 1:2], s_stt2[0:1, 1:2], s_junk[0:1, 16:17])
            pad(); pad()
            v.engine_nop().then_inc(sem["vbnd"], 1)
            v.wait_ge(sem["ssa"], NG + 2)
            v.wait_ge(sem["tB"], 2)
            v.tensor_copy(s_junk[:, 32:34], pb[5][:, 2:4])
            pad(); pad()
            v.tensor_scalar(s_ub[:], s_ub[:], s_junk[:, 32:33],
                            s_junk[:, 33:34], ALU.subtract, ALU.mult)
            pad(); pad()
            v.tensor_scalar(s_prevc[:], s_ub[:], s_hmask[:], None, ALU.mult)
            pad(); pad()
            v.engine_nop().then_inc(sem["vbnd"], 1)

            if stage < 3:
                return

            def v_u_steps(g):
                xb = s_xb0 if g % 2 == 0 else s_xb1
                v.wait_ge(sem["dxb"], 16 * (g + 1))
                v.wait_ge(sem["dmb"], 16 * (g + 1))
                v.wait_ge(sem["tB"], TB_PS[g])
                for cc in range(CC):
                    sl = slice(cc * N, (cc + 1) * N)
                    v.tensor_mul(s_mmb[:, sl], s_mmb[:, sl], pb[0][:])
                for cc in range(CC):
                    sl = slice(cc * N, (cc + 1) * N)
                    v.scalar_tensor_tensor(s_x1[:, sl], xb[:, sl],
                                           s_dp1[:, cc:cc + 1], s_mmb[:, sl],
                                           ALU.mult, ALU.add)
                for cc in range(CC):
                    sl = slice(cc * N, (cc + 1) * N)
                    v.tensor_scalar(s_x1[:, sl], s_x1[:, sl],
                                    s_bout[:, cc:cc + 1], None, ALU.add)
                v.engine_nop().then_inc(sem["vB"], 1)

            def v_usq(src, dst):
                for cc in range(CC):
                    sl = slice(cc * N, (cc + 1) * N)
                    v.tensor_mul(dst[:, sl], src[:, sl], src[:, sl])
                v.engine_nop().then_inc(sem["vub"], 1)

            def v_ln_vals(tb_n, base, mbank, qbank):
                v.wait_ge(sem["tB"], tb_n)
                v.tensor_scalar(s_stat[0:1, base:base + N], mbank[0:1, :],
                                1.0 / C, None, ALU.mult)
                v.tensor_mul(s_stat[0:1, base + 2 * N:base + 3 * N],
                             s_stat[0:1, base:base + N],
                             s_stat[0:1, base:base + N])
                v.scalar_tensor_tensor(s_stat[0:1, base + 2 * N:base + 3 * N],
                                       qbank[0:1, :], 1.0 / C,
                                       s_stat[0:1, base + 2 * N:base + 3 * N],
                                       ALU.mult, ALU.subtract)
                pad(); pad()
                v.engine_nop().then_inc(sem["vst"], 2)

            def v_apply(g):
                v.wait_ge(sem["sbc"], SBC1[g])
                for cc in range(CC):
                    sl = slice(cc * N, (cc + 1) * N)
                    v.tensor_sub(s_x1[:, sl], s_x1[:, sl], s_sres0[:])
                for cc in range(CC):
                    sl = slice(cc * N, (cc + 1) * N)
                    v.tensor_mul(s_x1[:, sl], s_x1[:, sl], s_sres1[:])

            def v_mixr(g):
                for cc in range(CC):
                    o = cc * N
                    v.tensor_sub(s_xkf[:, o + 1:o + N], s_x1[:, o + 1:o + N],
                                 s_x1[:, o:o + N - 1])
                for cc in range(CC):
                    o = cc * N
                    v.tensor_sub(s_xkf[:, o:o + 1], s_x1[:, o:o + 1],
                                 s_prevc[:, cc:cc + 1])
                for cc in range(CC):
                    o = cc * N
                    v.scalar_tensor_tensor(s_xrf[:, o + 1:o + N],
                                           s_xkf[:, o + 1:o + N],
                                           s_mur[:, cc:cc + 1],
                                           s_x1[:, o:o + N - 1],
                                           ALU.mult, ALU.add)
                for cc in range(CC):
                    o = cc * N
                    v.scalar_tensor_tensor(s_xrf[:, o:o + 1], s_xkf[:, o:o + 1],
                                           s_mur[:, cc:cc + 1],
                                           s_prevc[:, cc:cc + 1],
                                           ALU.mult, ALU.add)
                pad()
                v.engine_nop().then_inc(sem["vmxr"], 1)

            def v_mixk(g):
                for cc in range(CC):
                    o = cc * N
                    v.scalar_tensor_tensor(s_xkf[:, o:o + 1], s_xkf[:, o:o + 1],
                                           s_muk[:, cc:cc + 1],
                                           s_prevc[:, cc:cc + 1],
                                           ALU.mult, ALU.add)
                for cc in range(CC):
                    o = cc * N
                    v.scalar_tensor_tensor(s_xkf[:, o + 1:o + N],
                                           s_xkf[:, o + 1:o + N],
                                           s_muk[:, cc:cc + 1],
                                           s_x1[:, o:o + N - 1],
                                           ALU.mult, ALU.add)
                for cc in range(CC):
                    o = cc * N
                    v.tensor_copy(s_prevc[:, cc:cc + 1], s_x1[:, o + N - 1:o + N])
                pad()
                v.engine_nop().then_inc(sem["vmxk"], 1)

            v.wait_ge(sem["dw"], 16 * 12)
            v_u_steps(0)
            v_usq(s_x1, s_usq1)
            v_ln_vals(TB_LN1S[0], 0, pb[1], pb[2])
            v_apply(0)
            if dbg:
                pad()
                v.engine_nop().then_inc(sem["vdbg"], 1)
                v.wait_ge(sem["ddbg"], 16)
            v_mixr(0)
            v_mixk(0)
            for gi in range(NG):
                for cc in range(CC):
                    sl = slice(cc * N, (cc + 1) * N)
                    v.tensor_copy(s_u2[:, sl], s_x1[:, sl])
                if gi < NG - 1:
                    v_u_steps(gi + 1)
                    v_usq(s_x1, s_usq1)
                    v_ln_vals(TB_LN1S[gi + 1], 0, pb[1], pb[2])
                    v_apply(gi + 1)
                    v.wait_ge(sem["tB"], TB_WK[gi][CC - 1])
                    v_mixr(gi + 1)
                    v_mixk(gi + 1)
                if dbg and gi == 0:
                    v.wait_ge(sem["pfc"], 16)
                    pad()
                    v.engine_nop().then_inc(sem["vdbg"], 2)
                    v.wait_ge(sem["ddbg"], 48)
                v.wait_ge(sem["sbc"], SBC2[gi - 1] if gi >= 1 else 0)
                for cc in range(CC):
                    sl = slice(cc * N, (cc + 1) * N)
                    gt = s_gt0 if cc % 2 == 0 else s_gt1
                    v.wait_ge(sem["tB"], TB_WV[gi][cc])
                    v.tensor_mul(gt[:], s_r[:, sl], pb[5 + (16 + cc) % 2][:])
                    v.tensor_add(s_u2[:, sl], s_u2[:, sl], gt[:]
                                 ).then_inc(sem["vgt"], 1)
                if dbg and gi == 0:
                    pad()
                    v.engine_nop().then_inc(sem["vdbg"], 1)
                    v.wait_ge(sem["ddbg"], 64)
                v_ln_vals(TB_LN2S[gi], 3 * N, pb[3], pb[4])
                v.wait_ge(sem["sbc"], SBC2[gi])
                for cc in range(CC):
                    k = gi * CC + cc
                    if k >= 4:
                        v.wait_ge(sem["dod"], 16 * (k - 3))
                    sl = slice(cc * N, (cc + 1) * N)
                    ost = (s_ost0, s_ost1, s_ost2, s_ost3)[cc % 4]
                    v.tensor_sub(ost[:], s_u2[:, sl], s_gt0[:])
                    v.tensor_mul(ost[:], ost[:], s_gt1[:]).then_inc(sem["vod"], 1)

    nc.finalize()
    return nc


_NC_CACHE = {}


def _get_nc():
    if "nc" not in _NC_CACHE:
        _NC_CACHE["nc"] = build_nc()
    return _NC_CACHE["nc"]


def _chunk_weight(w, kchunks):
    K, M = w.shape
    assert K == kchunks * 128
    return np.ascontiguousarray(
        w.reshape(kchunks, 128, M).transpose(1, 0, 2).reshape(128, kchunks * M))


def _col_tile(vec, ncol):
    return np.ascontiguousarray(vec.reshape(ncol, 128).T)


def stage_inputs(L0_attn_x, L0_attn_W_in, L0_attn_b_in, L0_attn_W_xp,
                 L0_attn_b_xp, L0_attn_W_dt, L0_attn_b_dt, L0_attn_W_out,
                 L0_attn_b_out, L0_attn_A_log, L0_attn_D, L0_ffn_W_r,
                 L0_ffn_W_k, L0_ffn_W_v, L0_ffn_mu_r, L0_ffn_mu_k, T=T_FULL):
    f = np.float32
    x = np.asarray(L0_attn_x, f)
    w_in = np.asarray(L0_attn_W_in, f)
    win = np.ascontiguousarray(
        w_in.reshape(CC, 128, JC, 128).transpose(1, 2, 0, 3).reshape(128, -1))
    wdt = _chunk_weight(np.asarray(L0_attn_W_dt, f), JI)
    wxp = _chunk_weight(np.asarray(L0_attn_W_xp, f), JI)
    wout = _chunk_weight(np.asarray(L0_attn_W_out, f), JI)
    wr = _chunk_weight(np.ascontiguousarray(np.asarray(L0_ffn_W_r, f).T), CC)
    wk = _chunk_weight(np.ascontiguousarray(np.asarray(L0_ffn_W_k, f).T), CC)
    wv = _chunk_weight(np.ascontiguousarray(np.asarray(L0_ffn_W_v, f).T), CC)
    bin_t = _col_tile(np.asarray(L0_attn_b_in, f), JC)
    bdt_t = _col_tile(np.asarray(L0_attn_b_dt, f), CC)
    bout_t = _col_tile(np.asarray(L0_attn_b_out, f), CC)
    bxp_t = np.ascontiguousarray(np.asarray(L0_attn_b_xp, f).reshape(DS, 1))
    alog_t = np.ascontiguousarray(np.asarray(L0_attn_A_log, f).reshape(DS, 1))
    dvec_t = _col_tile(np.asarray(L0_attn_D, f), CC)
    mur_t = _col_tile(np.asarray(L0_ffn_mu_r, f), CC)
    muk_t = _col_tile(np.asarray(L0_ffn_mu_k, f), CC)

    shared = dict(win=win, wdt=wdt, wxp=wxp, wout=wout, wr=wr, wk=wk, wv=wv,
                  bin=bin_t, bdt=bdt_t, bout=bout_t, bxp=bxp_t, alog=alog_t,
                  dvec=dvec_t, mur=mur_t, muk=muk_t)

    in_maps = []
    for c in range(NCORES):
        b, h = c // 2, c % 2
        xT = np.ascontiguousarray(x[b, h * T:(h + 1) * T, :].T)
        if h == 1:
            xbT = _col_tile(x[b, T - 1, :], CC)
            hmask = np.ones((128, 1), f)
            rmask = np.zeros((128, NCORES * 9), f)
            rmask[:, (c - 1) * 9:c * 9] = 1.0
        else:
            xbT = np.zeros((128, CC), f)
            hmask = np.zeros((128, 1), f)
            rmask = np.zeros((128, NCORES * 9), f)
        in_maps.append(dict(shared, xT=xT, xbT=xbT, hmask=hmask, rmask=rmask))
    return in_maps


def kernel(**inputs):
    in_maps = stage_inputs(**inputs)
    nc = _get_nc()
    res = run_bass_kernel_spmd(nc, in_maps, core_ids=list(range(NCORES)))
    f = np.float32
    out = np.empty((B_, S_, C), f)
    for c in range(NCORES):
        b, h = c // 2, c % 2
        out[b, h * T_FULL:(h + 1) * T_FULL, :] = np.asarray(
            res.results[c]["outT"], f).T
    return out


# version 12
# speedup vs baseline: 1.0357x; 1.0291x over previous
import numpy as np
import ml_dtypes
import concourse.bass as bass
import concourse.mybir as mybir
from concourse.bass_utils import run_bass_kernel_spmd

F32 = mybir.dt.float32
BF16 = mybir.dt.bfloat16
AF = mybir.ActivationFunctionType
ALU = mybir.AluOpType
AX = mybir.AxisListType.X

NCORES = 8
B_, S_, C = 4, 8192, 1024
T_FULL = S_ // 2
N = 512
CC = C // 128
JC = 32
DI = 2048
JI = DI // 128
DS = 64


def build_nc(T=T_FULL, stage=3, dbg=False):
    NG = T // N
    nc = bass.Bass("TRN2", target_bir_lowering=False,
                   detect_race_conditions=False)

    d_xT = nc.dram_tensor("xT", [C, T], F32, kind="ExternalInput")
    d_win = nc.dram_tensor("win", [128, JC * CC * 128], F32, kind="ExternalInput")
    d_wdt = nc.dram_tensor("wdt", [128, JI * CC * 128], F32, kind="ExternalInput")
    d_wxp = nc.dram_tensor("wxp", [128, JI * DS], F32, kind="ExternalInput")
    d_wout = nc.dram_tensor("wout", [128, JI * CC * 128], F32, kind="ExternalInput")
    d_wr = nc.dram_tensor("wr", [128, CC * CC * 128], F32, kind="ExternalInput")
    d_wk = nc.dram_tensor("wk", [128, CC * CC * 128], F32, kind="ExternalInput")
    d_wv = nc.dram_tensor("wv", [128, CC * CC * 128], F32, kind="ExternalInput")
    d_bin = nc.dram_tensor("bin", [128, JC], F32, kind="ExternalInput")
    d_bdt = nc.dram_tensor("bdt", [128, CC], F32, kind="ExternalInput")
    d_bout = nc.dram_tensor("bout", [128, CC], F32, kind="ExternalInput")
    d_bxp = nc.dram_tensor("bxp", [DS, 1], F32, kind="ExternalInput")
    d_alog = nc.dram_tensor("alog", [DS, 1], F32, kind="ExternalInput")
    d_dvec = nc.dram_tensor("dvec", [128, CC], F32, kind="ExternalInput")
    d_mur = nc.dram_tensor("mur", [128, CC], F32, kind="ExternalInput")
    d_muk = nc.dram_tensor("muk", [128, CC], F32, kind="ExternalInput")
    d_xbT = nc.dram_tensor("xbT", [128, CC], F32, kind="ExternalInput")
    d_hmask = nc.dram_tensor("hmask", [128, 1], F32, kind="ExternalInput")
    d_rmask = nc.dram_tensor("rmask", [128, NCORES * 9], F32, kind="ExternalInput")
    d_out = nc.dram_tensor("outT", [C, T], F32, kind="ExternalOutput")
    if dbg:
        d_dbg1 = nc.dram_tensor("dbg1", [128, CC * N], F32, kind="ExternalOutput")
        d_dbg2 = nc.dram_tensor("dbg2", [128, CC * N], F32, kind="ExternalOutput")
        d_dbg3 = nc.dram_tensor("dbg3", [128, CC * N], F32, kind="ExternalOutput")
        d_dbg4 = nc.dram_tensor("dbg4", [128, CC * N], F32, kind="ExternalOutput")
    d_spill = nc.dram_tensor("spill", [C, T], BF16, kind="Internal")
    d_ccin = nc.dram_tensor("cc_in", [128, 9], F32, kind="Internal")
    d_ccout = nc.dram_tensor("cc_out", [NCORES * 128, 9], F32, kind="Internal")

    BASE = ((nc.sbuf_base + 31) // 32) * 32
    TOP = nc.sbuf_top
    cur = [BASE]

    def sb(name, shape, dtype, at=None):
        isz = 2 if dtype == BF16 else 4
        nb = int(np.prod(shape[1:])) * isz
        nb = ((nb + 31) // 32) * 32
        if at is None:
            off = cur[0]
            cur[0] += nb
        else:
            off = BASE_W0[0] + at
        assert off + nb <= TOP, (name, off, nb, TOP)
        return nc.alloc_sbuf_tensor_at(name, list(shape), dtype, offset=off)

    BASE_W0 = [0]
    s_ssum = sb("ssum", [1, T], BF16)
    s_G = sb("G", [DS, T], BF16)
    s_s0 = sb("s0", [DS, N], F32)
    s_s1 = sb("s1", [DS, N], F32)
    s_a = sb("a", [DS, N], F32)
    s_bx = sb("bx", [DS, N], F32)
    s_sres0 = sb("sres0", [128, N], BF16)
    s_sres1 = sb("sres1", [128, N], BF16)
    s_dsb = sb("dsb", [1, N], F32)
    s_bin = sb("v_bin", [128, JC], F32)
    s_bdt = sb("v_bdt", [128, CC], F32)
    s_bout = sb("v_bout", [128, CC], F32)
    s_bxp = sb("v_bxp", [DS, 1], F32)
    s_alog = sb("v_alog", [DS, 1], F32)
    s_A = sb("v_A", [DS, 1], F32)
    s_dp1 = sb("v_dp1", [128, CC], F32)
    s_mur = sb("v_mur", [128, CC], F32)
    s_muk = sb("v_muk", [128, CC], F32)
    s_xbT = sb("v_xbT", [128, CC], F32)
    s_hmask = sb("v_hmask", [128, 1], F32)
    s_ones1 = sb("ones1", [1, 128], F32)
    s_ones1b = sb("ones1b", [1, 128], BF16)
    s_ones1_64 = sb("ones1_64", [1, DS], F32)
    s_ones64c = sb("ones64c", [DS, 1], F32)
    s_onescb = sb("onescb", [128, 1], BF16)
    s_onescf = sb("onescf", [128, 1], F32)
    s_c1 = sb("c_one", [128, 1], F32)
    s_ceps = sb("c_eps", [1, 1], F32)
    s_cm05 = sb("c_m05", [1, 1], F32)
    s_junk = sb("junk", [128, 128], F32)
    s_junks = sb("junks", [128, 128], F32)
    s_pay = sb("pay", [128, 9], F32)
    s_gath = sb("gath", [128, NCORES * 9], F32)
    s_rmask = sb("rmaskT", [128, NCORES * 9], F32)
    s_sel = sb("sel", [128, 9], F32)
    s_hinb = sb("hinb", [DS, 1], BF16)
    s_ub = sb("ub", [128, CC], F32)
    s_prevc = sb("prevc", [128, CC], F32)
    s_slastc = sb("slastc", [128, 1], F32)
    s_stt2 = sb("stt2", [1, 2], F32)
    BASE_W0[0] = ((cur[0] + 31) // 32) * 32

    s_win = sb("w_in", [128, JC * CC * 128], BF16, at=0)
    s_wdt = sb("w_dt", [128, JI * CC * 128], BF16, at=65536)
    s_wxp = sb("w_xp", [128, JI * DS], BF16, at=98304)
    s_wout = sb("w_out", [128, JI * CC * 128], BF16, at=100352)
    s_xa0 = sb("xa0", [128, CC * N], BF16, at=133120)
    s_xa1 = sb("xa1", [128, CC * N], BF16, at=141312)
    s_xin = sb("xin", [128, JI * N], BF16, at=149504)
    s_del0 = sb("del0", [128, N], BF16, at=165888)
    s_del1 = sb("del1", [128, N], BF16, at=166912)
    s_sp0 = sb("sp0", [128, 4 * N], BF16, at=167936)
    s_sp1 = sb("sp1", [128, 4 * N], BF16, at=172032)
    s_diff = sb("diff", [128, CC * N], BF16, at=167936)
    A_END = 176128
    s_wr = sb("w_r", [128, CC * CC * 128], BF16, at=0)
    s_wk = sb("w_k", [128, CC * CC * 128], BF16, at=16384)
    s_wv = sb("w_v", [128, CC * CC * 128], BF16, at=32768)
    s_xb0 = sb("xb0", [128, CC * N], F32, at=49152)
    s_xb1 = sb("xb1", [128, CC * N], F32, at=65536)
    s_mmb = sb("mmb", [128, CC * N], F32, at=81920)
    s_x1 = sb("x1", [128, CC * N], BF16, at=98304)
    s_u2 = sb("u2", [128, CC * N], BF16, at=106496)
    s_kact = sb("kact", [128, CC * N], BF16, at=114688)
    s_r = sb("r", [128, CC * N], BF16, at=122880)
    s_usq1 = sb("usq1", [128, CC * N], BF16, at=131072)
    s_xrf = sb("xrf", [128, CC * N], BF16, at=139264)
    s_xkf = sb("xkf", [128, CC * N], BF16, at=147456)
    s_ost0 = sb("ost0", [128, N], F32, at=155648)
    s_ost1 = sb("ost1", [128, N], F32, at=157696)
    s_stat = sb("stat", [1, 6 * N], F32, at=159744)
    s_ost2 = sb("ost2", [128, N], F32, at=172032)
    s_ost3 = sb("ost3", [128, N], F32, at=174080)
    s_gt0 = sb("gt0", [128, N], BF16, at=176128)
    s_gt1 = sb("gt1", [128, N], BF16, at=177184)
    B_END = 178240
    assert BASE_W0[0] + max(A_END, B_END) <= TOP, (BASE_W0[0], TOP)

    pb = [nc.alloc_psum_tensor(f"pb{i}", [128, N], F32) for i in range(8)]

    semnames = [
        "dw", "dxa", "dxb", "dmb", "dso", "dod", "dsm", "dcc",
        "tpa", "tpd", "tpx", "tpm", "tdb", "tss", "tpo", "tgx", "t1",
        "tcr", "tB", "tst", "dpay", "dcol", "dgat",
        "sea", "sdt", "ssa", "sso", "srs", "pfc",
        "vin", "vz", "vd", "vsc", "vss", "vbx", "vpay", "vcor", "vbnd",
        "vB", "vub", "vst", "vmx", "vmxr", "vmxk", "vod", "vdbg", "ddbg", "sbc",
        "vgt", "vu2",
    ]
    sem = {n: nc.alloc_semaphore("sem_" + n) for n in semnames}
    rg = [list(range(NCORES))]

    TB_PS, TB_LN1S, TB_LN1BM, TB_LN1BR = {}, {}, {}, {}
    TB_WR, TB_WK, TB_WV = {}, {}, {}
    TB_LN2S, TB_LN2BM, TB_LN2BR = {}, {}, {}
    cnt = 2
    cnt += 1; TB_PS[0] = cnt
    cnt += 1; TB_LN1S[0] = cnt
    cnt += 1; TB_LN1BM[0] = cnt
    cnt += 1; TB_LN1BR[0] = cnt
    for g in range(NG):
        if g < NG - 1:
            cnt += 1; TB_PS[g + 1] = cnt
        TB_WR[g] = {}
        for dc in range(CC):
            cnt += 1; TB_WR[g][dc] = cnt
        if g >= 1:
            cnt += 1; TB_LN2S[g - 1] = cnt
            cnt += 1; TB_LN2BM[g - 1] = cnt
            cnt += 1; TB_LN2BR[g - 1] = cnt
        TB_WK[g] = {}
        for dc in range(CC):
            cnt += 1; TB_WK[g][dc] = cnt
        if g < NG - 1:
            cnt += 1; TB_LN1S[g + 1] = cnt
            cnt += 1; TB_LN1BM[g + 1] = cnt
            cnt += 1; TB_LN1BR[g + 1] = cnt
        TB_WV[g] = {}
        for cc in range(CC):
            cnt += 1; TB_WV[g][cc] = cnt
    cnt += 1; TB_LN2S[NG - 1] = cnt
    cnt += 1; TB_LN2BM[NG - 1] = cnt
    cnt += 1; TB_LN2BR[NG - 1] = cnt

    VB_N, VUB1, VUB2, VST1, VST2, VMXR, VMXK = {}, {}, {}, {}, {}, {}, {}
    vb = vub = vst = vmxr = vmxk = 0
    vb += 1; VB_N[0] = vb
    vub += 1; VUB1[0] = vub
    vst += 2; VST1[0] = vst
    vmxr += 1; VMXR[0] = vmxr
    vmxk += 1; VMXK[0] = vmxk
    for g in range(NG):
        if g < NG - 1:
            vb += 1; VB_N[g + 1] = vb
            vub += 1; VUB1[g + 1] = vub
            vst += 2; VST1[g + 1] = vst
            vmxr += 1; VMXR[g + 1] = vmxr
            vmxk += 1; VMXK[g + 1] = vmxk
        VUB2[g] = g + 1
        vst += 2; VST2[g] = vst
    SRS1, SRS2, SBC1, SBC2 = {}, {}, {}, {}
    srs = sbc = 0
    srs += 1; SRS1[0] = srs
    sbc += 2; SBC1[0] = sbc
    for g in range(NG):
        if g >= 1:
            srs += 1; SRS2[g - 1] = srs
            sbc += 2; SBC2[g - 1] = sbc
        if g < NG - 1:
            srs += 1; SRS1[g + 1] = srs
            sbc += 2; SBC1[g + 1] = sbc
    srs += 1; SRS2[NG - 1] = srs
    sbc += 2; SBC2[NG - 1] = sbc

    with nc.Block() as block:

        @block.gpsimd
        def _(g):
            def xa_load(gi):
                buf = s_xa0 if gi % 2 == 0 else s_xa1
                src = d_xT.ap().rearrange("(c p) t -> p c t", c=CC, p=128)
                src = src[:, :, gi * N:(gi + 1) * N]
                dst = buf[:].rearrange("p (c t) -> p c t", c=CC)
                g.dma_start(dst, src).then_inc(sem["dxa"], 16)

            def mmb_load(gi):
                src = d_spill.ap().rearrange("(c p) t -> p c t", c=CC, p=128)
                src = src[:, :, gi * N:(gi + 1) * N]
                dst = s_mmb[:].rearrange("p (c t) -> p c t", c=CC)
                g.dma_start(dst, src).then_inc(sem["dmb"], 16)

            xa_load(0)
            WQ = JC * CC * 128
            for q in range(4):
                sl = slice(q * WQ // 4, (q + 1) * WQ // 4)
                g.dma_start(s_win[:, sl], d_win[:, sl]).then_inc(sem["dw"], 16)
            if NG > 1:
                xa_load(1)
            WD = JI * CC * 128
            for q in range(2):
                sl = slice(q * WD // 2, (q + 1) * WD // 2)
                g.dma_start(s_wdt[:, sl], d_wdt[:, sl]).then_inc(sem["dw"], 16)
            g.dma_start(s_wxp[:], d_wxp[:]).then_inc(sem["dw"], 16)
            for q in range(2):
                sl = slice(q * WD // 2, (q + 1) * WD // 2)
                g.dma_start(s_wout[:, sl], d_wout[:, sl]).then_inc(sem["dw"], 16)
            for gi in range(2, NG):
                g.wait_ge(sem["tpa"], JC * (gi - 1))
                xa_load(gi)
            g.wait_ge(sem["tpa"], JC * NG)
            g.dma_start(s_wr[:], d_wr[:]).then_inc(sem["dw"], 16)
            g.dma_start(s_wk[:], d_wk[:]).then_inc(sem["dw"], 16)
            g.dma_start(s_wv[:], d_wv[:]).then_inc(sem["dw"], 16)
            if stage < 2:
                return
            if stage >= 3:
                g.wait_ge(sem["tpd"], CC * NG)
                g.wait_ge(sem["dso"], 16 * 2)
                mmb_load(0)
            g.wait_ge(sem["dpay"], 16)
            g.collective_compute(
                "AllGather", ALU.bypass, replica_groups=rg,
                ins=[d_ccin.ap().opt()], outs=[d_ccout.ap().opt()],
            ).then_inc(sem["dcol"])
            g.wait_ge(sem["dcol"], 1)
            src = d_ccout.ap().rearrange("(r p) c -> p r c", r=NCORES, p=128)
            dst = s_gath[:].rearrange("p (r c) -> p r c", r=NCORES)
            g.dma_start(dst, src).then_inc(sem["dgat"], 16)
            if stage < 3 and stage not in (4, 5, 7, 8, 9, 10):
                return
            for gi in range(1, NG):
                g.wait_ge(sem["dso"], 16 * 2 * (gi + 1))
                g.wait_ge(sem["vB"], gi)
                mmb_load(gi)
            if dbg:
                g.wait_ge(sem["vdbg"], 1)
                g.dma_start(d_dbg1[:], s_x1[:]).then_inc(sem["ddbg"], 16)
                g.wait_ge(sem["vdbg"], 3)
                g.dma_start(d_dbg3[:], s_r[:]).then_inc(sem["ddbg"], 16)
                g.dma_start(d_dbg4[:], s_kact[:]).then_inc(sem["ddbg"], 16)
                g.wait_ge(sem["vdbg"], 4)
                g.dma_start(d_dbg2[:], s_u2[:]).then_inc(sem["ddbg"], 16)

        @block.sync
        def _(sy):
            nsm = 0
            for dst, src in [(s_bin, d_bin), (s_bdt, d_bdt), (s_bout, d_bout),
                             (s_bxp, d_bxp), (s_alog, d_alog), (s_dp1, d_dvec),
                             (s_mur, d_mur), (s_muk, d_muk), (s_xbT, d_xbT),
                             (s_hmask, d_hmask), (s_rmask, d_rmask)]:
                sy.dma_start(dst[:], src[:]).then_inc(sem["dsm"], 16)
                nsm += 1
            for gi in range(NG):
                for h in range(2):
                    sy.wait_ge(sem["sso"], gi * CC + (h + 1) * 4)
                    buf = s_sp0 if h == 0 else s_sp1
                    src = buf[:].rearrange("p (c t) -> p c t", c=4)
                    dst = d_spill.ap().rearrange("(c p) t -> p c t", c=CC, p=128)
                    dst = dst[:, h * 4:(h + 1) * 4, gi * N:(gi + 1) * N]
                    sy.dma_start(dst, src).then_inc(sem["dso"], 16)

            def xb_load(gi):
                buf = s_xb0 if gi % 2 == 0 else s_xb1
                src = d_xT.ap().rearrange("(c p) t -> p c t", c=CC, p=128)
                src = src[:, :, gi * N:(gi + 1) * N]
                dst = buf[:].rearrange("p (c t) -> p c t", c=CC)
                sy.dma_start(dst, src).then_inc(sem["dxb"], 16)

            if stage >= 3:
                sy.wait_ge(sem["tpa"], JC * NG)
                xb_load(0)
                if NG > 1:
                    sy.wait_ge(sem["tpd"], CC * NG)
                    xb_load(1)
            if stage >= 2:
                sy.wait_ge(sem["vpay"], 1)
                sy.dma_start(d_ccin[:], s_pay[:]).then_inc(sem["dpay"], 16)
            if stage in (4, 5, 7, 8, 9, 10):
                sy.wait_ge(sem["vod"], NG)
                sy.dma_start(d_out[0:128, 0:128], s_junk[:]).then_inc(sem["dod"], 16)
                sy.wait_ge(sem["dod"], 16)
            elif stage >= 3:
                nod = 0
                for gi in range(NG):
                    if gi + 2 < NG:
                        sy.wait_ge(sem["vB"], gi + 1)
                        xb_load(gi + 2)
                    for cc in range(CC):
                        sy.wait_ge(sem["vod"], gi * CC + cc + 1)
                        buf = (s_ost0, s_ost1, s_ost2, s_ost3)[cc % 4]
                        dst = d_out[cc * 128:(cc + 1) * 128, gi * N:(gi + 1) * N]
                        sy.dma_start(dst, buf[:]).then_inc(sem["dod"], 16)
                        nod += 1
                sy.wait_ge(sem["dod"], 16 * nod)
            else:
                if stage == 1:
                    sy.wait_ge(sem["vss"], NG)
                else:
                    sy.wait_ge(sem["vbnd"], 3)
                sy.wait_ge(sem["sso"], CC * NG)
                sy.dma_start(d_out[0:128, 0:128], s_junk[:]).then_inc(sem["dod"], 16)
                sy.wait_ge(sem["dod"], 16)
            sy.wait_ge(sem["dso"], 16 * 2 * NG)

        @block.tensor
        def _(t):
            def win_ap(cc, jc):
                o = (jc * CC + cc) * 128
                return s_win[:, o:o + 128]

            def wdt_ap(jc, dc):
                o = (jc * CC + dc) * 128
                return s_wdt[:, o:o + 128]

            def wout_ap(jc, dc):
                o = (jc * CC + dc) * 128
                return s_wout[:, o:o + 128]

            def wffn_ap(w, cc, dc):
                o = (cc * CC + dc) * 128
                return w[:, o:o + 128]

            t.wait_ge(sem["dsm"], 16 * 11)
            t.wait_ge(sem["vin"], 1)
            for gi in range(NG):
                xa = s_xa0 if gi % 2 == 0 else s_xa1
                t.wait_ge(sem["dxa"], 16 * (gi + 1))
                def win_jc(jc):
                    i = gi * JC + jc
                    if gi == 0:
                        t.wait_ge(sem["dw"], 16 * (jc // 8 + 1))
                    if i >= 2:
                        t.wait_ge(sem["sea"], i - 1)
                    for cc in range(CC):
                        mm = t.matmul(pb[jc % 2][:], win_ap(cc, jc),
                                      xa[:, cc * N:(cc + 1) * N],
                                      start=(cc == 0), stop=(cc == CC - 1))
                    mm.then_inc(sem["tpa"], 1)

                for jc in range(JI):
                    win_jc(jc)

                def mean_mm(dc):
                    t.wait_ge(sem["sdt"], gi * CC + dc + 1)
                    if dc == 0:
                        t.wait_ge(sem["vss"], gi)
                    dl = s_del0 if dc % 2 == 0 else s_del1
                    mm = t.matmul(pb[5][0:1, :], s_onescb[:], dl[:],
                                  start=(dc == 0), stop=(dc == CC - 1))
                    mm.then_inc(sem["tpm"], 1)

                if gi == 0:
                    t.wait_ge(sem["dw"], 16 * 6)
                t.wait_ge(sem["sea"], gi * JC + JI)
                for dc in range(CC):
                    j = gi * CC + dc
                    if j >= 2:
                        t.wait_ge(sem["sdt"], j - 1)
                    for jc in range(JI):
                        mm = t.matmul(pb[2 + dc % 2][:], wdt_ap(jc, dc),
                                      s_xin[:, jc * N:(jc + 1) * N],
                                      start=(jc == 0), stop=(jc == JI - 1))
                    mm.then_inc(sem["tpd"], 1)
                    if dc >= 2:
                        mean_mm(dc - 2)
                mean_mm(CC - 2)
                mean_mm(CC - 1)
                if gi == 0:
                    t.wait_ge(sem["dw"], 16 * 7)
                t.wait_ge(sem["vbx"], gi)
                for jc in range(JI):
                    mm = t.matmul(pb[4][0:DS, :], s_wxp[:, jc * DS:(jc + 1) * DS],
                                  s_xin[:, jc * N:(jc + 1) * N],
                                  start=(jc == 0), stop=(jc == JI - 1))
                mm.then_inc(sem["tpx"], 1)
                for jc in range(JI, JC):
                    win_jc(jc)
                t.wait_ge(sem["vd"], gi + 1)
                t.wait_ge(sem["ssa"], gi + 1)
                t.matmul(pb[5][0:DS, :], s_ones1_64[:], s_dsb[:],
                         start=True, stop=True).then_inc(sem["tdb"], 1)
                if gi == 0:
                    t.wait_ge(sem["dw"], 16 * 9)
                t.wait_ge(sem["vz"], (gi + 1) * JI)
                for dc in range(CC):
                    j = gi * CC + dc
                    if j >= 2:
                        t.wait_ge(sem["sso"], j - 1)
                    for jc in range(JI):
                        mm = t.matmul(pb[6 + dc % 2][:], wout_ap(jc, dc),
                                      s_xin[:, jc * N:(jc + 1) * N],
                                      start=(jc == 0), stop=(jc == JI - 1))
                    mm.then_inc(sem["tpo"], 1)
                t.wait_ge(sem["vsc"], gi + 1)
                t.wait_ge(sem["vss"], gi)
                sbf = s_s0 if gi % 2 == 0 else s_s1
                t.matmul(pb[5][0:1, :], s_ones64c[:], sbf[:],
                         start=True, stop=True).then_inc(sem["tss"], 1)

            if stage < 2:
                return
            t.wait_ge(sem["vss"], NG)
            t.matmul(pb[5][:, 0:1], s_ones1b[:], s_ssum[0:1, T - 1:T],
                     start=True, stop=True).then_inc(sem["tss"], 1)
            t.wait_ge(sem["vpay"], 2)
            for gi in range(NG):
                if gi >= 2:
                    t.wait_ge(sem["vcor"], gi - 1)
                mm = t.matmul(pb[6 + gi % 2][0:1, :], s_hinb[:],
                              s_G[:, gi * N:(gi + 1) * N], start=True, stop=True)
                mm.then_inc(sem["tcr"], 1)
            t.wait_ge(sem["vbnd"], 1)
            t.matmul(pb[5][0:1, 0:2 * CC], s_onescf[:], s_junk[:, 0:2 * CC],
                     start=True, stop=True).then_inc(sem["tB"], 1)
            t.wait_ge(sem["ssa"], NG + 2)
            t.matmul(pb[5][:, 2:4], s_ones1[:], s_stt2[:],
                     start=True, stop=True).then_inc(sem["tB"], 1)

            if stage < 3:
                return

            def t_ps(g):
                if g == 0:
                    t.wait_ge(sem["vcor"], NG)
                    t.wait_ge(sem["dw"], 16 * 12)
                t.wait_ge(sem["vB"], g)
                t.matmul(pb[0][:], s_ones1b[:], s_ssum[0:1, g * N:(g + 1) * N],
                         start=True, stop=True).then_inc(sem["tB"], 1)

            def t_ln_stats(g, mean_src, sq_src, mb, qb, vub_n, vst_n, sbc_n,
                           usem="vub"):
                t.wait_ge(sem[usem], vub_n)
                if vst_n:
                    t.wait_ge(sem["vst"], vst_n)
                if sbc_n:
                    t.wait_ge(sem["sbc"], sbc_n)
                for cc in range(CC):
                    sl = slice(cc * N, (cc + 1) * N)
                    t.matmul(mb[0:1, :], s_onescb[:], mean_src[:, sl],
                             start=(cc == 0), stop=(cc == CC - 1))
                for cc in range(CC):
                    sl = slice(cc * N, (cc + 1) * N)
                    mm = t.matmul(qb[0:1, :], s_onescb[:], sq_src[:, sl],
                                  start=(cc == 0), stop=(cc == CC - 1))
                mm.then_inc(sem["tB"], 1)

            def t_ln_bcast(mb, qb, base, vst_n, srs_n):
                t.wait_ge(sem["vst"], vst_n)
                t.matmul(mb[:], s_ones1[:], s_stat[0:1, base:base + N],
                         start=True, stop=True).then_inc(sem["tB"], 1)
                t.wait_ge(sem["srs"], srs_n)
                t.matmul(qb[:], s_ones1[:], s_stat[0:1, base + N:base + 2 * N],
                         start=True, stop=True).then_inc(sem["tB"], 1)

            def t_ln1(g):
                t_ln_stats(g, s_x1, s_usq1, pb[1], pb[2], VUB1[g],
                           VST1[g - 1] if g >= 1 else 0,
                           SBC1[g - 1] if g >= 1 else 0)
                t_ln_bcast(pb[1], pb[2], 0, VST1[g], SRS1[g])

            def t_ln2(g):
                t_ln_stats(g, s_u2, s_usq1, pb[3], pb[4], g + 1,
                           VST2[g - 1] if g >= 1 else 0,
                           SBC2[g - 1] if g >= 1 else 0, usem="vu2")
                t_ln_bcast(pb[3], pb[4], 3 * N, VST2[g], SRS2[g])

            t_ps(0)
            t_ln1(0)
            for gi in range(NG):
                if gi < NG - 1:
                    t_ps(gi + 1)
                t.wait_ge(sem["vmxr"], gi + 1)
                fi = 0
                for dc in range(CC):
                    if fi < 2:
                        if 8 * gi + fi - 1 > 0:
                            t.wait_ge(sem["vgt"], 8 * gi + fi - 1)
                    else:
                        t.wait_ge(sem["pfc"], 16 * gi + fi - 1)
                    for cc in range(CC):
                        mm = t.matmul(pb[5 + fi % 2][:], wffn_ap(s_wr, cc, dc),
                                      s_xrf[:, cc * N:(cc + 1) * N],
                                      start=(cc == 0), stop=(cc == CC - 1))
                    mm.then_inc(sem["tB"], 1)
                    fi += 1
                if gi >= 1:
                    t_ln2(gi - 1)
                t.wait_ge(sem["vmxk"], gi + 1)
                for dc in range(CC):
                    t.wait_ge(sem["pfc"], 16 * gi + fi - 1)
                    for cc in range(CC):
                        mm = t.matmul(pb[5 + fi % 2][:], wffn_ap(s_wk, cc, dc),
                                      s_xkf[:, cc * N:(cc + 1) * N],
                                      start=(cc == 0), stop=(cc == CC - 1))
                    mm.then_inc(sem["tB"], 1)
                    fi += 1
                if gi < NG - 1:
                    t_ln1(gi + 1)
                for cc in range(CC):
                    if cc == 0:
                        t.wait_ge(sem["pfc"], 16 * gi + 16)
                    elif cc == 1:
                        pass
                    else:
                        t.wait_ge(sem["vgt"], 8 * gi + fi - 17)
                    for dc in range(CC):
                        mm = t.matmul(pb[5 + fi % 2][:], wffn_ap(s_wv, dc, cc),
                                      s_kact[:, dc * N:(dc + 1) * N],
                                      start=(dc == 0), stop=(dc == CC - 1))
                    mm.then_inc(sem["tB"], 1)
                    fi += 1
            t_ln2(NG - 1)

        @block.scalar
        def _(s):
            s.wait_ge(sem["dsm"], 16 * 11)
            s.activation(s_A[:], s_alog[:], AF.Exp)
            s.memzero(s_junks[:])
            s.memzero(s_junks[:]).then_inc(sem["ssa"], 1)
            for gi in range(NG):
                for jc in range(JI):
                    s.wait_ge(sem["tpa"], gi * JC + jc + 1)
                    if gi > 0 and jc == 0:
                        s.wait_ge(sem["tpo"], gi * CC)
                    s.activation(s_xin[:, jc * N:(jc + 1) * N], pb[jc % 2][:],
                                 AF.Silu, bias=s_bin[:, jc:jc + 1]
                                 ).then_inc(sem["sea"], 1)
                for dc in range(CC):
                    j = gi * CC + dc
                    s.wait_ge(sem["tpd"], j + 1)
                    if j >= 2:
                        s.wait_ge(sem["tpm"], j - 1)
                    dl = s_del0 if dc % 2 == 0 else s_del1
                    s.activation(dl[:], pb[2 + dc % 2][:], AF.Exp,
                                 bias=s_bdt[:, dc:dc + 1])
                    s.activation(dl[:], dl[:], AF.Ln,
                                 bias=s_c1[:]).then_inc(sem["sdt"], 1)
                for jc in range(JI, JC):
                    s.wait_ge(sem["tpa"], gi * JC + jc + 1)
                    if gi * JI + (jc - JI) >= 2:
                        s.wait_ge(sem["vz"], gi * JI + (jc - JI) - 1)
                    dst = (s_sres0 if jc % 2 == 0 else s_sres1)[:]
                    s.activation(dst, pb[jc % 2][:], AF.Silu,
                                 bias=s_bin[:, jc:jc + 1]).then_inc(sem["sea"], 1)
                s.wait_ge(sem["tdb"], gi + 1)
                s.wait_ge(sem["vsc"], gi)
                s.activation(s_a[:], pb[5][0:DS, :], AF.Exp,
                             scale=s_A[:]).then_inc(sem["ssa"], 1)
                for dc in range(CC):
                    s.wait_ge(sem["tpo"], gi * CC + dc + 1)
                    if gi > 0:
                        h = 0 if dc < 4 else 1
                        s.wait_ge(sem["dso"], 16 * (2 * (gi - 1) + h + 1))
                    buf = s_sp0 if dc < 4 else s_sp1
                    s.activation(buf[:, (dc % 4) * N:(dc % 4 + 1) * N],
                                 pb[6 + dc % 2][:], AF.Copy).then_inc(sem["sso"], 1)
            if stage < 2:
                return
            s.wait_ge(sem["vbnd"], 2)
            s.activation(s_junk[0:1, 24:25], s_stt2[0:1, 1:2], AF.Ln,
                         bias=s_ceps[0:1, 0:1])
            s.memzero(s_junks[:])
            s.memzero(s_junks[:])
            s.activation(s_stt2[0:1, 1:2], s_junk[0:1, 24:25], AF.Exp,
                         scale=s_cm05[0:1, 0:1])
            s.memzero(s_junks[:])
            s.memzero(s_junks[:]).then_inc(sem["ssa"], 1)
            if stage < 3:
                return

            def s_rstd(vst_n, base):
                s.wait_ge(sem["vst"], vst_n)
                s.activation(s_stat[0:1, base + 2 * N:base + 3 * N],
                             s_stat[0:1, base + 2 * N:base + 3 * N],
                             AF.Ln, bias=s_ceps[0:1, 0:1])
                s.activation(s_stat[0:1, base + N:base + 2 * N],
                             s_stat[0:1, base + 2 * N:base + 3 * N],
                             AF.Exp, scale=s_cm05[0:1, 0:1])
                s.memzero(s_junks[:])
                s.memzero(s_junks[:]).then_inc(sem["srs"], 1)

            def s_evac(tb_m, tb_r, mb, qb, dst_m, dst_r):
                s.wait_ge(sem["tB"], tb_m)
                s.activation(dst_m[:], mb[:], AF.Copy)
                s.wait_ge(sem["tB"], tb_r)
                s.activation(dst_r[:], qb[:], AF.Copy)
                s.memzero(s_junks[:])
                s.memzero(s_junks[:]).then_inc(sem["sbc"], 2)

            s_rstd(VST1[0], 0)
            s_evac(TB_LN1BM[0], TB_LN1BR[0], pb[1], pb[2], s_sres0, s_sres1)
            for gi in range(NG):
                for dc in range(CC):
                    s.wait_ge(sem["tB"], TB_WR[gi][dc])
                    s.activation(s_r[:, dc * N:(dc + 1) * N], pb[5 + dc % 2][:],
                                 AF.Sigmoid).then_inc(sem["pfc"], 1)
                if gi >= 1:
                    s_rstd(VST2[gi - 1], 3 * N)
                    s_evac(TB_LN2BM[gi - 1], TB_LN2BR[gi - 1], pb[3], pb[4],
                           s_gt0, s_gt1)
                for dc in range(CC):
                    s.wait_ge(sem["tB"], TB_WK[gi][dc])
                    ka = s_kact[:, dc * N:(dc + 1) * N]
                    s.activation(ka, pb[5 + dc % 2][:], AF.Relu)
                    s.activation(ka, ka, AF.Square).then_inc(sem["pfc"], 1)
                if gi < NG - 1:
                    s_rstd(VST1[gi + 1], 0)
                    s_evac(TB_LN1BM[gi + 1], TB_LN1BR[gi + 1], pb[1], pb[2],
                           s_sres0, s_sres1)
                s.wait_ge(sem["vgt"], 8 * gi + 8)
                for dc in range(CC):
                    sl = slice(dc * N, (dc + 1) * N)
                    s.activation(s_usq1[:, sl], s_u2[:, sl], AF.Square)
                s.memzero(s_junks[:])
                s.memzero(s_junks[:]).then_inc(sem["vu2"], 1)
            s_rstd(VST2[NG - 1], 3 * N)
            s_evac(TB_LN2BM[NG - 1], TB_LN2BR[NG - 1], pb[3], pb[4],
                   s_gt0, s_gt1)

        @block.vector
        def _(v):
            def pad():
                v.memset(s_junk[:, 64:96], 0.0)

            v.wait_ge(sem["dsm"], 16 * 11)
            v.memset(s_ones1[:], 1.0)
            v.memset(s_ones1b[:], 1.0)
            v.memset(s_ones1_64[:], 1.0)
            v.memset(s_ones64c[:], 1.0)
            v.memset(s_onescb[:], 1.0)
            v.memset(s_onescf[:], 1.0)
            v.memset(s_c1[:], 1.0)
            v.memset(s_ceps[:], 1e-5)
            v.memset(s_cm05[:], -0.5)
            v.memset(s_pay[:], 0.0)
            v.memset(s_junk[:], 0.0)
            v.wait_ge(sem["ssa"], 1)
            v.tensor_scalar(s_A[:], s_A[:], -1.0, None, ALU.mult)
            v.tensor_scalar(s_dp1[:], s_dp1[:], 1.0, None, ALU.add)
            pad()
            v.engine_nop().then_inc(sem["vin"], 1)

            for gi in range(NG):
                v.wait_ge(sem["tpx"], gi + 1)
                v.wait_ge(sem["tpd"], CC * (gi + 1))
                for jc in range(JI):
                    v.wait_ge(sem["sea"], gi * JC + JI + jc + 1)
                    sr = s_sres0 if jc % 2 == 0 else s_sres1
                    v.tensor_mul(s_xin[:, jc * N:(jc + 1) * N],
                                 s_xin[:, jc * N:(jc + 1) * N], sr[:]
                                 ).then_inc(sem["vz"], 1)
                v.wait_ge(sem["tpm"], CC * (gi + 1))
                v.tensor_scalar(s_dsb[:], pb[5][0:1, :], 1.0 / C, None,
                                ALU.mult)
                pad(); pad()
                v.engine_nop().then_inc(sem["vd"], 1)
                v.wait_ge(sem["tpx"], gi + 1)
                v.tensor_scalar(s_bx[:], pb[4][0:DS, :], s_bxp[:], None,
                                ALU.add).then_inc(sem["vbx"], 1)
                v.wait_ge(sem["ssa"], gi + 2)
                sbf = s_s0 if gi % 2 == 0 else s_s1
                pbf = s_s1 if gi % 2 == 0 else s_s0
                if gi >= 2:
                    v.wait_ge(sem["tss"], gi - 1)
                init_s = 0.0 if gi == 0 else pbf[:, N - 1:N]
                init_g = 1.0 if gi == 0 else s_G[:, gi * N - 1:gi * N]
                v.tensor_tensor_scan(sbf[:], s_a[:], s_bx[:], init_s,
                                     ALU.mult, ALU.add)
                v.tensor_tensor_scan(s_G[:, gi * N:(gi + 1) * N], s_a[:],
                                     s_bx[:], init_g, ALU.mult, ALU.bypass)
                v.engine_nop().then_inc(sem["vsc"], 1)
                v.wait_ge(sem["tss"], gi + 1)
                v.tensor_copy(s_ssum[0:1, gi * N:(gi + 1) * N], pb[5][0:1, :])
                pad()
                v.engine_nop().then_inc(sem["vss"], 1)

            if stage < 2:
                return
            sbf7 = s_s0 if (NG - 1) % 2 == 0 else s_s1
            v.wait_ge(sem["sso"], CC * NG)
            v.tensor_copy(s_pay[0:DS, 0:1], sbf7[:, N - 1:N])
            v.wait_ge(sem["tss"], NG + 1)
            v.tensor_copy(s_slastc[:], pb[5][:, 0:1])
            pad(); pad()
            for cc in range(CC):
                sp = s_sp0 if cc < 4 else s_sp1
                col = (cc % 4) * N + (N - 1)
                v.tensor_scalar(s_pay[:, 1 + cc:2 + cc], sp[:, col:col + 1],
                                s_slastc[:], None, ALU.mult)
            pad(); pad(); pad()
            v.engine_nop().then_inc(sem["vpay"], 1)
            v.wait_ge(sem["dgat"], 16)
            v.tensor_mul(s_gath[:], s_gath[:], s_rmask[:])
            pad(); pad()
            half = NCORES * 9 // 2
            v.tensor_add(s_gath[:, 0:half], s_gath[:, 0:half], s_gath[:, half:])
            pad(); pad()
            v.tensor_add(s_gath[:, 0:18], s_gath[:, 0:18], s_gath[:, 18:36])
            pad(); pad()
            v.tensor_add(s_sel[:, 0:9], s_gath[:, 0:9], s_gath[:, 9:18])
            pad(); pad()
            v.tensor_copy(s_hinb[:], s_sel[0:DS, 0:1])
            pad(); pad()
            v.engine_nop().then_inc(sem["vpay"], 1)
            for gi in range(NG):
                v.wait_ge(sem["tcr"], gi + 1)
                sl = slice(gi * N, (gi + 1) * N)
                if gi < NG - 1:
                    v.tensor_add(s_ssum[0:1, sl], s_ssum[0:1, sl],
                                 pb[6 + gi % 2][0:1, :]).then_inc(sem["vcor"], 1)
                else:
                    v.tensor_add(s_ssum[0:1, sl], s_ssum[0:1, sl],
                                 pb[6 + gi % 2][0:1, :])
                    pad(); pad()
                    v.engine_nop().then_inc(sem["vcor"], 1)
            v.tensor_mul(s_ub[:], s_xbT[:], s_dp1[:])
            pad(); pad()
            v.tensor_add(s_ub[:], s_ub[:], s_bout[:])
            pad(); pad()
            v.tensor_add(s_ub[:], s_ub[:], s_sel[:, 1:9])
            pad(); pad()
            v.tensor_copy(s_junk[:, 0:CC], s_ub[:])
            v.tensor_mul(s_junk[:, CC:2 * CC], s_ub[:], s_ub[:])
            pad(); pad()
            v.engine_nop().then_inc(sem["vbnd"], 1)
            v.wait_ge(sem["tB"], 1)
            v.reduce_sum(s_stt2[0:1, 0:1], pb[5][0:1, 0:CC], axis=AX)
            v.reduce_sum(s_stt2[0:1, 1:2], pb[5][0:1, CC:2 * CC], axis=AX)
            pad(); pad()
            v.tensor_scalar(s_stt2[0:1, 0:2], s_stt2[0:1, 0:2], 1.0 / C, None,
                            ALU.mult)
            pad(); pad()
            v.tensor_mul(s_junk[0:1, 16:17], s_stt2[0:1, 0:1], s_stt2[0:1, 0:1])
            pad(); pad()
            v.tensor_sub(s_stt2[0:1, 1:2], s_stt2[0:1, 1:2], s_junk[0:1, 16:17])
            pad(); pad()
            v.engine_nop().then_inc(sem["vbnd"], 1)
            v.wait_ge(sem["ssa"], NG + 2)
            v.wait_ge(sem["tB"], 2)
            v.tensor_copy(s_junk[:, 32:34], pb[5][:, 2:4])
            pad(); pad()
            v.tensor_scalar(s_ub[:], s_ub[:], s_junk[:, 32:33],
                            s_junk[:, 33:34], ALU.subtract, ALU.mult)
            pad(); pad()
            v.tensor_scalar(s_prevc[:], s_ub[:], s_hmask[:], None, ALU.mult)
            pad(); pad()
            v.engine_nop().then_inc(sem["vbnd"], 1)

            if stage < 3:
                return

            def v_u_steps(g):
                xb = s_xb0 if g % 2 == 0 else s_xb1
                v.wait_ge(sem["dxb"], 16 * (g + 1))
                v.wait_ge(sem["dmb"], 16 * (g + 1))
                v.wait_ge(sem["tB"], TB_PS[g])
                for cc in range(CC):
                    sl = slice(cc * N, (cc + 1) * N)
                    v.tensor_mul(s_mmb[:, sl], s_mmb[:, sl], pb[0][:])
                for cc in range(CC):
                    sl = slice(cc * N, (cc + 1) * N)
                    v.scalar_tensor_tensor(s_x1[:, sl], xb[:, sl],
                                           s_dp1[:, cc:cc + 1], s_mmb[:, sl],
                                           ALU.mult, ALU.add)
                for cc in range(CC):
                    sl = slice(cc * N, (cc + 1) * N)
                    v.tensor_scalar(s_x1[:, sl], s_x1[:, sl],
                                    s_bout[:, cc:cc + 1], None, ALU.add)
                v.engine_nop().then_inc(sem["vB"], 1)

            def v_usq(src, dst):
                for cc in range(CC):
                    sl = slice(cc * N, (cc + 1) * N)
                    v.tensor_mul(dst[:, sl], src[:, sl], src[:, sl])
                v.engine_nop().then_inc(sem["vub"], 1)

            def v_ln_vals(tb_n, base, mbank, qbank):
                v.wait_ge(sem["tB"], tb_n)
                v.tensor_scalar(s_stat[0:1, base:base + N], mbank[0:1, :],
                                1.0 / C, None, ALU.mult)
                v.tensor_mul(s_stat[0:1, base + 2 * N:base + 3 * N],
                             s_stat[0:1, base:base + N],
                             s_stat[0:1, base:base + N])
                v.scalar_tensor_tensor(s_stat[0:1, base + 2 * N:base + 3 * N],
                                       qbank[0:1, :], 1.0 / C,
                                       s_stat[0:1, base + 2 * N:base + 3 * N],
                                       ALU.mult, ALU.subtract)
                pad(); pad()
                v.engine_nop().then_inc(sem["vst"], 2)

            def v_apply(g):
                v.wait_ge(sem["sbc"], SBC1[g])
                for cc in range(CC):
                    sl = slice(cc * N, (cc + 1) * N)
                    v.tensor_sub(s_x1[:, sl], s_x1[:, sl], s_sres0[:])
                for cc in range(CC):
                    sl = slice(cc * N, (cc + 1) * N)
                    v.tensor_mul(s_x1[:, sl], s_x1[:, sl], s_sres1[:])

            def v_mixr(g):
                for cc in range(CC):
                    o = cc * N
                    v.tensor_sub(s_diff[:, o + 1:o + N], s_x1[:, o + 1:o + N],
                                 s_x1[:, o:o + N - 1])
                for cc in range(CC):
                    o = cc * N
                    v.tensor_sub(s_diff[:, o:o + 1], s_x1[:, o:o + 1],
                                 s_prevc[:, cc:cc + 1])
                for cc in range(CC):
                    o = cc * N
                    v.scalar_tensor_tensor(s_xrf[:, o + 1:o + N],
                                           s_diff[:, o + 1:o + N],
                                           s_mur[:, cc:cc + 1],
                                           s_x1[:, o:o + N - 1],
                                           ALU.mult, ALU.add)
                for cc in range(CC):
                    o = cc * N
                    v.scalar_tensor_tensor(s_xrf[:, o:o + 1], s_diff[:, o:o + 1],
                                           s_mur[:, cc:cc + 1],
                                           s_prevc[:, cc:cc + 1],
                                           ALU.mult, ALU.add)
                pad()
                v.engine_nop().then_inc(sem["vmxr"], 1)

            def v_mixk(g):
                for cc in range(CC):
                    o = cc * N
                    v.scalar_tensor_tensor(s_xkf[:, o:o + 1], s_diff[:, o:o + 1],
                                           s_muk[:, cc:cc + 1],
                                           s_prevc[:, cc:cc + 1],
                                           ALU.mult, ALU.add)
                for cc in range(CC):
                    o = cc * N
                    v.scalar_tensor_tensor(s_xkf[:, o + 1:o + N],
                                           s_diff[:, o + 1:o + N],
                                           s_muk[:, cc:cc + 1],
                                           s_x1[:, o:o + N - 1],
                                           ALU.mult, ALU.add)
                for cc in range(CC):
                    o = cc * N
                    v.tensor_copy(s_prevc[:, cc:cc + 1], s_x1[:, o + N - 1:o + N])
                pad()
                v.engine_nop().then_inc(sem["vmxk"], 1)

            v.wait_ge(sem["dw"], 16 * 12)
            v_u_steps(0)
            v_usq(s_x1, s_usq1)
            v_ln_vals(TB_LN1S[0], 0, pb[1], pb[2])
            v_apply(0)
            if dbg:
                pad()
                v.engine_nop().then_inc(sem["vdbg"], 1)
                v.wait_ge(sem["ddbg"], 16)
            v.wait_ge(sem["dso"], 16 * 2 * NG)
            v_mixr(0)
            v_mixk(0)
            for gi in range(NG):
                for cc in range(CC):
                    sl = slice(cc * N, (cc + 1) * N)
                    v.tensor_copy(s_u2[:, sl], s_x1[:, sl])
                if gi < NG - 1:
                    v_u_steps(gi + 1)
                    v_usq(s_x1, s_usq1)
                    v_ln_vals(TB_LN1S[gi + 1], 0, pb[1], pb[2])
                    v_apply(gi + 1)
                    v_mixr(gi + 1)
                    v.wait_ge(sem["tB"], TB_WK[gi][CC - 1])
                    v_mixk(gi + 1)
                if dbg and gi == 0:
                    v.wait_ge(sem["pfc"], 16)
                    pad()
                    v.engine_nop().then_inc(sem["vdbg"], 2)
                    v.wait_ge(sem["ddbg"], 48)
                v.wait_ge(sem["sbc"], SBC2[gi - 1] if gi >= 1 else 0)
                for cc in range(CC):
                    sl = slice(cc * N, (cc + 1) * N)
                    gt = s_gt0 if cc % 2 == 0 else s_gt1
                    v.wait_ge(sem["tB"], TB_WV[gi][cc])
                    v.tensor_mul(gt[:], s_r[:, sl], pb[5 + (16 + cc) % 2][:])
                    v.tensor_add(s_u2[:, sl], s_u2[:, sl], gt[:]
                                 ).then_inc(sem["vgt"], 1)
                if dbg and gi == 0:
                    pad()
                    v.engine_nop().then_inc(sem["vdbg"], 1)
                    v.wait_ge(sem["ddbg"], 64)
                v_ln_vals(TB_LN2S[gi], 3 * N, pb[3], pb[4])
                v.wait_ge(sem["sbc"], SBC2[gi])
                for cc in range(CC):
                    k = gi * CC + cc
                    if k >= 4:
                        v.wait_ge(sem["dod"], 16 * (k - 3))
                    sl = slice(cc * N, (cc + 1) * N)
                    ost = (s_ost0, s_ost1, s_ost2, s_ost3)[cc % 4]
                    v.tensor_sub(ost[:], s_u2[:, sl], s_gt0[:])
                    v.tensor_mul(ost[:], ost[:], s_gt1[:]).then_inc(sem["vod"], 1)

    nc.finalize()
    return nc


_NC_CACHE = {}


def _get_nc():
    if "nc" not in _NC_CACHE:
        _NC_CACHE["nc"] = build_nc()
    return _NC_CACHE["nc"]


def _chunk_weight(w, kchunks):
    K, M = w.shape
    assert K == kchunks * 128
    return np.ascontiguousarray(
        w.reshape(kchunks, 128, M).transpose(1, 0, 2).reshape(128, kchunks * M))


def _col_tile(vec, ncol):
    return np.ascontiguousarray(vec.reshape(ncol, 128).T)


def stage_inputs(L0_attn_x, L0_attn_W_in, L0_attn_b_in, L0_attn_W_xp,
                 L0_attn_b_xp, L0_attn_W_dt, L0_attn_b_dt, L0_attn_W_out,
                 L0_attn_b_out, L0_attn_A_log, L0_attn_D, L0_ffn_W_r,
                 L0_ffn_W_k, L0_ffn_W_v, L0_ffn_mu_r, L0_ffn_mu_k, T=T_FULL):
    f = np.float32
    x = np.asarray(L0_attn_x, f)
    w_in = np.asarray(L0_attn_W_in, f)
    win = np.ascontiguousarray(
        w_in.reshape(CC, 128, JC, 128).transpose(1, 2, 0, 3).reshape(128, -1))
    wdt = _chunk_weight(np.asarray(L0_attn_W_dt, f), JI)
    wxp = _chunk_weight(np.asarray(L0_attn_W_xp, f), JI)
    wout = _chunk_weight(np.asarray(L0_attn_W_out, f), JI)
    wr = _chunk_weight(np.ascontiguousarray(np.asarray(L0_ffn_W_r, f).T), CC)
    wk = _chunk_weight(np.ascontiguousarray(np.asarray(L0_ffn_W_k, f).T), CC)
    wv = _chunk_weight(np.ascontiguousarray(np.asarray(L0_ffn_W_v, f).T), CC)
    bin_t = _col_tile(np.asarray(L0_attn_b_in, f), JC)
    bdt_t = _col_tile(np.asarray(L0_attn_b_dt, f), CC)
    bout_t = _col_tile(np.asarray(L0_attn_b_out, f), CC)
    bxp_t = np.ascontiguousarray(np.asarray(L0_attn_b_xp, f).reshape(DS, 1))
    alog_t = np.ascontiguousarray(np.asarray(L0_attn_A_log, f).reshape(DS, 1))
    dvec_t = _col_tile(np.asarray(L0_attn_D, f), CC)
    mur_t = _col_tile(np.asarray(L0_ffn_mu_r, f), CC)
    muk_t = _col_tile(np.asarray(L0_ffn_mu_k, f), CC)

    shared = dict(win=win, wdt=wdt, wxp=wxp, wout=wout, wr=wr, wk=wk, wv=wv,
                  bin=bin_t, bdt=bdt_t, bout=bout_t, bxp=bxp_t, alog=alog_t,
                  dvec=dvec_t, mur=mur_t, muk=muk_t)

    in_maps = []
    for c in range(NCORES):
        b, h = c // 2, c % 2
        xT = np.ascontiguousarray(x[b, h * T:(h + 1) * T, :].T)
        if h == 1:
            xbT = _col_tile(x[b, T - 1, :], CC)
            hmask = np.ones((128, 1), f)
            rmask = np.zeros((128, NCORES * 9), f)
            rmask[:, (c - 1) * 9:c * 9] = 1.0
        else:
            xbT = np.zeros((128, CC), f)
            hmask = np.zeros((128, 1), f)
            rmask = np.zeros((128, NCORES * 9), f)
        in_maps.append(dict(shared, xT=xT, xbT=xbT, hmask=hmask, rmask=rmask))
    return in_maps


def kernel(**inputs):
    in_maps = stage_inputs(**inputs)
    nc = _get_nc()
    res = run_bass_kernel_spmd(nc, in_maps, core_ids=list(range(NCORES)))
    f = np.float32
    out = np.empty((B_, S_, C), f)
    for c in range(NCORES):
        b, h = c // 2, c % 2
        out[b, h * T_FULL:(h + 1) * T_FULL, :] = np.asarray(
            res.results[c]["outT"], f).T
    return out


# version 13 (final)
# speedup vs baseline: 1.1306x; 1.0916x over previous
import numpy as np
import ml_dtypes
import concourse.bass as bass
import concourse.mybir as mybir
from concourse.bass_utils import run_bass_kernel_spmd

F32 = mybir.dt.float32
BF16 = mybir.dt.bfloat16
AF = mybir.ActivationFunctionType
ALU = mybir.AluOpType
AX = mybir.AxisListType.X

NCORES = 8
B_, S_, C = 4, 8192, 1024
T_FULL = S_ // 2
N = 512
CC = C // 128
JC = 32
DI = 2048
JI = DI // 128
DS = 64


def build_nc(T=T_FULL, stage=3, dbg=False):
    NG = T // N
    nc = bass.Bass("TRN2", target_bir_lowering=False,
                   detect_race_conditions=False)

    d_xT = nc.dram_tensor("xT", [C, T], F32, kind="ExternalInput")
    d_win = nc.dram_tensor("win", [128, JC * CC * 128], F32, kind="ExternalInput")
    d_wdt = nc.dram_tensor("wdt", [128, JI * CC * 128], F32, kind="ExternalInput")
    d_wxp = nc.dram_tensor("wxp", [128, JI * DS], F32, kind="ExternalInput")
    d_wout = nc.dram_tensor("wout", [128, JI * CC * 128], F32, kind="ExternalInput")
    d_wr = nc.dram_tensor("wr", [128, CC * CC * 128], F32, kind="ExternalInput")
    d_wk = nc.dram_tensor("wk", [128, CC * CC * 128], F32, kind="ExternalInput")
    d_wv = nc.dram_tensor("wv", [128, CC * CC * 128], F32, kind="ExternalInput")
    d_bin = nc.dram_tensor("bin", [128, JC], F32, kind="ExternalInput")
    d_bdt = nc.dram_tensor("bdt", [128, CC], F32, kind="ExternalInput")
    d_bout = nc.dram_tensor("bout", [128, CC], F32, kind="ExternalInput")
    d_bxp = nc.dram_tensor("bxp", [DS, 1], F32, kind="ExternalInput")
    d_alog = nc.dram_tensor("alog", [DS, 1], F32, kind="ExternalInput")
    d_dvec = nc.dram_tensor("dvec", [128, CC], F32, kind="ExternalInput")
    d_mur = nc.dram_tensor("mur", [128, CC], F32, kind="ExternalInput")
    d_muk = nc.dram_tensor("muk", [128, CC], F32, kind="ExternalInput")
    d_xbT = nc.dram_tensor("xbT", [128, CC], F32, kind="ExternalInput")
    d_hmask = nc.dram_tensor("hmask", [128, 1], F32, kind="ExternalInput")
    d_rmask = nc.dram_tensor("rmask", [128, NCORES * 9], F32, kind="ExternalInput")
    d_out = nc.dram_tensor("outT", [C, T], F32, kind="ExternalOutput")
    if dbg:
        d_dbg1 = nc.dram_tensor("dbg1", [128, CC * N], F32, kind="ExternalOutput")
        d_dbg2 = nc.dram_tensor("dbg2", [128, CC * N], F32, kind="ExternalOutput")
        d_dbg3 = nc.dram_tensor("dbg3", [128, CC * N], F32, kind="ExternalOutput")
        d_dbg4 = nc.dram_tensor("dbg4", [128, CC * N], F32, kind="ExternalOutput")
    d_spill = nc.dram_tensor("spill", [C, T], BF16, kind="Internal")
    d_ccin = nc.dram_tensor("cc_in", [128, 9], F32, kind="Internal")
    d_ccout = nc.dram_tensor("cc_out", [NCORES * 128, 9], F32, kind="Internal")

    BASE = ((nc.sbuf_base + 31) // 32) * 32
    TOP = nc.sbuf_top
    cur = [BASE]

    def sb(name, shape, dtype, at=None):
        isz = 2 if dtype == BF16 else 4
        nb = int(np.prod(shape[1:])) * isz
        nb = ((nb + 31) // 32) * 32
        if at is None:
            off = cur[0]
            cur[0] += nb
        else:
            off = BASE_W0[0] + at
        assert off + nb <= TOP, (name, off, nb, TOP)
        return nc.alloc_sbuf_tensor_at(name, list(shape), dtype, offset=off)

    BASE_W0 = [0]
    s_ssum = sb("ssum", [1, T], BF16)
    s_G = sb("G", [DS, T], BF16)
    s_s0 = sb("s0", [DS, N], F32)
    s_s1 = sb("s1", [DS, N], F32)
    s_a = sb("a", [DS, N], F32)
    s_bx = sb("bx", [DS, N], F32)
    s_sres0 = sb("sres0", [128, N], BF16)
    s_sres1 = sb("sres1", [128, N], BF16)
    s_dsb = sb("dsb", [1, N], F32)
    s_bin = sb("v_bin", [128, JC], F32)
    s_bdt = sb("v_bdt", [128, CC], F32)
    s_bout = sb("v_bout", [128, CC], F32)
    s_bxp = sb("v_bxp", [DS, 1], F32)
    s_alog = sb("v_alog", [DS, 1], F32)
    s_A = sb("v_A", [DS, 1], F32)
    s_dp1 = sb("v_dp1", [128, CC], F32)
    s_mur = sb("v_mur", [128, CC], F32)
    s_muk = sb("v_muk", [128, CC], F32)
    s_xbT = sb("v_xbT", [128, CC], F32)
    s_hmask = sb("v_hmask", [128, 1], F32)
    s_ones1 = sb("ones1", [1, 128], F32)
    s_ones1b = sb("ones1b", [1, 128], BF16)
    s_ones1_64 = sb("ones1_64", [1, DS], F32)
    s_ones64c = sb("ones64c", [DS, 1], F32)
    s_onescb = sb("onescb", [128, 1], BF16)
    s_onescf = sb("onescf", [128, 1], F32)
    s_c1 = sb("c_one", [128, 1], F32)
    s_ceps = sb("c_eps", [1, 1], F32)
    s_cm05 = sb("c_m05", [1, 1], F32)
    s_junk = sb("junk", [128, 128], F32)
    s_junks = sb("junks", [128, 128], F32)
    s_pay = sb("pay", [128, 9], F32)
    s_gath = sb("gath", [128, NCORES * 9], F32)
    s_rmask = sb("rmaskT", [128, NCORES * 9], F32)
    s_sel = sb("sel", [128, 9], F32)
    s_hinb = sb("hinb", [DS, 1], BF16)
    s_ub = sb("ub", [128, CC], F32)
    s_prevc = sb("prevc", [128, CC], F32)
    s_slastc = sb("slastc", [128, 1], F32)
    s_stt2 = sb("stt2", [1, 2], F32)
    BASE_W0[0] = ((cur[0] + 31) // 32) * 32

    s_win = sb("w_in", [128, JC * CC * 128], BF16, at=0)
    s_wdt = sb("w_dt", [128, JI * CC * 128], BF16, at=65536)
    s_wxp = sb("w_xp", [128, JI * DS], BF16, at=98304)
    s_wout = sb("w_out", [128, JI * CC * 128], BF16, at=100352)
    s_xa0 = sb("xa0", [128, CC * N], BF16, at=133120)
    s_xa1 = sb("xa1", [128, CC * N], BF16, at=141312)
    s_xin = sb("xin", [128, JI * N], BF16, at=149504)
    s_del0 = sb("del0", [128, N], BF16, at=165888)
    s_del1 = sb("del1", [128, N], BF16, at=166912)
    s_sp0 = sb("sp0", [128, 4 * N], BF16, at=167936)
    s_sp1 = sb("sp1", [128, 4 * N], BF16, at=172032)
    s_diff = sb("diff", [128, CC * N], BF16, at=167936)
    s_u2bf = sb("u2b", [128, CC * N], BF16,
                at=(BASE + ((T * 2 + 31) // 32) * 32) - BASE_W0[0])
    s_u2b = [s_u2bf[:, k * N:(k + 1) * N] for k in range(CC)]
    A_END = 176128
    s_wr = sb("w_r", [128, CC * CC * 128], BF16, at=0)
    s_wk = sb("w_k", [128, CC * CC * 128], BF16, at=16384)
    s_wv = sb("w_v", [128, CC * CC * 128], BF16, at=32768)
    s_xb0 = sb("xb0", [128, CC * N], F32, at=49152)
    s_xb1 = sb("xb1", [128, CC * N], F32, at=65536)
    s_mmb = sb("mmb", [128, CC * N], F32, at=81920)
    s_x1 = sb("x1", [128, CC * N], BF16, at=98304)
    s_u2 = sb("u2", [128, CC * N], BF16, at=106496)
    s_kact = sb("kact", [128, CC * N], BF16, at=114688)
    s_r = sb("r", [128, CC * N], BF16, at=122880)
    s_usq1 = sb("usq1", [128, CC * N], BF16, at=131072)
    s_xrf = sb("xrf", [128, CC * N], BF16, at=139264)
    s_xkf = sb("xkf", [128, CC * N], BF16, at=147456)
    s_ost0 = sb("ost0", [128, N], F32, at=155648)
    s_ost1 = sb("ost1", [128, N], F32, at=157696)
    s_stat = sb("stat", [1, 6 * N], F32, at=159744)
    s_ost2 = sb("ost2", [128, N], F32, at=172032)
    s_ost3 = sb("ost3", [128, N], F32, at=174080)
    s_gt0 = sb("gt0", [128, N], BF16, at=176128)
    s_gt1 = sb("gt1", [128, N], BF16, at=177184)
    B_END = 178240
    assert BASE_W0[0] + max(A_END, B_END) <= TOP, (BASE_W0[0], TOP)

    pb = [nc.alloc_psum_tensor(f"pb{i}", [128, N], F32) for i in range(8)]

    semnames = [
        "dw", "dxa", "dxb", "dmb", "dso", "dod", "dsm", "dcc",
        "tpa", "tpd", "tpx", "tpm", "tdb", "tss", "tpo", "tgx", "t1",
        "tcr", "tB", "tst", "dpay", "dcol", "dgat",
        "sea", "sdt", "ssa", "sso", "srs", "pfc",
        "vin", "vz", "vd", "vsc", "vss", "vbx", "vpay", "vcor", "vbnd",
        "vB", "vub", "vst", "vmx", "vmxr", "vmxk", "vod", "vdbg", "ddbg", "sbc",
        "vgt", "vu2",
    ]
    sem = {n: nc.alloc_semaphore("sem_" + n) for n in semnames}
    rg = [list(range(NCORES))]

    TB_PS, TB_LN1S, TB_LN1BM, TB_LN1BR = {}, {}, {}, {}
    TB_WR, TB_WK, TB_WV = {}, {}, {}
    TB_LN2S, TB_LN2BM, TB_LN2BR = {}, {}, {}
    cnt = 2
    cnt += 1; TB_PS[0] = cnt
    cnt += 1; TB_LN1S[0] = cnt
    cnt += 1; TB_LN1BM[0] = cnt
    cnt += 1; TB_LN1BR[0] = cnt
    for g in range(NG):
        if g < NG - 1:
            cnt += 1; TB_PS[g + 1] = cnt
        TB_WR[g] = {}
        for dc in range(CC):
            cnt += 1; TB_WR[g][dc] = cnt
        if g >= 1:
            cnt += 1; TB_LN2S[g - 1] = cnt
            cnt += 1; TB_LN2BM[g - 1] = cnt
            cnt += 1; TB_LN2BR[g - 1] = cnt
        TB_WK[g] = {}
        for dc in range(CC):
            cnt += 1; TB_WK[g][dc] = cnt
        if g < NG - 1:
            cnt += 1; TB_LN1S[g + 1] = cnt
            cnt += 1; TB_LN1BM[g + 1] = cnt
            cnt += 1; TB_LN1BR[g + 1] = cnt
        TB_WV[g] = {}
        for cc in range(CC):
            cnt += 1; TB_WV[g][cc] = cnt
    cnt += 1; TB_LN2S[NG - 1] = cnt
    cnt += 1; TB_LN2BM[NG - 1] = cnt
    cnt += 1; TB_LN2BR[NG - 1] = cnt

    VB_N, VUB1, VUB2, VST1, VST2, VMXR, VMXK = {}, {}, {}, {}, {}, {}, {}
    vb = vub = vst = vmxr = vmxk = 0
    vb += 1; VB_N[0] = vb
    vub += 1; VUB1[0] = vub
    vst += 2; VST1[0] = vst
    vmxr += 1; VMXR[0] = vmxr
    vmxk += 1; VMXK[0] = vmxk
    for g in range(NG):
        if g < NG - 1:
            vb += 1; VB_N[g + 1] = vb
            vub += 1; VUB1[g + 1] = vub
            vst += 2; VST1[g + 1] = vst
            vmxr += 1; VMXR[g + 1] = vmxr
            vmxk += 1; VMXK[g + 1] = vmxk
        VUB2[g] = g + 1
        vst += 2; VST2[g] = vst
    SRS1, SRS2, SBC1, SBC2 = {}, {}, {}, {}
    srs = sbc = 0
    srs += 1; SRS1[0] = srs
    sbc += 2; SBC1[0] = sbc
    for g in range(NG):
        if g >= 1:
            srs += 1; SRS2[g - 1] = srs
            sbc += 2; SBC2[g - 1] = sbc
        if g < NG - 1:
            srs += 1; SRS1[g + 1] = srs
            sbc += 2; SBC1[g + 1] = sbc
    srs += 1; SRS2[NG - 1] = srs
    sbc += 2; SBC2[NG - 1] = sbc

    with nc.Block() as block:

        @block.gpsimd
        def _(g):
            def xa_load(gi):
                buf = s_xa0 if gi % 2 == 0 else s_xa1
                src = d_xT.ap().rearrange("(c p) t -> p c t", c=CC, p=128)
                src = src[:, :, gi * N:(gi + 1) * N]
                dst = buf[:].rearrange("p (c t) -> p c t", c=CC)
                g.dma_start(dst, src).then_inc(sem["dxa"], 16)

            def mmb_load(gi):
                src = d_spill.ap().rearrange("(c p) t -> p c t", c=CC, p=128)
                src = src[:, :, gi * N:(gi + 1) * N]
                dst = s_mmb[:].rearrange("p (c t) -> p c t", c=CC)
                g.dma_start(dst, src).then_inc(sem["dmb"], 16)

            xa_load(0)
            WQ = JC * CC * 128
            for q in range(4):
                sl = slice(q * WQ // 4, (q + 1) * WQ // 4)
                g.dma_start(s_win[:, sl], d_win[:, sl]).then_inc(sem["dw"], 16)
            if NG > 1:
                xa_load(1)
            WD = JI * CC * 128
            for q in range(2):
                sl = slice(q * WD // 2, (q + 1) * WD // 2)
                g.dma_start(s_wdt[:, sl], d_wdt[:, sl]).then_inc(sem["dw"], 16)
            g.dma_start(s_wxp[:], d_wxp[:]).then_inc(sem["dw"], 16)
            for q in range(2):
                sl = slice(q * WD // 2, (q + 1) * WD // 2)
                g.dma_start(s_wout[:, sl], d_wout[:, sl]).then_inc(sem["dw"], 16)
            for gi in range(2, NG):
                g.wait_ge(sem["tpa"], JC * (gi - 1))
                xa_load(gi)
            g.wait_ge(sem["tpa"], JC * NG)
            g.dma_start(s_wr[:], d_wr[:]).then_inc(sem["dw"], 16)
            g.dma_start(s_wk[:], d_wk[:]).then_inc(sem["dw"], 16)
            g.dma_start(s_wv[:], d_wv[:]).then_inc(sem["dw"], 16)
            if stage < 2:
                return
            if stage >= 3:
                g.wait_ge(sem["tpd"], CC * NG)
                g.wait_ge(sem["dso"], 16 * 2)
                mmb_load(0)
            g.wait_ge(sem["dpay"], 16)
            g.collective_compute(
                "AllGather", ALU.bypass, replica_groups=rg,
                ins=[d_ccin.ap().opt()], outs=[d_ccout.ap().opt()],
            ).then_inc(sem["dcol"])
            g.wait_ge(sem["dcol"], 1)
            src = d_ccout.ap().rearrange("(r p) c -> p r c", r=NCORES, p=128)
            dst = s_gath[:].rearrange("p (r c) -> p r c", r=NCORES)
            g.dma_start(dst, src).then_inc(sem["dgat"], 16)
            if stage < 3 and stage not in (4, 5, 7, 8, 9, 10):
                return
            for gi in range(1, NG):
                g.wait_ge(sem["dso"], 16 * 2 * (gi + 1))
                g.wait_ge(sem["vB"], gi)
                mmb_load(gi)
            if dbg:
                g.wait_ge(sem["vdbg"], 1)
                g.dma_start(d_dbg1[:], s_x1[:]).then_inc(sem["ddbg"], 16)
                g.wait_ge(sem["vdbg"], 3)
                g.dma_start(d_dbg3[:], s_r[:]).then_inc(sem["ddbg"], 16)
                g.dma_start(d_dbg4[:], s_kact[:]).then_inc(sem["ddbg"], 16)
                g.wait_ge(sem["vdbg"], 4)
                g.dma_start(d_dbg2[:], s_u2[:]).then_inc(sem["ddbg"], 16)

        @block.sync
        def _(sy):
            nsm = 0
            for dst, src in [(s_bin, d_bin), (s_bdt, d_bdt), (s_bout, d_bout),
                             (s_bxp, d_bxp), (s_alog, d_alog), (s_dp1, d_dvec),
                             (s_mur, d_mur), (s_muk, d_muk), (s_xbT, d_xbT),
                             (s_hmask, d_hmask), (s_rmask, d_rmask)]:
                sy.dma_start(dst[:], src[:]).then_inc(sem["dsm"], 16)
                nsm += 1
            for gi in range(NG):
                for h in range(2):
                    sy.wait_ge(sem["sso"], gi * CC + (h + 1) * 4)
                    buf = s_sp0 if h == 0 else s_sp1
                    src = buf[:].rearrange("p (c t) -> p c t", c=4)
                    dst = d_spill.ap().rearrange("(c p) t -> p c t", c=CC, p=128)
                    dst = dst[:, h * 4:(h + 1) * 4, gi * N:(gi + 1) * N]
                    sy.dma_start(dst, src).then_inc(sem["dso"], 16)

            def xb_load(gi):
                buf = s_xb0 if gi % 2 == 0 else s_xb1
                src = d_xT.ap().rearrange("(c p) t -> p c t", c=CC, p=128)
                src = src[:, :, gi * N:(gi + 1) * N]
                dst = buf[:].rearrange("p (c t) -> p c t", c=CC)
                sy.dma_start(dst, src).then_inc(sem["dxb"], 16)

            if stage >= 3:
                sy.wait_ge(sem["tpa"], JC * NG)
                xb_load(0)
                if NG > 1:
                    sy.wait_ge(sem["tpd"], CC * NG)
                    xb_load(1)
            if stage >= 2:
                sy.wait_ge(sem["vpay"], 1)
                sy.dma_start(d_ccin[:], s_pay[:]).then_inc(sem["dpay"], 16)
            if stage in (4, 5, 7, 8, 9, 10):
                sy.wait_ge(sem["vod"], NG)
                sy.dma_start(d_out[0:128, 0:128], s_junk[:]).then_inc(sem["dod"], 16)
                sy.wait_ge(sem["dod"], 16)
            elif stage >= 3:
                nod = 0
                for gi in range(NG):
                    if gi + 2 < NG:
                        sy.wait_ge(sem["vB"], gi + 1)
                        xb_load(gi + 2)
                    for cc in range(CC):
                        sy.wait_ge(sem["vod"], gi * CC + cc + 1)
                        buf = (s_ost0, s_ost1, s_ost2, s_ost3)[cc % 4]
                        dst = d_out[cc * 128:(cc + 1) * 128, gi * N:(gi + 1) * N]
                        sy.dma_start(dst, buf[:]).then_inc(sem["dod"], 16)
                        nod += 1
                sy.wait_ge(sem["dod"], 16 * nod)
            else:
                if stage == 1:
                    sy.wait_ge(sem["vss"], NG)
                else:
                    sy.wait_ge(sem["vbnd"], 3)
                sy.wait_ge(sem["sso"], CC * NG)
                sy.dma_start(d_out[0:128, 0:128], s_junk[:]).then_inc(sem["dod"], 16)
                sy.wait_ge(sem["dod"], 16)
            sy.wait_ge(sem["dso"], 16 * 2 * NG)

        @block.tensor
        def _(t):
            def win_ap(cc, jc):
                o = (jc * CC + cc) * 128
                return s_win[:, o:o + 128]

            def wdt_ap(jc, dc):
                o = (jc * CC + dc) * 128
                return s_wdt[:, o:o + 128]

            def wout_ap(jc, dc):
                o = (jc * CC + dc) * 128
                return s_wout[:, o:o + 128]

            def wffn_ap(w, cc, dc):
                o = (cc * CC + dc) * 128
                return w[:, o:o + 128]

            t.wait_ge(sem["dsm"], 16 * 11)
            t.wait_ge(sem["vin"], 1)
            for gi in range(NG):
                xa = s_xa0 if gi % 2 == 0 else s_xa1
                t.wait_ge(sem["dxa"], 16 * (gi + 1))
                def win_jc(jc):
                    i = gi * JC + jc
                    if gi == 0:
                        t.wait_ge(sem["dw"], 16 * (jc // 8 + 1))
                    if i >= 2:
                        t.wait_ge(sem["sea"], i - 1)
                    for cc in range(CC):
                        mm = t.matmul(pb[jc % 2][:], win_ap(cc, jc),
                                      xa[:, cc * N:(cc + 1) * N],
                                      start=(cc == 0), stop=(cc == CC - 1))
                    mm.then_inc(sem["tpa"], 1)

                for jc in range(JI):
                    win_jc(jc)

                def mean_mm(dc):
                    t.wait_ge(sem["sdt"], gi * CC + dc + 1)
                    if dc == 0:
                        t.wait_ge(sem["vss"], gi)
                    dl = s_del0 if dc % 2 == 0 else s_del1
                    mm = t.matmul(pb[5][0:1, :], s_onescb[:], dl[:],
                                  start=(dc == 0), stop=(dc == CC - 1))
                    mm.then_inc(sem["tpm"], 1)

                if gi == 0:
                    t.wait_ge(sem["dw"], 16 * 6)
                t.wait_ge(sem["sea"], gi * JC + JI)
                for dc in range(CC):
                    j = gi * CC + dc
                    if j >= 2:
                        t.wait_ge(sem["sdt"], j - 1)
                    for jc in range(JI):
                        mm = t.matmul(pb[2 + dc % 2][:], wdt_ap(jc, dc),
                                      s_xin[:, jc * N:(jc + 1) * N],
                                      start=(jc == 0), stop=(jc == JI - 1))
                    mm.then_inc(sem["tpd"], 1)
                    if dc >= 2:
                        mean_mm(dc - 2)
                mean_mm(CC - 2)
                mean_mm(CC - 1)
                if gi == 0:
                    t.wait_ge(sem["dw"], 16 * 7)
                t.wait_ge(sem["vbx"], gi)
                for jc in range(JI):
                    mm = t.matmul(pb[4][0:DS, :], s_wxp[:, jc * DS:(jc + 1) * DS],
                                  s_xin[:, jc * N:(jc + 1) * N],
                                  start=(jc == 0), stop=(jc == JI - 1))
                mm.then_inc(sem["tpx"], 1)
                for jc in range(JI, JC):
                    win_jc(jc)
                t.wait_ge(sem["vd"], gi + 1)
                t.wait_ge(sem["ssa"], gi + 1)
                t.matmul(pb[5][0:DS, :], s_ones1_64[:], s_dsb[:],
                         start=True, stop=True).then_inc(sem["tdb"], 1)
                if gi == 0:
                    t.wait_ge(sem["dw"], 16 * 9)
                t.wait_ge(sem["vz"], (gi + 1) * JI)
                for dc in range(CC):
                    j = gi * CC + dc
                    if j >= 2:
                        t.wait_ge(sem["sso"], j - 1)
                    for jc in range(JI):
                        mm = t.matmul(pb[6 + dc % 2][:], wout_ap(jc, dc),
                                      s_xin[:, jc * N:(jc + 1) * N],
                                      start=(jc == 0), stop=(jc == JI - 1))
                    mm.then_inc(sem["tpo"], 1)
                t.wait_ge(sem["vsc"], gi + 1)
                t.wait_ge(sem["vss"], gi)
                sbf = s_s0 if gi % 2 == 0 else s_s1
                t.matmul(pb[5][0:1, :], s_ones64c[:], sbf[:],
                         start=True, stop=True).then_inc(sem["tss"], 1)

            if stage < 2:
                return
            t.wait_ge(sem["vss"], NG)
            t.matmul(pb[5][:, 0:1], s_ones1b[:], s_ssum[0:1, T - 1:T],
                     start=True, stop=True).then_inc(sem["tss"], 1)
            t.wait_ge(sem["vpay"], 2)
            for gi in range(NG):
                if gi >= 2:
                    t.wait_ge(sem["vcor"], gi - 1)
                mm = t.matmul(pb[6 + gi % 2][0:1, :], s_hinb[:],
                              s_G[:, gi * N:(gi + 1) * N], start=True, stop=True)
                mm.then_inc(sem["tcr"], 1)
            t.wait_ge(sem["vbnd"], 1)
            t.matmul(pb[5][0:1, 0:2 * CC], s_onescf[:], s_junk[:, 0:2 * CC],
                     start=True, stop=True).then_inc(sem["tB"], 1)
            t.wait_ge(sem["ssa"], NG + 2)
            t.matmul(pb[5][:, 2:4], s_ones1[:], s_stt2[:],
                     start=True, stop=True).then_inc(sem["tB"], 1)

            if stage < 3:
                return

            def t_ps(g):
                if g == 0:
                    t.wait_ge(sem["vcor"], NG)
                    t.wait_ge(sem["dw"], 16 * 12)
                t.wait_ge(sem["vB"], g)
                t.matmul(pb[0][:], s_ones1b[:], s_ssum[0:1, g * N:(g + 1) * N],
                         start=True, stop=True).then_inc(sem["tB"], 1)

            def t_ln_stats(g, mean_aps, sq_src, mb, qb, vub_n, vst_n, sbc_n,
                           usem="vub"):
                t.wait_ge(sem[usem], vub_n)
                if vst_n:
                    t.wait_ge(sem["vst"], vst_n)
                if sbc_n:
                    t.wait_ge(sem["sbc"], sbc_n)
                for cc in range(CC):
                    t.matmul(mb[0:1, :], s_onescb[:], mean_aps[cc],
                             start=(cc == 0), stop=(cc == CC - 1))
                for cc in range(CC):
                    sl = slice(cc * N, (cc + 1) * N)
                    mm = t.matmul(qb[0:1, :], s_onescb[:], sq_src[:, sl],
                                  start=(cc == 0), stop=(cc == CC - 1))
                mm.then_inc(sem["tB"], 1)

            def t_ln_bcast(mb, qb, base, vst_n, srs_n):
                t.wait_ge(sem["vst"], vst_n)
                t.matmul(mb[:], s_ones1[:], s_stat[0:1, base:base + N],
                         start=True, stop=True).then_inc(sem["tB"], 1)
                t.wait_ge(sem["srs"], srs_n)
                t.matmul(qb[:], s_ones1[:], s_stat[0:1, base + N:base + 2 * N],
                         start=True, stop=True).then_inc(sem["tB"], 1)

            def t_ln1(g):
                t_ln_stats(g, [s_x1[:, cc * N:(cc + 1) * N] for cc in range(CC)],
                           s_usq1, pb[1], pb[2], VUB1[g],
                           VST1[g - 1] if g >= 1 else 0,
                           SBC1[g - 1] if g >= 1 else 0)
                t_ln_bcast(pb[1], pb[2], 0, VST1[g], SRS1[g])

            def t_ln2(g):
                u2aps = ([s_u2[:, cc * N:(cc + 1) * N] for cc in range(CC)]
                         if g % 2 == 0 else list(s_u2b))
                t_ln_stats(g, u2aps, s_usq1, pb[3], pb[4], g + 1,
                           VST2[g - 1] if g >= 1 else 0,
                           SBC2[g - 1] if g >= 1 else 0, usem="vu2")
                t_ln_bcast(pb[3], pb[4], 3 * N, VST2[g], SRS2[g])

            t_ps(0)
            t_ln1(0)
            for gi in range(NG):
                if gi < NG - 1:
                    t_ps(gi + 1)
                t.wait_ge(sem["vmxr"], gi + 1)
                fi = 0
                for dc in range(CC):
                    if fi < 3:
                        if 8 * gi + fi - 2 > 0:
                            t.wait_ge(sem["vgt"], 8 * gi + fi - 2)
                    else:
                        t.wait_ge(sem["pfc"], 16 * gi + fi - 2)
                    for cc in range(CC):
                        mm = t.matmul(pb[5 + fi % 3][:], wffn_ap(s_wr, cc, dc),
                                      s_xrf[:, cc * N:(cc + 1) * N],
                                      start=(cc == 0), stop=(cc == CC - 1))
                    mm.then_inc(sem["tB"], 1)
                    fi += 1
                if gi >= 1:
                    t_ln2(gi - 1)
                t.wait_ge(sem["vmxk"], gi + 1)
                for dc in range(CC):
                    t.wait_ge(sem["pfc"], 16 * gi + fi - 2)
                    for cc in range(CC):
                        mm = t.matmul(pb[5 + fi % 3][:], wffn_ap(s_wk, cc, dc),
                                      s_xkf[:, cc * N:(cc + 1) * N],
                                      start=(cc == 0), stop=(cc == CC - 1))
                    mm.then_inc(sem["tB"], 1)
                    fi += 1
                if gi < NG - 1:
                    t_ln1(gi + 1)
                for cc in range(CC):
                    if cc == 0:
                        t.wait_ge(sem["pfc"], 16 * gi + 16)
                    elif cc in (1, 2):
                        pass
                    else:
                        t.wait_ge(sem["vgt"], 8 * gi + fi - 18)
                    for dc in range(CC):
                        mm = t.matmul(pb[5 + fi % 3][:], wffn_ap(s_wv, dc, cc),
                                      s_kact[:, dc * N:(dc + 1) * N],
                                      start=(dc == 0), stop=(dc == CC - 1))
                    mm.then_inc(sem["tB"], 1)
                    fi += 1
            t_ln2(NG - 1)

        @block.scalar
        def _(s):
            s.wait_ge(sem["dsm"], 16 * 11)
            s.activation(s_A[:], s_alog[:], AF.Exp)
            s.memzero(s_junks[:])
            s.memzero(s_junks[:]).then_inc(sem["ssa"], 1)
            for gi in range(NG):
                for jc in range(JI):
                    s.wait_ge(sem["tpa"], gi * JC + jc + 1)
                    if gi > 0 and jc == 0:
                        s.wait_ge(sem["tpo"], gi * CC)
                    s.activation(s_xin[:, jc * N:(jc + 1) * N], pb[jc % 2][:],
                                 AF.Silu, bias=s_bin[:, jc:jc + 1]
                                 ).then_inc(sem["sea"], 1)
                for dc in range(CC):
                    j = gi * CC + dc
                    s.wait_ge(sem["tpd"], j + 1)
                    if j >= 2:
                        s.wait_ge(sem["tpm"], j - 1)
                    dl = s_del0 if dc % 2 == 0 else s_del1
                    s.activation(dl[:], pb[2 + dc % 2][:], AF.Exp,
                                 bias=s_bdt[:, dc:dc + 1])
                    s.activation(dl[:], dl[:], AF.Ln,
                                 bias=s_c1[:]).then_inc(sem["sdt"], 1)
                for jc in range(JI, JC):
                    s.wait_ge(sem["tpa"], gi * JC + jc + 1)
                    if gi * JI + (jc - JI) >= 2:
                        s.wait_ge(sem["vz"], gi * JI + (jc - JI) - 1)
                    dst = (s_sres0 if jc % 2 == 0 else s_sres1)[:]
                    s.activation(dst, pb[jc % 2][:], AF.Silu,
                                 bias=s_bin[:, jc:jc + 1]).then_inc(sem["sea"], 1)
                s.wait_ge(sem["tdb"], gi + 1)
                s.wait_ge(sem["vsc"], gi)
                s.activation(s_a[:], pb[5][0:DS, :], AF.Exp,
                             scale=s_A[:]).then_inc(sem["ssa"], 1)
                for dc in range(CC):
                    s.wait_ge(sem["tpo"], gi * CC + dc + 1)
                    if gi > 0:
                        h = 0 if dc < 4 else 1
                        s.wait_ge(sem["dso"], 16 * (2 * (gi - 1) + h + 1))
                    buf = s_sp0 if dc < 4 else s_sp1
                    s.activation(buf[:, (dc % 4) * N:(dc % 4 + 1) * N],
                                 pb[6 + dc % 2][:], AF.Copy).then_inc(sem["sso"], 1)
            if stage < 2:
                return
            s.wait_ge(sem["vbnd"], 2)
            s.activation(s_junk[0:1, 24:25], s_stt2[0:1, 1:2], AF.Ln,
                         bias=s_ceps[0:1, 0:1])
            s.memzero(s_junks[:])
            s.memzero(s_junks[:])
            s.activation(s_stt2[0:1, 1:2], s_junk[0:1, 24:25], AF.Exp,
                         scale=s_cm05[0:1, 0:1])
            s.memzero(s_junks[:])
            s.memzero(s_junks[:]).then_inc(sem["ssa"], 1)
            if stage < 3:
                return

            def s_rstd(vst_n, base):
                s.wait_ge(sem["vst"], vst_n)
                s.activation(s_stat[0:1, base + 2 * N:base + 3 * N],
                             s_stat[0:1, base + 2 * N:base + 3 * N],
                             AF.Ln, bias=s_ceps[0:1, 0:1])
                s.activation(s_stat[0:1, base + N:base + 2 * N],
                             s_stat[0:1, base + 2 * N:base + 3 * N],
                             AF.Exp, scale=s_cm05[0:1, 0:1])
                s.memzero(s_junks[:])
                s.memzero(s_junks[:]).then_inc(sem["srs"], 1)

            def s_evac(tb_m, tb_r, mb, qb, dst_m, dst_r):
                s.wait_ge(sem["tB"], tb_m)
                s.activation(dst_m[:], mb[:], AF.Copy)
                s.wait_ge(sem["tB"], tb_r)
                s.activation(dst_r[:], qb[:], AF.Copy)
                s.memzero(s_junks[:])
                s.memzero(s_junks[:]).then_inc(sem["sbc"], 2)

            s_rstd(VST1[0], 0)
            s_evac(TB_LN1BM[0], TB_LN1BR[0], pb[1], pb[2], s_sres0, s_sres1)
            for gi in range(NG):
                for dc in range(CC):
                    s.wait_ge(sem["tB"], TB_WR[gi][dc])
                    s.activation(s_r[:, dc * N:(dc + 1) * N], pb[5 + dc % 3][:],
                                 AF.Sigmoid).then_inc(sem["pfc"], 1)
                if gi >= 1:
                    s_rstd(VST2[gi - 1], 3 * N)
                    s_evac(TB_LN2BM[gi - 1], TB_LN2BR[gi - 1], pb[3], pb[4],
                           s_gt0, s_gt1)
                for dc in range(CC):
                    s.wait_ge(sem["tB"], TB_WK[gi][dc])
                    ka = s_kact[:, dc * N:(dc + 1) * N]
                    s.activation(ka, pb[5 + (8 + dc) % 3][:], AF.Relu)
                    s.activation(ka, ka, AF.Square).then_inc(sem["pfc"], 1)
                if gi < NG - 1:
                    s_rstd(VST1[gi + 1], 0)
                    s_evac(TB_LN1BM[gi + 1], TB_LN1BR[gi + 1], pb[1], pb[2],
                           s_sres0, s_sres1)
                s.wait_ge(sem["vgt"], 8 * gi + 8)
                for dc in range(CC):
                    src = (s_u2[:, dc * N:(dc + 1) * N] if gi % 2 == 0
                           else s_u2b[dc])
                    s.activation(s_usq1[:, dc * N:(dc + 1) * N], src, AF.Square)
                s.memzero(s_junks[:])
                s.memzero(s_junks[:]).then_inc(sem["vu2"], 1)
            s_rstd(VST2[NG - 1], 3 * N)
            s_evac(TB_LN2BM[NG - 1], TB_LN2BR[NG - 1], pb[3], pb[4],
                   s_gt0, s_gt1)

        @block.vector
        def _(v):
            def pad():
                v.memset(s_junk[:, 64:96], 0.0)

            v.wait_ge(sem["dsm"], 16 * 11)
            v.memset(s_ones1[:], 1.0)
            v.memset(s_ones1b[:], 1.0)
            v.memset(s_ones1_64[:], 1.0)
            v.memset(s_ones64c[:], 1.0)
            v.memset(s_onescb[:], 1.0)
            v.memset(s_onescf[:], 1.0)
            v.memset(s_c1[:], 1.0)
            v.memset(s_ceps[:], 1e-5)
            v.memset(s_cm05[:], -0.5)
            v.memset(s_pay[:], 0.0)
            v.memset(s_junk[:], 0.0)
            v.wait_ge(sem["ssa"], 1)
            v.tensor_scalar(s_A[:], s_A[:], -1.0, None, ALU.mult)
            v.tensor_scalar(s_dp1[:], s_dp1[:], 1.0, None, ALU.add)
            pad()
            v.engine_nop().then_inc(sem["vin"], 1)

            for gi in range(NG):
                v.wait_ge(sem["tpx"], gi + 1)
                v.wait_ge(sem["tpd"], CC * (gi + 1))
                for jc in range(JI):
                    v.wait_ge(sem["sea"], gi * JC + JI + jc + 1)
                    sr = s_sres0 if jc % 2 == 0 else s_sres1
                    v.tensor_mul(s_xin[:, jc * N:(jc + 1) * N],
                                 s_xin[:, jc * N:(jc + 1) * N], sr[:]
                                 ).then_inc(sem["vz"], 1)
                v.wait_ge(sem["tpm"], CC * (gi + 1))
                v.tensor_scalar(s_dsb[:], pb[5][0:1, :], 1.0 / C, None,
                                ALU.mult)
                pad(); pad()
                v.engine_nop().then_inc(sem["vd"], 1)
                v.wait_ge(sem["tpx"], gi + 1)
                v.tensor_scalar(s_bx[:], pb[4][0:DS, :], s_bxp[:], None,
                                ALU.add).then_inc(sem["vbx"], 1)
                v.wait_ge(sem["ssa"], gi + 2)
                sbf = s_s0 if gi % 2 == 0 else s_s1
                pbf = s_s1 if gi % 2 == 0 else s_s0
                if gi >= 2:
                    v.wait_ge(sem["tss"], gi - 1)
                init_s = 0.0 if gi == 0 else pbf[:, N - 1:N]
                init_g = 1.0 if gi == 0 else s_G[:, gi * N - 1:gi * N]
                v.tensor_tensor_scan(sbf[:], s_a[:], s_bx[:], init_s,
                                     ALU.mult, ALU.add)
                v.tensor_tensor_scan(s_G[:, gi * N:(gi + 1) * N], s_a[:],
                                     s_bx[:], init_g, ALU.mult, ALU.bypass)
                v.engine_nop().then_inc(sem["vsc"], 1)
                v.wait_ge(sem["tss"], gi + 1)
                v.tensor_copy(s_ssum[0:1, gi * N:(gi + 1) * N], pb[5][0:1, :])
                pad()
                v.engine_nop().then_inc(sem["vss"], 1)

            if stage < 2:
                return
            sbf7 = s_s0 if (NG - 1) % 2 == 0 else s_s1
            v.wait_ge(sem["sso"], CC * NG)
            v.tensor_copy(s_pay[0:DS, 0:1], sbf7[:, N - 1:N])
            v.wait_ge(sem["tss"], NG + 1)
            v.tensor_copy(s_slastc[:], pb[5][:, 0:1])
            pad(); pad()
            for cc in range(CC):
                sp = s_sp0 if cc < 4 else s_sp1
                col = (cc % 4) * N + (N - 1)
                v.tensor_scalar(s_pay[:, 1 + cc:2 + cc], sp[:, col:col + 1],
                                s_slastc[:], None, ALU.mult)
            pad(); pad(); pad()
            v.engine_nop().then_inc(sem["vpay"], 1)
            v.wait_ge(sem["dgat"], 16)
            v.tensor_mul(s_gath[:], s_gath[:], s_rmask[:])
            pad(); pad()
            half = NCORES * 9 // 2
            v.tensor_add(s_gath[:, 0:half], s_gath[:, 0:half], s_gath[:, half:])
            pad(); pad()
            v.tensor_add(s_gath[:, 0:18], s_gath[:, 0:18], s_gath[:, 18:36])
            pad(); pad()
            v.tensor_add(s_sel[:, 0:9], s_gath[:, 0:9], s_gath[:, 9:18])
            pad(); pad()
            v.tensor_copy(s_hinb[:], s_sel[0:DS, 0:1])
            pad(); pad()
            v.engine_nop().then_inc(sem["vpay"], 1)
            for gi in range(NG):
                v.wait_ge(sem["tcr"], gi + 1)
                sl = slice(gi * N, (gi + 1) * N)
                if gi < NG - 1:
                    v.tensor_add(s_ssum[0:1, sl], s_ssum[0:1, sl],
                                 pb[6 + gi % 2][0:1, :]).then_inc(sem["vcor"], 1)
                else:
                    v.tensor_add(s_ssum[0:1, sl], s_ssum[0:1, sl],
                                 pb[6 + gi % 2][0:1, :])
                    pad(); pad()
                    v.engine_nop().then_inc(sem["vcor"], 1)
            v.tensor_mul(s_ub[:], s_xbT[:], s_dp1[:])
            pad(); pad()
            v.tensor_add(s_ub[:], s_ub[:], s_bout[:])
            pad(); pad()
            v.tensor_add(s_ub[:], s_ub[:], s_sel[:, 1:9])
            pad(); pad()
            v.tensor_copy(s_junk[:, 0:CC], s_ub[:])
            v.tensor_mul(s_junk[:, CC:2 * CC], s_ub[:], s_ub[:])
            pad(); pad()
            v.engine_nop().then_inc(sem["vbnd"], 1)
            v.wait_ge(sem["tB"], 1)
            v.reduce_sum(s_stt2[0:1, 0:1], pb[5][0:1, 0:CC], axis=AX)
            v.reduce_sum(s_stt2[0:1, 1:2], pb[5][0:1, CC:2 * CC], axis=AX)
            pad(); pad()
            v.tensor_scalar(s_stt2[0:1, 0:2], s_stt2[0:1, 0:2], 1.0 / C, None,
                            ALU.mult)
            pad(); pad()
            v.tensor_mul(s_junk[0:1, 16:17], s_stt2[0:1, 0:1], s_stt2[0:1, 0:1])
            pad(); pad()
            v.tensor_sub(s_stt2[0:1, 1:2], s_stt2[0:1, 1:2], s_junk[0:1, 16:17])
            pad(); pad()
            v.engine_nop().then_inc(sem["vbnd"], 1)
            v.wait_ge(sem["ssa"], NG + 2)
            v.wait_ge(sem["tB"], 2)
            v.tensor_copy(s_junk[:, 32:34], pb[5][:, 2:4])
            pad(); pad()
            v.tensor_scalar(s_ub[:], s_ub[:], s_junk[:, 32:33],
                            s_junk[:, 33:34], ALU.subtract, ALU.mult)
            pad(); pad()
            v.tensor_scalar(s_prevc[:], s_ub[:], s_hmask[:], None, ALU.mult)
            pad(); pad()
            v.engine_nop().then_inc(sem["vbnd"], 1)

            if stage < 3:
                return

            def v_u_steps(g):
                xb = s_xb0 if g % 2 == 0 else s_xb1
                v.wait_ge(sem["dxb"], 16 * (g + 1))
                v.wait_ge(sem["dmb"], 16 * (g + 1))
                v.wait_ge(sem["tB"], TB_PS[g])
                for cc in range(CC):
                    sl = slice(cc * N, (cc + 1) * N)
                    v.tensor_mul(s_mmb[:, sl], s_mmb[:, sl], pb[0][:])
                for cc in range(CC):
                    sl = slice(cc * N, (cc + 1) * N)
                    v.scalar_tensor_tensor(s_x1[:, sl], xb[:, sl],
                                           s_dp1[:, cc:cc + 1], s_mmb[:, sl],
                                           ALU.mult, ALU.add)
                for cc in range(CC):
                    sl = slice(cc * N, (cc + 1) * N)
                    v.tensor_scalar(s_x1[:, sl], s_x1[:, sl],
                                    s_bout[:, cc:cc + 1], None, ALU.add)
                v.engine_nop().then_inc(sem["vB"], 1)

            def v_usq(src, dst):
                for cc in range(CC):
                    sl = slice(cc * N, (cc + 1) * N)
                    v.tensor_mul(dst[:, sl], src[:, sl], src[:, sl])
                v.engine_nop().then_inc(sem["vub"], 1)

            def v_ln_vals(tb_n, base, mbank, qbank):
                v.wait_ge(sem["tB"], tb_n)
                v.tensor_scalar(s_stat[0:1, base:base + N], mbank[0:1, :],
                                1.0 / C, None, ALU.mult)
                v.tensor_mul(s_stat[0:1, base + 2 * N:base + 3 * N],
                             s_stat[0:1, base:base + N],
                             s_stat[0:1, base:base + N])
                v.scalar_tensor_tensor(s_stat[0:1, base + 2 * N:base + 3 * N],
                                       qbank[0:1, :], 1.0 / C,
                                       s_stat[0:1, base + 2 * N:base + 3 * N],
                                       ALU.mult, ALU.subtract)
                pad(); pad()
                v.engine_nop().then_inc(sem["vst"], 2)

            def v_apply(g):
                v.wait_ge(sem["sbc"], SBC1[g])
                for cc in range(CC):
                    sl = slice(cc * N, (cc + 1) * N)
                    v.tensor_sub(s_x1[:, sl], s_x1[:, sl], s_sres0[:])
                for cc in range(CC):
                    sl = slice(cc * N, (cc + 1) * N)
                    v.tensor_mul(s_x1[:, sl], s_x1[:, sl], s_sres1[:])

            def v_mixr(g):
                for cc in range(CC):
                    o = cc * N
                    v.tensor_sub(s_diff[:, o + 1:o + N], s_x1[:, o + 1:o + N],
                                 s_x1[:, o:o + N - 1])
                for cc in range(CC):
                    o = cc * N
                    v.tensor_sub(s_diff[:, o:o + 1], s_x1[:, o:o + 1],
                                 s_prevc[:, cc:cc + 1])
                for cc in range(CC):
                    o = cc * N
                    v.scalar_tensor_tensor(s_xrf[:, o + 1:o + N],
                                           s_diff[:, o + 1:o + N],
                                           s_mur[:, cc:cc + 1],
                                           s_x1[:, o:o + N - 1],
                                           ALU.mult, ALU.add)
                for cc in range(CC):
                    o = cc * N
                    v.scalar_tensor_tensor(s_xrf[:, o:o + 1], s_diff[:, o:o + 1],
                                           s_mur[:, cc:cc + 1],
                                           s_prevc[:, cc:cc + 1],
                                           ALU.mult, ALU.add)
                pad()
                v.engine_nop().then_inc(sem["vmxr"], 1)

            def v_mixk(g):
                for cc in range(CC):
                    o = cc * N
                    v.scalar_tensor_tensor(s_xkf[:, o:o + 1], s_diff[:, o:o + 1],
                                           s_muk[:, cc:cc + 1],
                                           s_prevc[:, cc:cc + 1],
                                           ALU.mult, ALU.add)
                for cc in range(CC):
                    o = cc * N
                    v.scalar_tensor_tensor(s_xkf[:, o + 1:o + N],
                                           s_diff[:, o + 1:o + N],
                                           s_muk[:, cc:cc + 1],
                                           s_x1[:, o:o + N - 1],
                                           ALU.mult, ALU.add)
                for cc in range(CC):
                    o = cc * N
                    v.tensor_copy(s_prevc[:, cc:cc + 1], s_x1[:, o + N - 1:o + N])
                pad()
                v.engine_nop().then_inc(sem["vmxk"], 1)

            def u2ap(g, cc):
                return (s_u2[:, cc * N:(cc + 1) * N] if g % 2 == 0
                        else s_u2b[cc])

            def v_out(g):
                v.wait_ge(sem["sbc"], SBC2[g])
                for cc in range(CC):
                    k = g * CC + cc
                    if k >= 4:
                        v.wait_ge(sem["dod"], 16 * (k - 3))
                    ost = (s_ost0, s_ost1, s_ost2, s_ost3)[cc % 4]
                    v.tensor_sub(ost[:], u2ap(g, cc), s_gt0[:])
                    v.tensor_mul(ost[:], ost[:], s_gt1[:]).then_inc(sem["vod"], 1)

            v.wait_ge(sem["dw"], 16 * 12)
            v_u_steps(0)
            v_usq(s_x1, s_usq1)
            v_ln_vals(TB_LN1S[0], 0, pb[1], pb[2])
            v_apply(0)
            if dbg:
                pad()
                v.engine_nop().then_inc(sem["vdbg"], 1)
                v.wait_ge(sem["ddbg"], 16)
            v.wait_ge(sem["dso"], 16 * 2 * NG)
            v_mixr(0)
            v_mixk(0)
            for gi in range(NG):
                for cc in range(CC):
                    v.tensor_copy(u2ap(gi, cc), s_x1[:, cc * N:(cc + 1) * N])
                if gi < NG - 1:
                    v_u_steps(gi + 1)
                    v_usq(s_x1, s_usq1)
                if gi >= 1:
                    v_out(gi - 1)
                if gi < NG - 1:
                    v_ln_vals(TB_LN1S[gi + 1], 0, pb[1], pb[2])
                if dbg and gi == 0:
                    v.wait_ge(sem["pfc"], 16)
                    pad()
                    v.engine_nop().then_inc(sem["vdbg"], 2)
                    v.wait_ge(sem["ddbg"], 48)
                for cc in range(CC):
                    o = cc * N
                    gt = s_gt0 if cc % 2 == 0 else s_gt1
                    v.wait_ge(sem["tB"], TB_WV[gi][cc])
                    v.tensor_mul(gt[:], s_r[:, o:o + N], pb[5 + (16 + cc) % 3][:])
                    v.tensor_add(u2ap(gi, cc), u2ap(gi, cc), gt[:]
                                 ).then_inc(sem["vgt"], 1)
                    if gi < NG - 1:
                        if cc == 0:
                            v.wait_ge(sem["sbc"], SBC1[gi + 1])
                        v.tensor_sub(s_x1[:, o:o + N], s_x1[:, o:o + N],
                                     s_sres0[:])
                        v.tensor_mul(s_x1[:, o:o + N], s_x1[:, o:o + N],
                                     s_sres1[:])
                        v.tensor_sub(s_diff[:, o + 1:o + N],
                                     s_x1[:, o + 1:o + N],
                                     s_x1[:, o:o + N - 1])
                        v.tensor_sub(s_diff[:, o:o + 1], s_x1[:, o:o + 1],
                                     s_prevc[:, cc:cc + 1])
                        v.scalar_tensor_tensor(s_xrf[:, o + 1:o + N],
                                               s_diff[:, o + 1:o + N],
                                               s_mur[:, cc:cc + 1],
                                               s_x1[:, o:o + N - 1],
                                               ALU.mult, ALU.add)
                        v.scalar_tensor_tensor(s_xrf[:, o:o + 1],
                                               s_diff[:, o:o + 1],
                                               s_mur[:, cc:cc + 1],
                                               s_prevc[:, cc:cc + 1],
                                               ALU.mult, ALU.add)
                if gi < NG - 1:
                    pad()
                    v.engine_nop().then_inc(sem["vmxr"], 1)
                    v.wait_ge(sem["tB"], TB_WK[gi][CC - 1])
                    v_mixk(gi + 1)
                if dbg and gi == 0:
                    pad()
                    v.engine_nop().then_inc(sem["vdbg"], 1)
                    v.wait_ge(sem["ddbg"], 64)
                v_ln_vals(TB_LN2S[gi], 3 * N, pb[3], pb[4])
            v_out(NG - 1)

    nc.finalize()
    return nc


_NC_CACHE = {}


def _get_nc():
    if "nc" not in _NC_CACHE:
        _NC_CACHE["nc"] = build_nc()
    return _NC_CACHE["nc"]


def _chunk_weight(w, kchunks):
    K, M = w.shape
    assert K == kchunks * 128
    return np.ascontiguousarray(
        w.reshape(kchunks, 128, M).transpose(1, 0, 2).reshape(128, kchunks * M))


def _col_tile(vec, ncol):
    return np.ascontiguousarray(vec.reshape(ncol, 128).T)


def stage_inputs(L0_attn_x, L0_attn_W_in, L0_attn_b_in, L0_attn_W_xp,
                 L0_attn_b_xp, L0_attn_W_dt, L0_attn_b_dt, L0_attn_W_out,
                 L0_attn_b_out, L0_attn_A_log, L0_attn_D, L0_ffn_W_r,
                 L0_ffn_W_k, L0_ffn_W_v, L0_ffn_mu_r, L0_ffn_mu_k, T=T_FULL):
    f = np.float32
    x = np.asarray(L0_attn_x, f)
    w_in = np.asarray(L0_attn_W_in, f)
    win = np.ascontiguousarray(
        w_in.reshape(CC, 128, JC, 128).transpose(1, 2, 0, 3).reshape(128, -1))
    wdt = _chunk_weight(np.asarray(L0_attn_W_dt, f), JI)
    wxp = _chunk_weight(np.asarray(L0_attn_W_xp, f), JI)
    wout = _chunk_weight(np.asarray(L0_attn_W_out, f), JI)
    wr = _chunk_weight(np.ascontiguousarray(np.asarray(L0_ffn_W_r, f).T), CC)
    wk = _chunk_weight(np.ascontiguousarray(np.asarray(L0_ffn_W_k, f).T), CC)
    wv = _chunk_weight(np.ascontiguousarray(np.asarray(L0_ffn_W_v, f).T), CC)
    bin_t = _col_tile(np.asarray(L0_attn_b_in, f), JC)
    bdt_t = _col_tile(np.asarray(L0_attn_b_dt, f), CC)
    bout_t = _col_tile(np.asarray(L0_attn_b_out, f), CC)
    bxp_t = np.ascontiguousarray(np.asarray(L0_attn_b_xp, f).reshape(DS, 1))
    alog_t = np.ascontiguousarray(np.asarray(L0_attn_A_log, f).reshape(DS, 1))
    dvec_t = _col_tile(np.asarray(L0_attn_D, f), CC)
    mur_t = _col_tile(np.asarray(L0_ffn_mu_r, f), CC)
    muk_t = _col_tile(np.asarray(L0_ffn_mu_k, f), CC)

    shared = dict(win=win, wdt=wdt, wxp=wxp, wout=wout, wr=wr, wk=wk, wv=wv,
                  bin=bin_t, bdt=bdt_t, bout=bout_t, bxp=bxp_t, alog=alog_t,
                  dvec=dvec_t, mur=mur_t, muk=muk_t)

    in_maps = []
    for c in range(NCORES):
        b, h = c // 2, c % 2
        xT = np.ascontiguousarray(x[b, h * T:(h + 1) * T, :].T)
        if h == 1:
            xbT = _col_tile(x[b, T - 1, :], CC)
            hmask = np.ones((128, 1), f)
            rmask = np.zeros((128, NCORES * 9), f)
            rmask[:, (c - 1) * 9:c * 9] = 1.0
        else:
            xbT = np.zeros((128, CC), f)
            hmask = np.zeros((128, 1), f)
            rmask = np.zeros((128, NCORES * 9), f)
        in_maps.append(dict(shared, xT=xT, xbT=xbT, hmask=hmask, rmask=rmask))
    return in_maps


def kernel(**inputs):
    in_maps = stage_inputs(**inputs)
    nc = _get_nc()
    res = run_bass_kernel_spmd(nc, in_maps, core_ids=list(range(NCORES)))
    f = np.float32
    out = np.empty((B_, S_, C), f)
    for c in range(NCORES):
        b, h = c // 2, c % 2
        out[b, h * T_FULL:(h + 1) * T_FULL, :] = np.asarray(
            res.results[c]["outT"], f).T
    return out


# version 14
# speedup vs baseline: 1.1309x; 1.0002x over previous
import numpy as np
import ml_dtypes
import concourse.bass as bass
import concourse.mybir as mybir
from concourse.bass_utils import run_bass_kernel_spmd

F32 = mybir.dt.float32
BF16 = mybir.dt.bfloat16
AF = mybir.ActivationFunctionType
ALU = mybir.AluOpType
AX = mybir.AxisListType.X

NCORES = 8
B_, S_, C = 4, 8192, 1024
T_FULL = S_ // 2
N = 512
CC = C // 128
JC = 32
DI = 2048
JI = DI // 128
DS = 64


def build_nc(T=T_FULL, stage=3, dbg=False):
    NG = T // N
    nc = bass.Bass("TRN2", target_bir_lowering=False,
                   detect_race_conditions=False)

    d_xT = nc.dram_tensor("xT", [C, T], F32, kind="ExternalInput")
    d_win = nc.dram_tensor("win", [128, JC * CC * 128], F32, kind="ExternalInput")
    d_wdt = nc.dram_tensor("wdt", [128, JI * CC * 128], F32, kind="ExternalInput")
    d_wxp = nc.dram_tensor("wxp", [128, JI * DS], F32, kind="ExternalInput")
    d_wout = nc.dram_tensor("wout", [128, JI * CC * 128], F32, kind="ExternalInput")
    d_wr = nc.dram_tensor("wr", [128, CC * CC * 128], F32, kind="ExternalInput")
    d_wk = nc.dram_tensor("wk", [128, CC * CC * 128], F32, kind="ExternalInput")
    d_wv = nc.dram_tensor("wv", [128, CC * CC * 128], F32, kind="ExternalInput")
    d_bin = nc.dram_tensor("bin", [128, JC], F32, kind="ExternalInput")
    d_bdt = nc.dram_tensor("bdt", [128, CC], F32, kind="ExternalInput")
    d_bout = nc.dram_tensor("bout", [128, CC], F32, kind="ExternalInput")
    d_bxp = nc.dram_tensor("bxp", [DS, 1], F32, kind="ExternalInput")
    d_alog = nc.dram_tensor("alog", [DS, 1], F32, kind="ExternalInput")
    d_dvec = nc.dram_tensor("dvec", [128, CC], F32, kind="ExternalInput")
    d_mur = nc.dram_tensor("mur", [128, CC], F32, kind="ExternalInput")
    d_muk = nc.dram_tensor("muk", [128, CC], F32, kind="ExternalInput")
    d_xbT = nc.dram_tensor("xbT", [128, CC], F32, kind="ExternalInput")
    d_hmask = nc.dram_tensor("hmask", [128, 1], F32, kind="ExternalInput")
    d_rmask = nc.dram_tensor("rmask", [128, NCORES * 9], F32, kind="ExternalInput")
    d_out = nc.dram_tensor("outT", [C, T], F32, kind="ExternalOutput")
    if dbg:
        d_dbg1 = nc.dram_tensor("dbg1", [128, CC * N], F32, kind="ExternalOutput")
        d_dbg2 = nc.dram_tensor("dbg2", [128, CC * N], F32, kind="ExternalOutput")
        d_dbg3 = nc.dram_tensor("dbg3", [128, CC * N], F32, kind="ExternalOutput")
        d_dbg4 = nc.dram_tensor("dbg4", [128, CC * N], F32, kind="ExternalOutput")
    d_spill = nc.dram_tensor("spill", [C, T], BF16, kind="Internal")
    d_ccin = nc.dram_tensor("cc_in", [128, 9], F32, kind="Internal")
    d_ccout = nc.dram_tensor("cc_out", [NCORES * 128, 9], F32, kind="Internal")

    BASE = ((nc.sbuf_base + 31) // 32) * 32
    TOP = nc.sbuf_top
    cur = [BASE]

    def sb(name, shape, dtype, at=None):
        isz = 2 if dtype == BF16 else 4
        nb = int(np.prod(shape[1:])) * isz
        nb = ((nb + 31) // 32) * 32
        if at is None:
            off = cur[0]
            cur[0] += nb
        else:
            off = BASE_W0[0] + at
        assert off + nb <= TOP, (name, off, nb, TOP)
        return nc.alloc_sbuf_tensor_at(name, list(shape), dtype, offset=off)

    BASE_W0 = [0]
    s_ssum = sb("ssum", [1, T], BF16)
    s_G = sb("G", [DS, T], BF16)
    s_s0 = sb("s0", [DS, N], F32)
    s_s1 = sb("s1", [DS, N], F32)
    s_a = sb("a", [DS, N], F32)
    s_bx = sb("bx", [DS, N], F32)
    s_sres0 = sb("sres0", [128, N], BF16)
    s_sres1 = sb("sres1", [128, N], BF16)
    s_dsb = sb("dsb", [1, N], F32)
    s_bin = sb("v_bin", [128, JC], F32)
    s_bdt = sb("v_bdt", [128, CC], F32)
    s_bout = sb("v_bout", [128, CC], F32)
    s_bxp = sb("v_bxp", [DS, 1], F32)
    s_alog = sb("v_alog", [DS, 1], F32)
    s_A = sb("v_A", [DS, 1], F32)
    s_dp1 = sb("v_dp1", [128, CC], F32)
    s_mur = sb("v_mur", [128, CC], F32)
    s_muk = sb("v_muk", [128, CC], F32)
    s_xbT = sb("v_xbT", [128, CC], F32)
    s_hmask = sb("v_hmask", [128, 1], F32)
    s_ones1 = sb("ones1", [1, 128], F32)
    s_ones1b = sb("ones1b", [1, 128], BF16)
    s_ones1_64 = sb("ones1_64", [1, DS], F32)
    s_ones64c = sb("ones64c", [DS, 1], F32)
    s_onescb = sb("onescb", [128, 1], BF16)
    s_onescf = sb("onescf", [128, 1], F32)
    s_c1 = sb("c_one", [128, 1], F32)
    s_ceps = sb("c_eps", [1, 1], F32)
    s_cm05 = sb("c_m05", [1, 1], F32)
    s_junk = sb("junk", [128, 128], F32)
    s_junks = sb("junks", [128, 128], F32)
    s_pay = sb("pay", [128, 9], F32)
    s_gath = sb("gath", [128, NCORES * 9], F32)
    s_rmask = sb("rmaskT", [128, NCORES * 9], F32)
    s_sel = sb("sel", [128, 9], F32)
    s_hinb = sb("hinb", [DS, 1], BF16)
    s_ub = sb("ub", [128, CC], F32)
    s_prevc = sb("prevc", [128, CC], F32)
    s_slastc = sb("slastc", [128, 1], F32)
    s_stt2 = sb("stt2", [1, 2], F32)
    BASE_W0[0] = ((cur[0] + 31) // 32) * 32

    s_win = sb("w_in", [128, JC * CC * 128], BF16, at=0)
    s_wdt = sb("w_dt", [128, JI * CC * 128], BF16, at=65536)
    s_wxp = sb("w_xp", [128, JI * DS], BF16, at=98304)
    s_wout = sb("w_out", [128, JI * CC * 128], BF16, at=100352)
    s_xa0 = sb("xa0", [128, CC * N], BF16, at=133120)
    s_xa1 = sb("xa1", [128, CC * N], BF16, at=141312)
    s_xin = sb("xin", [128, JI * N], BF16, at=149504)
    s_del0 = sb("del0", [128, N], BF16, at=165888)
    s_del1 = sb("del1", [128, N], BF16, at=166912)
    s_sp0 = sb("sp0", [128, 4 * N], BF16, at=167936)
    s_sp1 = sb("sp1", [128, 4 * N], BF16, at=172032)
    s_diff = sb("diff", [128, CC * N], BF16, at=167936)
    s_u2bf = sb("u2b", [128, CC * N], BF16,
                at=(BASE + ((T * 2 + 31) // 32) * 32) - BASE_W0[0])
    s_u2b = [s_u2bf[:, k * N:(k + 1) * N] for k in range(CC)]
    A_END = 176128
    s_wr = sb("w_r", [128, CC * CC * 128], BF16, at=0)
    s_wk = sb("w_k", [128, CC * CC * 128], BF16, at=16384)
    s_wv = sb("w_v", [128, CC * CC * 128], BF16, at=32768)
    s_xb0 = sb("xb0", [128, CC * N], F32, at=49152)
    s_xb1 = sb("xb1", [128, CC * N], F32, at=65536)
    s_mmb = sb("mmb", [128, CC * N], F32, at=81920)
    s_x1 = sb("x1", [128, CC * N], BF16, at=98304)
    s_u2 = sb("u2", [128, CC * N], BF16, at=106496)
    s_kact = sb("kact", [128, CC * N], BF16, at=114688)
    s_r = sb("r", [128, CC * N], BF16, at=122880)
    s_usq1 = sb("usq1", [128, CC * N], BF16, at=131072)
    s_xrf = sb("xrf", [128, CC * N], BF16, at=139264)
    s_xkf = sb("xkf", [128, CC * N], BF16, at=147456)
    s_ost0 = sb("ost0", [128, N], F32, at=155648)
    s_ost1 = sb("ost1", [128, N], F32, at=157696)
    s_stat = sb("stat", [1, 6 * N], F32, at=159744)
    s_ost2 = sb("ost2", [128, N], F32, at=172032)
    s_ost3 = sb("ost3", [128, N], F32, at=174080)
    s_gt0 = sb("gt0", [128, N], BF16, at=176128)
    s_gt1 = sb("gt1", [128, N], BF16, at=177184)
    B_END = 178240
    assert BASE_W0[0] + max(A_END, B_END) <= TOP, (BASE_W0[0], TOP)

    pb = [nc.alloc_psum_tensor(f"pb{i}", [128, N], F32) for i in range(8)]

    semnames = [
        "dw", "dxa", "dxb", "dmb", "dso", "dod", "dsm", "dcc",
        "tpa", "tpd", "tpx", "tpm", "tdb", "tss", "tpo", "tgx", "t1",
        "tcr", "tB", "tst", "dpay", "dcol", "dgat",
        "sea", "sdt", "ssa", "sso", "srs", "pfc",
        "vin", "vz", "vd", "vsc", "vss", "vbx", "vpay", "vcor", "vbnd",
        "vB", "vub", "vst", "vmx", "vmxr", "vmxk", "vod", "vdbg", "ddbg", "sbc",
        "vgt", "vu2",
    ]
    sem = {n: nc.alloc_semaphore("sem_" + n) for n in semnames}
    rg = [list(range(NCORES))]

    TB_PS, TB_LN1S, TB_LN1BM, TB_LN1BR = {}, {}, {}, {}
    TB_WR, TB_WK, TB_WV = {}, {}, {}
    TB_LN2S, TB_LN2BM, TB_LN2BR = {}, {}, {}
    cnt = 2
    cnt += 1; TB_PS[0] = cnt
    cnt += 1; TB_LN1S[0] = cnt
    cnt += 1; TB_LN1BM[0] = cnt
    cnt += 1; TB_LN1BR[0] = cnt
    for g in range(NG):
        if g < NG - 1:
            cnt += 1; TB_PS[g + 1] = cnt
        TB_WR[g] = {}
        for dc in range(CC):
            cnt += 1; TB_WR[g][dc] = cnt
        if g >= 1:
            cnt += 1; TB_LN2S[g - 1] = cnt
            cnt += 1; TB_LN2BM[g - 1] = cnt
            cnt += 1; TB_LN2BR[g - 1] = cnt
        TB_WK[g] = {}
        for dc in range(CC):
            cnt += 1; TB_WK[g][dc] = cnt
        if g < NG - 1:
            cnt += 1; TB_LN1S[g + 1] = cnt
            cnt += 1; TB_LN1BM[g + 1] = cnt
            cnt += 1; TB_LN1BR[g + 1] = cnt
        TB_WV[g] = {}
        for cc in range(CC):
            cnt += 1; TB_WV[g][cc] = cnt
    cnt += 1; TB_LN2S[NG - 1] = cnt
    cnt += 1; TB_LN2BM[NG - 1] = cnt
    cnt += 1; TB_LN2BR[NG - 1] = cnt

    VB_N, VUB1, VUB2, VST1, VST2, VMXR, VMXK = {}, {}, {}, {}, {}, {}, {}
    vb = vub = vst = vmxr = vmxk = 0
    vb += 1; VB_N[0] = vb
    vub += 1; VUB1[0] = vub
    vst += 2; VST1[0] = vst
    vmxr += 1; VMXR[0] = vmxr
    vmxk += 1; VMXK[0] = vmxk
    for g in range(NG):
        if g < NG - 1:
            vb += 1; VB_N[g + 1] = vb
            vub += 1; VUB1[g + 1] = vub
            vst += 2; VST1[g + 1] = vst
            vmxr += 1; VMXR[g + 1] = vmxr
            vmxk += 1; VMXK[g + 1] = vmxk
        VUB2[g] = g + 1
        vst += 2; VST2[g] = vst
    SRS1, SRS2, SBC1, SBC2 = {}, {}, {}, {}
    srs = sbc = 0
    srs += 1; SRS1[0] = srs
    sbc += 2; SBC1[0] = sbc
    for g in range(NG):
        if g >= 1:
            srs += 1; SRS2[g - 1] = srs
            sbc += 2; SBC2[g - 1] = sbc
        if g < NG - 1:
            srs += 1; SRS1[g + 1] = srs
            sbc += 2; SBC1[g + 1] = sbc
    srs += 1; SRS2[NG - 1] = srs
    sbc += 2; SBC2[NG - 1] = sbc

    with nc.Block() as block:

        @block.gpsimd
        def _(g):
            def xa_load(gi):
                buf = s_xa0 if gi % 2 == 0 else s_xa1
                src = d_xT.ap().rearrange("(c p) t -> p c t", c=CC, p=128)
                src = src[:, :, gi * N:(gi + 1) * N]
                dst = buf[:].rearrange("p (c t) -> p c t", c=CC)
                g.dma_start(dst, src).then_inc(sem["dxa"], 16)

            def mmb_load(gi):
                src = d_spill.ap().rearrange("(c p) t -> p c t", c=CC, p=128)
                src = src[:, :, gi * N:(gi + 1) * N]
                dst = s_mmb[:].rearrange("p (c t) -> p c t", c=CC)
                g.dma_start(dst, src).then_inc(sem["dmb"], 16)

            xa_load(0)
            WQ = JC * CC * 128
            for q in range(4):
                sl = slice(q * WQ // 4, (q + 1) * WQ // 4)
                g.dma_start(s_win[:, sl], d_win[:, sl]).then_inc(sem["dw"], 16)
            if NG > 1:
                xa_load(1)
            WD = JI * CC * 128
            for q in range(2):
                sl = slice(q * WD // 2, (q + 1) * WD // 2)
                g.dma_start(s_wdt[:, sl], d_wdt[:, sl]).then_inc(sem["dw"], 16)
            g.dma_start(s_wxp[:], d_wxp[:]).then_inc(sem["dw"], 16)
            for q in range(2):
                sl = slice(q * WD // 2, (q + 1) * WD // 2)
                g.dma_start(s_wout[:, sl], d_wout[:, sl]).then_inc(sem["dw"], 16)
            for gi in range(2, NG):
                g.wait_ge(sem["tpa"], JC * (gi - 1))
                xa_load(gi)
            g.wait_ge(sem["tpa"], JC * NG)
            g.dma_start(s_wr[:], d_wr[:]).then_inc(sem["dw"], 16)
            g.dma_start(s_wk[:], d_wk[:]).then_inc(sem["dw"], 16)
            g.dma_start(s_wv[:], d_wv[:]).then_inc(sem["dw"], 16)
            if stage < 2:
                return
            if stage >= 3:
                g.wait_ge(sem["tpd"], CC * NG)
                g.wait_ge(sem["dso"], 16 * 2)
                mmb_load(0)
            g.wait_ge(sem["dpay"], 16)
            g.collective_compute(
                "AllGather", ALU.bypass, replica_groups=rg,
                ins=[d_ccin.ap().opt()], outs=[d_ccout.ap().opt()],
            ).then_inc(sem["dcol"])
            g.wait_ge(sem["dcol"], 1)
            src = d_ccout.ap().rearrange("(r p) c -> p r c", r=NCORES, p=128)
            dst = s_gath[:].rearrange("p (r c) -> p r c", r=NCORES)
            g.dma_start(dst, src).then_inc(sem["dgat"], 16)
            if stage < 3 and stage not in (4, 5, 7, 8, 9, 10):
                return
            for gi in range(1, NG):
                g.wait_ge(sem["dso"], 16 * 2 * (gi + 1))
                g.wait_ge(sem["vB"], gi)
                mmb_load(gi)
            if dbg:
                g.wait_ge(sem["vdbg"], 1)
                g.dma_start(d_dbg1[:], s_x1[:]).then_inc(sem["ddbg"], 16)
                g.wait_ge(sem["vdbg"], 3)
                g.dma_start(d_dbg3[:], s_r[:]).then_inc(sem["ddbg"], 16)
                g.dma_start(d_dbg4[:], s_kact[:]).then_inc(sem["ddbg"], 16)
                g.wait_ge(sem["vdbg"], 4)
                g.dma_start(d_dbg2[:], s_u2[:]).then_inc(sem["ddbg"], 16)

        @block.sync
        def _(sy):
            nsm = 0
            for dst, src in [(s_bin, d_bin), (s_bdt, d_bdt), (s_bout, d_bout),
                             (s_bxp, d_bxp), (s_alog, d_alog), (s_dp1, d_dvec),
                             (s_mur, d_mur), (s_muk, d_muk), (s_xbT, d_xbT),
                             (s_hmask, d_hmask), (s_rmask, d_rmask)]:
                sy.dma_start(dst[:], src[:]).then_inc(sem["dsm"], 16)
                nsm += 1
            for gi in range(NG):
                for h in range(2):
                    sy.wait_ge(sem["sso"], gi * CC + (h + 1) * 4)
                    buf = s_sp0 if h == 0 else s_sp1
                    src = buf[:].rearrange("p (c t) -> p c t", c=4)
                    dst = d_spill.ap().rearrange("(c p) t -> p c t", c=CC, p=128)
                    dst = dst[:, h * 4:(h + 1) * 4, gi * N:(gi + 1) * N]
                    sy.dma_start(dst, src).then_inc(sem["dso"], 16)

            def xb_load(gi):
                buf = s_xb0 if gi % 2 == 0 else s_xb1
                src = d_xT.ap().rearrange("(c p) t -> p c t", c=CC, p=128)
                src = src[:, :, gi * N:(gi + 1) * N]
                dst = buf[:].rearrange("p (c t) -> p c t", c=CC)
                sy.dma_start(dst, src).then_inc(sem["dxb"], 16)

            if stage >= 3:
                sy.wait_ge(sem["tpa"], JC * NG)
                xb_load(0)
                if NG > 1:
                    sy.wait_ge(sem["tpd"], CC * NG)
                    xb_load(1)
            if stage >= 2:
                sy.wait_ge(sem["vpay"], 1)
                sy.dma_start(d_ccin[:], s_pay[:]).then_inc(sem["dpay"], 16)
            if stage in (4, 5, 7, 8, 9, 10):
                sy.wait_ge(sem["vod"], NG)
                sy.dma_start(d_out[0:128, 0:128], s_junk[:]).then_inc(sem["dod"], 16)
                sy.wait_ge(sem["dod"], 16)
            elif stage >= 3:
                nod = 0
                for gi in range(NG):
                    if gi + 2 < NG:
                        sy.wait_ge(sem["vB"], gi + 1)
                        xb_load(gi + 2)
                    for cc in range(CC):
                        sy.wait_ge(sem["vod"], gi * CC + cc + 1)
                        buf = (s_ost0, s_ost1, s_ost2, s_ost3)[cc % 4]
                        dst = d_out[cc * 128:(cc + 1) * 128, gi * N:(gi + 1) * N]
                        sy.dma_start(dst, buf[:]).then_inc(sem["dod"], 16)
                        nod += 1
                sy.wait_ge(sem["dod"], 16 * nod)
            else:
                if stage == 1:
                    sy.wait_ge(sem["vss"], NG)
                else:
                    sy.wait_ge(sem["vbnd"], 3)
                sy.wait_ge(sem["sso"], CC * NG)
                sy.dma_start(d_out[0:128, 0:128], s_junk[:]).then_inc(sem["dod"], 16)
                sy.wait_ge(sem["dod"], 16)
            sy.wait_ge(sem["dso"], 16 * 2 * NG)

        @block.tensor
        def _(t):
            def win_ap(cc, jc):
                o = (jc * CC + cc) * 128
                return s_win[:, o:o + 128]

            def wdt_ap(jc, dc):
                o = (jc * CC + dc) * 128
                return s_wdt[:, o:o + 128]

            def wout_ap(jc, dc):
                o = (jc * CC + dc) * 128
                return s_wout[:, o:o + 128]

            def wffn_ap(w, cc, dc):
                o = (cc * CC + dc) * 128
                return w[:, o:o + 128]

            t.wait_ge(sem["dsm"], 16 * 11)
            t.wait_ge(sem["vin"], 1)
            for gi in range(NG):
                xa = s_xa0 if gi % 2 == 0 else s_xa1
                t.wait_ge(sem["dxa"], 16 * (gi + 1))
                def win_jc(jc):
                    i = gi * JC + jc
                    if gi == 0:
                        t.wait_ge(sem["dw"], 16 * (jc // 8 + 1))
                    if i >= 2:
                        t.wait_ge(sem["sea"], i - 1)
                    for cc in range(CC):
                        mm = t.matmul(pb[jc % 2][:], win_ap(cc, jc),
                                      xa[:, cc * N:(cc + 1) * N],
                                      start=(cc == 0), stop=(cc == CC - 1))
                    mm.then_inc(sem["tpa"], 1)

                for jc in range(JI):
                    win_jc(jc)

                def mean_mm(dc):
                    t.wait_ge(sem["sdt"], gi * CC + dc + 1)
                    if dc == 0:
                        t.wait_ge(sem["vss"], gi)
                    dl = s_del0 if dc % 2 == 0 else s_del1
                    mm = t.matmul(pb[5][0:1, :], s_onescb[:], dl[:],
                                  start=(dc == 0), stop=(dc == CC - 1))
                    mm.then_inc(sem["tpm"], 1)

                if gi == 0:
                    t.wait_ge(sem["dw"], 16 * 6)
                t.wait_ge(sem["sea"], gi * JC + JI)
                for dc in range(CC):
                    j = gi * CC + dc
                    if j >= 2:
                        t.wait_ge(sem["sdt"], j - 1)
                    for jc in range(JI):
                        mm = t.matmul(pb[2 + dc % 2][:], wdt_ap(jc, dc),
                                      s_xin[:, jc * N:(jc + 1) * N],
                                      start=(jc == 0), stop=(jc == JI - 1))
                    mm.then_inc(sem["tpd"], 1)
                    if dc >= 2:
                        mean_mm(dc - 2)
                mean_mm(CC - 2)
                mean_mm(CC - 1)
                if gi == 0:
                    t.wait_ge(sem["dw"], 16 * 7)
                t.wait_ge(sem["vbx"], gi)
                for jc in range(JI):
                    mm = t.matmul(pb[4][0:DS, :], s_wxp[:, jc * DS:(jc + 1) * DS],
                                  s_xin[:, jc * N:(jc + 1) * N],
                                  start=(jc == 0), stop=(jc == JI - 1))
                mm.then_inc(sem["tpx"], 1)
                for jc in range(JI, JC):
                    win_jc(jc)
                t.wait_ge(sem["vd"], gi + 1)
                t.wait_ge(sem["ssa"], gi + 1)
                t.matmul(pb[5][0:DS, :], s_ones1_64[:], s_dsb[:],
                         start=True, stop=True).then_inc(sem["tdb"], 1)
                if gi == 0:
                    t.wait_ge(sem["dw"], 16 * 9)
                t.wait_ge(sem["vz"], (gi + 1) * JI)
                for dc in range(CC):
                    j = gi * CC + dc
                    if j >= 2:
                        t.wait_ge(sem["sso"], j - 1)
                    for jc in range(JI):
                        mm = t.matmul(pb[6 + dc % 2][:], wout_ap(jc, dc),
                                      s_xin[:, jc * N:(jc + 1) * N],
                                      start=(jc == 0), stop=(jc == JI - 1))
                    mm.then_inc(sem["tpo"], 1)
                t.wait_ge(sem["vsc"], gi + 1)
                t.wait_ge(sem["vss"], gi)
                sbf = s_s0 if gi % 2 == 0 else s_s1
                t.matmul(pb[5][0:1, :], s_ones64c[:], sbf[:],
                         start=True, stop=True).then_inc(sem["tss"], 1)

            if stage < 2:
                return
            t.wait_ge(sem["vss"], NG)
            t.matmul(pb[5][:, 0:1], s_ones1b[:], s_ssum[0:1, T - 1:T],
                     start=True, stop=True).then_inc(sem["tss"], 1)
            t.wait_ge(sem["vpay"], 2)
            for gi in range(NG):
                if gi >= 2:
                    t.wait_ge(sem["vcor"], gi - 1)
                mm = t.matmul(pb[6 + gi % 2][0:1, :], s_hinb[:],
                              s_G[:, gi * N:(gi + 1) * N], start=True, stop=True)
                mm.then_inc(sem["tcr"], 1)
            t.wait_ge(sem["vbnd"], 1)
            t.matmul(pb[5][0:1, 0:2 * CC], s_onescf[:], s_junk[:, 0:2 * CC],
                     start=True, stop=True).then_inc(sem["tB"], 1)
            t.wait_ge(sem["ssa"], NG + 2)
            t.matmul(pb[5][:, 2:4], s_ones1[:], s_stt2[:],
                     start=True, stop=True).then_inc(sem["tB"], 1)

            if stage < 3:
                return

            def t_ps(g):
                if g == 0:
                    t.wait_ge(sem["vcor"], NG)
                    t.wait_ge(sem["dw"], 16 * 12)
                t.wait_ge(sem["vB"], g)
                t.matmul(pb[0][:], s_ones1b[:], s_ssum[0:1, g * N:(g + 1) * N],
                         start=True, stop=True).then_inc(sem["tB"], 1)

            def t_ln_stats(g, mean_aps, sq_src, mb, qb, vub_n, vst_n, sbc_n,
                           usem="vub"):
                t.wait_ge(sem[usem], vub_n)
                if vst_n:
                    t.wait_ge(sem["vst"], vst_n)
                if sbc_n:
                    t.wait_ge(sem["sbc"], sbc_n)
                for cc in range(CC):
                    t.matmul(mb[0:1, :], s_onescb[:], mean_aps[cc],
                             start=(cc == 0), stop=(cc == CC - 1))
                for cc in range(CC):
                    sl = slice(cc * N, (cc + 1) * N)
                    mm = t.matmul(qb[0:1, :], s_onescb[:], sq_src[:, sl],
                                  start=(cc == 0), stop=(cc == CC - 1))
                mm.then_inc(sem["tB"], 1)

            def t_ln_bcast(mb, qb, base, vst_n, srs_n):
                t.wait_ge(sem["vst"], vst_n)
                t.matmul(mb[:], s_ones1[:], s_stat[0:1, base:base + N],
                         start=True, stop=True).then_inc(sem["tB"], 1)
                t.wait_ge(sem["srs"], srs_n)
                t.matmul(qb[:], s_ones1[:], s_stat[0:1, base + N:base + 2 * N],
                         start=True, stop=True).then_inc(sem["tB"], 1)

            def t_ln1(g):
                t_ln_stats(g, [s_x1[:, cc * N:(cc + 1) * N] for cc in range(CC)],
                           s_usq1, pb[1], pb[2], VUB1[g],
                           VST1[g - 1] if g >= 1 else 0,
                           SBC1[g - 1] if g >= 1 else 0)
                t_ln_bcast(pb[1], pb[2], 0, VST1[g], SRS1[g])

            def t_ln2(g):
                u2aps = ([s_u2[:, cc * N:(cc + 1) * N] for cc in range(CC)]
                         if g % 2 == 0 else list(s_u2b))
                t_ln_stats(g, u2aps, s_usq1, pb[3], pb[4], g + 1,
                           VST2[g - 1] if g >= 1 else 0,
                           SBC2[g - 1] if g >= 1 else 0, usem="vu2")
                t_ln_bcast(pb[3], pb[4], 3 * N, VST2[g], SRS2[g])

            t_ps(0)
            t_ln1(0)
            for gi in range(NG):
                if gi < NG - 1:
                    t_ps(gi + 1)
                t.wait_ge(sem["vmxr"], gi + 1)
                fi = 0
                for dc in range(CC):
                    if fi < 3:
                        if 8 * gi + fi - 2 > 0:
                            t.wait_ge(sem["vgt"], 8 * gi + fi - 2)
                    else:
                        t.wait_ge(sem["pfc"], 16 * gi + fi - 2)
                    for cc in range(CC):
                        mm = t.matmul(pb[5 + fi % 3][:], wffn_ap(s_wr, cc, dc),
                                      s_xrf[:, cc * N:(cc + 1) * N],
                                      start=(cc == 0), stop=(cc == CC - 1))
                    mm.then_inc(sem["tB"], 1)
                    fi += 1
                if gi >= 1:
                    t_ln2(gi - 1)
                t.wait_ge(sem["vmxk"], gi + 1)
                for dc in range(CC):
                    t.wait_ge(sem["pfc"], 16 * gi + fi - 2)
                    for cc in range(CC):
                        mm = t.matmul(pb[5 + fi % 3][:], wffn_ap(s_wk, cc, dc),
                                      s_xkf[:, cc * N:(cc + 1) * N],
                                      start=(cc == 0), stop=(cc == CC - 1))
                    mm.then_inc(sem["tB"], 1)
                    fi += 1
                if gi < NG - 1:
                    t_ln1(gi + 1)
                for cc in range(CC):
                    if cc == 0:
                        t.wait_ge(sem["pfc"], 16 * gi + 16)
                    elif cc in (1, 2):
                        pass
                    else:
                        t.wait_ge(sem["vgt"], 8 * gi + fi - 18)
                    for dc in range(CC):
                        mm = t.matmul(pb[5 + fi % 3][:], wffn_ap(s_wv, dc, cc),
                                      s_kact[:, dc * N:(dc + 1) * N],
                                      start=(dc == 0), stop=(dc == CC - 1))
                    mm.then_inc(sem["tB"], 1)
                    fi += 1
            t_ln2(NG - 1)

        @block.scalar
        def _(s):
            s.wait_ge(sem["dsm"], 16 * 11)
            s.activation(s_A[:], s_alog[:], AF.Exp)
            s.memzero(s_junks[:])
            s.memzero(s_junks[:]).then_inc(sem["ssa"], 1)
            for gi in range(NG):
                for jc in range(JI):
                    s.wait_ge(sem["tpa"], gi * JC + jc + 1)
                    if gi > 0 and jc == 0:
                        s.wait_ge(sem["tpo"], gi * CC)
                    s.activation(s_xin[:, jc * N:(jc + 1) * N], pb[jc % 2][:],
                                 AF.Silu, bias=s_bin[:, jc:jc + 1]
                                 ).then_inc(sem["sea"], 1)
                for dc in range(CC):
                    j = gi * CC + dc
                    s.wait_ge(sem["tpd"], j + 1)
                    if j >= 2:
                        s.wait_ge(sem["tpm"], j - 1)
                    dl = s_del0 if dc % 2 == 0 else s_del1
                    s.activation(dl[:], pb[2 + dc % 2][:], AF.Exp,
                                 bias=s_bdt[:, dc:dc + 1])
                    s.activation(dl[:], dl[:], AF.Ln,
                                 bias=s_c1[:]).then_inc(sem["sdt"], 1)
                for jc in range(JI, JC):
                    s.wait_ge(sem["tpa"], gi * JC + jc + 1)
                    if gi * JI + (jc - JI) >= 2:
                        s.wait_ge(sem["vz"], gi * JI + (jc - JI) - 1)
                    dst = (s_sres0 if jc % 2 == 0 else s_sres1)[:]
                    s.activation(dst, pb[jc % 2][:], AF.Silu,
                                 bias=s_bin[:, jc:jc + 1]).then_inc(sem["sea"], 1)
                s.wait_ge(sem["tdb"], gi + 1)
                s.wait_ge(sem["vsc"], gi)
                s.activation(s_a[:], pb[5][0:DS, :], AF.Exp,
                             scale=s_A[:]).then_inc(sem["ssa"], 1)
                for dc in range(CC):
                    s.wait_ge(sem["tpo"], gi * CC + dc + 1)
                    if gi > 0:
                        h = 0 if dc < 4 else 1
                        s.wait_ge(sem["dso"], 16 * (2 * (gi - 1) + h + 1))
                    buf = s_sp0 if dc < 4 else s_sp1
                    s.activation(buf[:, (dc % 4) * N:(dc % 4 + 1) * N],
                                 pb[6 + dc % 2][:], AF.Copy).then_inc(sem["sso"], 1)
            if stage < 2:
                return
            s.wait_ge(sem["vbnd"], 2)
            s.activation(s_junk[0:1, 24:25], s_stt2[0:1, 1:2], AF.Ln,
                         bias=s_ceps[0:1, 0:1])
            s.memzero(s_junks[:])
            s.memzero(s_junks[:])
            s.activation(s_stt2[0:1, 1:2], s_junk[0:1, 24:25], AF.Exp,
                         scale=s_cm05[0:1, 0:1])
            s.memzero(s_junks[:])
            s.memzero(s_junks[:]).then_inc(sem["ssa"], 1)
            if stage < 3:
                return

            def s_rstd(vst_n, base):
                s.wait_ge(sem["vst"], vst_n)
                s.activation(s_stat[0:1, base + 2 * N:base + 3 * N],
                             s_stat[0:1, base + 2 * N:base + 3 * N],
                             AF.Ln, bias=s_ceps[0:1, 0:1])
                s.activation(s_stat[0:1, base + N:base + 2 * N],
                             s_stat[0:1, base + 2 * N:base + 3 * N],
                             AF.Exp, scale=s_cm05[0:1, 0:1])
                s.memzero(s_junks[:])
                s.memzero(s_junks[:]).then_inc(sem["srs"], 1)

            def s_evac(tb_m, tb_r, mb, qb, dst_m, dst_r):
                s.wait_ge(sem["tB"], tb_m)
                s.activation(dst_m[:], mb[:], AF.Copy)
                s.wait_ge(sem["tB"], tb_r)
                s.activation(dst_r[:], qb[:], AF.Copy)
                s.memzero(s_junks[:])
                s.memzero(s_junks[:]).then_inc(sem["sbc"], 2)

            s_rstd(VST1[0], 0)
            s_evac(TB_LN1BM[0], TB_LN1BR[0], pb[1], pb[2], s_sres0, s_sres1)
            for gi in range(NG):
                for dc in range(CC):
                    s.wait_ge(sem["tB"], TB_WR[gi][dc])
                    s.activation(s_r[:, dc * N:(dc + 1) * N], pb[5 + dc % 3][:],
                                 AF.Sigmoid).then_inc(sem["pfc"], 1)
                if gi >= 1:
                    s_rstd(VST2[gi - 1], 3 * N)
                    s_evac(TB_LN2BM[gi - 1], TB_LN2BR[gi - 1], pb[3], pb[4],
                           s_gt0, s_gt1)
                for dc in range(CC):
                    s.wait_ge(sem["tB"], TB_WK[gi][dc])
                    ka = s_kact[:, dc * N:(dc + 1) * N]
                    s.activation(ka, pb[5 + (8 + dc) % 3][:], AF.Relu)
                    s.activation(ka, ka, AF.Square).then_inc(sem["pfc"], 1)
                if gi < NG - 1:
                    s_rstd(VST1[gi + 1], 0)
                    s_evac(TB_LN1BM[gi + 1], TB_LN1BR[gi + 1], pb[1], pb[2],
                           s_sres0, s_sres1)
                s.wait_ge(sem["vgt"], 8 * gi + 8)
                for dc in range(CC):
                    src = (s_u2[:, dc * N:(dc + 1) * N] if gi % 2 == 0
                           else s_u2b[dc])
                    s.activation(s_usq1[:, dc * N:(dc + 1) * N], src, AF.Square)
                s.memzero(s_junks[:])
                s.memzero(s_junks[:]).then_inc(sem["vu2"], 1)
            s_rstd(VST2[NG - 1], 3 * N)
            s_evac(TB_LN2BM[NG - 1], TB_LN2BR[NG - 1], pb[3], pb[4],
                   s_gt0, s_gt1)

        @block.vector
        def _(v):
            def pad():
                v.memset(s_junk[:, 64:96], 0.0)

            v.wait_ge(sem["dsm"], 16 * 11)
            v.memset(s_ones1[:], 1.0)
            v.memset(s_ones1b[:], 1.0)
            v.memset(s_ones1_64[:], 1.0)
            v.memset(s_ones64c[:], 1.0)
            v.memset(s_onescb[:], 1.0)
            v.memset(s_onescf[:], 1.0)
            v.memset(s_c1[:], 1.0)
            v.memset(s_ceps[:], 1e-5)
            v.memset(s_cm05[:], -0.5)
            v.memset(s_pay[:], 0.0)
            v.memset(s_junk[:], 0.0)
            v.wait_ge(sem["ssa"], 1)
            v.tensor_scalar(s_A[:], s_A[:], -1.0, None, ALU.mult)
            v.tensor_scalar(s_dp1[:], s_dp1[:], 1.0, None, ALU.add)
            pad()
            v.engine_nop().then_inc(sem["vin"], 1)

            for gi in range(NG):
                v.wait_ge(sem["tpx"], gi + 1)
                v.wait_ge(sem["tpd"], CC * (gi + 1))
                for jc in range(JI):
                    v.wait_ge(sem["sea"], gi * JC + JI + jc + 1)
                    sr = s_sres0 if jc % 2 == 0 else s_sres1
                    v.tensor_mul(s_xin[:, jc * N:(jc + 1) * N],
                                 s_xin[:, jc * N:(jc + 1) * N], sr[:]
                                 ).then_inc(sem["vz"], 1)
                v.wait_ge(sem["tpm"], CC * (gi + 1))
                v.tensor_scalar(s_dsb[:], pb[5][0:1, :], 1.0 / C, None,
                                ALU.mult)
                pad(); pad()
                v.engine_nop().then_inc(sem["vd"], 1)
                v.wait_ge(sem["tpx"], gi + 1)
                v.tensor_scalar(s_bx[:], pb[4][0:DS, :], s_bxp[:], None,
                                ALU.add).then_inc(sem["vbx"], 1)
                v.wait_ge(sem["ssa"], gi + 2)
                sbf = s_s0 if gi % 2 == 0 else s_s1
                pbf = s_s1 if gi % 2 == 0 else s_s0
                if gi >= 2:
                    v.wait_ge(sem["tss"], gi - 1)
                init_s = 0.0 if gi == 0 else pbf[:, N - 1:N]
                init_g = 1.0 if gi == 0 else s_G[:, gi * N - 1:gi * N]
                v.tensor_tensor_scan(sbf[:], s_a[:], s_bx[:], init_s,
                                     ALU.mult, ALU.add)
                v.tensor_tensor_scan(s_G[:, gi * N:(gi + 1) * N], s_a[:],
                                     s_bx[:], init_g, ALU.mult, ALU.bypass)
                v.engine_nop().then_inc(sem["vsc"], 1)
                v.wait_ge(sem["tss"], gi + 1)
                v.tensor_copy(s_ssum[0:1, gi * N:(gi + 1) * N], pb[5][0:1, :])
                pad()
                v.engine_nop().then_inc(sem["vss"], 1)

            if stage < 2:
                return
            sbf7 = s_s0 if (NG - 1) % 2 == 0 else s_s1
            v.wait_ge(sem["sso"], CC * NG)
            v.tensor_copy(s_pay[0:DS, 0:1], sbf7[:, N - 1:N])
            v.wait_ge(sem["tss"], NG + 1)
            v.tensor_copy(s_slastc[:], pb[5][:, 0:1])
            pad(); pad()
            for cc in range(CC):
                sp = s_sp0 if cc < 4 else s_sp1
                col = (cc % 4) * N + (N - 1)
                v.tensor_scalar(s_pay[:, 1 + cc:2 + cc], sp[:, col:col + 1],
                                s_slastc[:], None, ALU.mult)
            pad(); pad(); pad()
            v.engine_nop().then_inc(sem["vpay"], 1)
            v.wait_ge(sem["dgat"], 16)
            v.tensor_mul(s_gath[:], s_gath[:], s_rmask[:])
            pad(); pad()
            half = NCORES * 9 // 2
            v.tensor_add(s_gath[:, 0:half], s_gath[:, 0:half], s_gath[:, half:])
            pad(); pad()
            v.tensor_add(s_gath[:, 0:18], s_gath[:, 0:18], s_gath[:, 18:36])
            pad(); pad()
            v.tensor_add(s_sel[:, 0:9], s_gath[:, 0:9], s_gath[:, 9:18])
            pad(); pad()
            v.tensor_copy(s_hinb[:], s_sel[0:DS, 0:1])
            pad(); pad()
            v.engine_nop().then_inc(sem["vpay"], 1)
            for gi in range(NG):
                v.wait_ge(sem["tcr"], gi + 1)
                sl = slice(gi * N, (gi + 1) * N)
                if gi < NG - 1:
                    v.tensor_add(s_ssum[0:1, sl], s_ssum[0:1, sl],
                                 pb[6 + gi % 2][0:1, :]).then_inc(sem["vcor"], 1)
                else:
                    v.tensor_add(s_ssum[0:1, sl], s_ssum[0:1, sl],
                                 pb[6 + gi % 2][0:1, :])
                    pad(); pad()
                    v.engine_nop().then_inc(sem["vcor"], 1)
            v.tensor_mul(s_ub[:], s_xbT[:], s_dp1[:])
            pad(); pad()
            v.tensor_add(s_ub[:], s_ub[:], s_bout[:])
            pad(); pad()
            v.tensor_add(s_ub[:], s_ub[:], s_sel[:, 1:9])
            pad(); pad()
            v.tensor_copy(s_junk[:, 0:CC], s_ub[:])
            v.tensor_mul(s_junk[:, CC:2 * CC], s_ub[:], s_ub[:])
            pad(); pad()
            v.engine_nop().then_inc(sem["vbnd"], 1)
            v.wait_ge(sem["tB"], 1)
            v.reduce_sum(s_stt2[0:1, 0:1], pb[5][0:1, 0:CC], axis=AX)
            v.reduce_sum(s_stt2[0:1, 1:2], pb[5][0:1, CC:2 * CC], axis=AX)
            pad(); pad()
            v.tensor_scalar(s_stt2[0:1, 0:2], s_stt2[0:1, 0:2], 1.0 / C, None,
                            ALU.mult)
            pad(); pad()
            v.tensor_mul(s_junk[0:1, 16:17], s_stt2[0:1, 0:1], s_stt2[0:1, 0:1])
            pad(); pad()
            v.tensor_sub(s_stt2[0:1, 1:2], s_stt2[0:1, 1:2], s_junk[0:1, 16:17])
            pad(); pad()
            v.engine_nop().then_inc(sem["vbnd"], 1)
            v.wait_ge(sem["ssa"], NG + 2)
            v.wait_ge(sem["tB"], 2)
            v.tensor_copy(s_junk[:, 32:34], pb[5][:, 2:4])
            pad(); pad()
            v.tensor_scalar(s_ub[:], s_ub[:], s_junk[:, 32:33],
                            s_junk[:, 33:34], ALU.subtract, ALU.mult)
            pad(); pad()
            v.tensor_scalar(s_prevc[:], s_ub[:], s_hmask[:], None, ALU.mult)
            pad(); pad()
            v.engine_nop().then_inc(sem["vbnd"], 1)

            if stage < 3:
                return

            def v_u_steps(g):
                xb = s_xb0 if g % 2 == 0 else s_xb1
                v.wait_ge(sem["dxb"], 16 * (g + 1))
                v.wait_ge(sem["dmb"], 16 * (g + 1))
                v.wait_ge(sem["tB"], TB_PS[g])
                for cc in range(CC):
                    sl = slice(cc * N, (cc + 1) * N)
                    v.tensor_mul(s_mmb[:, sl], s_mmb[:, sl], pb[0][:])
                for cc in range(CC):
                    sl = slice(cc * N, (cc + 1) * N)
                    v.scalar_tensor_tensor(s_x1[:, sl], xb[:, sl],
                                           s_dp1[:, cc:cc + 1], s_mmb[:, sl],
                                           ALU.mult, ALU.add)
                for cc in range(CC):
                    sl = slice(cc * N, (cc + 1) * N)
                    v.tensor_scalar(s_x1[:, sl], s_x1[:, sl],
                                    s_bout[:, cc:cc + 1], None, ALU.add)
                v.engine_nop().then_inc(sem["vB"], 1)

            def v_usq(src, dst):
                for cc in range(CC):
                    sl = slice(cc * N, (cc + 1) * N)
                    v.tensor_mul(dst[:, sl], src[:, sl], src[:, sl])
                v.engine_nop().then_inc(sem["vub"], 1)

            def v_ln_vals(tb_n, base, mbank, qbank):
                v.wait_ge(sem["tB"], tb_n)
                v.tensor_scalar(s_stat[0:1, base:base + N], mbank[0:1, :],
                                1.0 / C, None, ALU.mult)
                v.tensor_mul(s_stat[0:1, base + 2 * N:base + 3 * N],
                             s_stat[0:1, base:base + N],
                             s_stat[0:1, base:base + N])
                v.scalar_tensor_tensor(s_stat[0:1, base + 2 * N:base + 3 * N],
                                       qbank[0:1, :], 1.0 / C,
                                       s_stat[0:1, base + 2 * N:base + 3 * N],
                                       ALU.mult, ALU.subtract)
                pad(); pad()
                v.engine_nop().then_inc(sem["vst"], 2)

            def v_apply(g):
                v.wait_ge(sem["sbc"], SBC1[g])
                for cc in range(CC):
                    sl = slice(cc * N, (cc + 1) * N)
                    v.tensor_sub(s_x1[:, sl], s_x1[:, sl], s_sres0[:])
                for cc in range(CC):
                    sl = slice(cc * N, (cc + 1) * N)
                    v.tensor_mul(s_x1[:, sl], s_x1[:, sl], s_sres1[:])

            def v_mixr(g):
                for cc in range(CC):
                    o = cc * N
                    v.tensor_sub(s_diff[:, o + 1:o + N], s_x1[:, o + 1:o + N],
                                 s_x1[:, o:o + N - 1])
                for cc in range(CC):
                    o = cc * N
                    v.tensor_sub(s_diff[:, o:o + 1], s_x1[:, o:o + 1],
                                 s_prevc[:, cc:cc + 1])
                for cc in range(CC):
                    o = cc * N
                    v.scalar_tensor_tensor(s_xrf[:, o + 1:o + N],
                                           s_diff[:, o + 1:o + N],
                                           s_mur[:, cc:cc + 1],
                                           s_x1[:, o:o + N - 1],
                                           ALU.mult, ALU.add)
                for cc in range(CC):
                    o = cc * N
                    v.scalar_tensor_tensor(s_xrf[:, o:o + 1], s_diff[:, o:o + 1],
                                           s_mur[:, cc:cc + 1],
                                           s_prevc[:, cc:cc + 1],
                                           ALU.mult, ALU.add)
                pad()
                v.engine_nop().then_inc(sem["vmxr"], 1)

            def v_mixk(g):
                for cc in range(CC):
                    o = cc * N
                    v.scalar_tensor_tensor(s_xkf[:, o:o + 1], s_diff[:, o:o + 1],
                                           s_muk[:, cc:cc + 1],
                                           s_prevc[:, cc:cc + 1],
                                           ALU.mult, ALU.add)
                for cc in range(CC):
                    o = cc * N
                    v.scalar_tensor_tensor(s_xkf[:, o + 1:o + N],
                                           s_diff[:, o + 1:o + N],
                                           s_muk[:, cc:cc + 1],
                                           s_x1[:, o:o + N - 1],
                                           ALU.mult, ALU.add)
                for cc in range(CC):
                    o = cc * N
                    v.tensor_copy(s_prevc[:, cc:cc + 1], s_x1[:, o + N - 1:o + N])
                pad()
                v.engine_nop().then_inc(sem["vmxk"], 1)

            def u2ap(g, cc):
                return (s_u2[:, cc * N:(cc + 1) * N] if g % 2 == 0
                        else s_u2b[cc])

            def v_out(g):
                v.wait_ge(sem["sbc"], SBC2[g])
                for cc in range(CC):
                    k = g * CC + cc
                    if k >= 4:
                        v.wait_ge(sem["dod"], 16 * (k - 3))
                    ost = (s_ost0, s_ost1, s_ost2, s_ost3)[cc % 4]
                    v.tensor_sub(ost[:], u2ap(g, cc), s_gt0[:])
                    v.tensor_mul(ost[:], ost[:], s_gt1[:]).then_inc(sem["vod"], 1)

            v.wait_ge(sem["dw"], 16 * 12)
            v_u_steps(0)
            v_usq(s_x1, s_usq1)
            v_ln_vals(TB_LN1S[0], 0, pb[1], pb[2])
            v_apply(0)
            if dbg:
                pad()
                v.engine_nop().then_inc(sem["vdbg"], 1)
                v.wait_ge(sem["ddbg"], 16)
            v.wait_ge(sem["dso"], 16 * 2 * NG)
            v_mixr(0)
            v_mixk(0)
            for gi in range(NG):
                for cc in range(CC):
                    v.tensor_copy(u2ap(gi, cc), s_x1[:, cc * N:(cc + 1) * N])
                if gi < NG - 1:
                    v_u_steps(gi + 1)
                    v_usq(s_x1, s_usq1)
                if gi >= 1:
                    v_out(gi - 1)
                if gi < NG - 1:
                    v_ln_vals(TB_LN1S[gi + 1], 0, pb[1], pb[2])
                if dbg and gi == 0:
                    v.wait_ge(sem["pfc"], 16)
                    pad()
                    v.engine_nop().then_inc(sem["vdbg"], 2)
                    v.wait_ge(sem["ddbg"], 48)
                def v_gate(cc):
                    gt = s_gt0 if cc % 2 == 0 else s_gt1
                    o = cc * N
                    v.wait_ge(sem["tB"], TB_WV[gi][cc])
                    v.tensor_mul(gt[:], s_r[:, o:o + N], pb[5 + (16 + cc) % 3][:])
                    v.tensor_add(u2ap(gi, cc), u2ap(gi, cc), gt[:]
                                 ).then_inc(sem["vgt"], 1)

                def v_chunk(cc):
                    o = cc * N
                    if cc == 0:
                        v.wait_ge(sem["sbc"], SBC1[gi + 1])
                    v.tensor_sub(s_x1[:, o:o + N], s_x1[:, o:o + N],
                                 s_sres0[:])
                    v.tensor_mul(s_x1[:, o:o + N], s_x1[:, o:o + N],
                                 s_sres1[:])
                    v.tensor_sub(s_diff[:, o + 1:o + N],
                                 s_x1[:, o + 1:o + N],
                                 s_x1[:, o:o + N - 1])
                    v.tensor_sub(s_diff[:, o:o + 1], s_x1[:, o:o + 1],
                                 s_prevc[:, cc:cc + 1])
                    v.scalar_tensor_tensor(s_xrf[:, o + 1:o + N],
                                           s_diff[:, o + 1:o + N],
                                           s_mur[:, cc:cc + 1],
                                           s_x1[:, o:o + N - 1],
                                           ALU.mult, ALU.add)
                    v.scalar_tensor_tensor(s_xrf[:, o:o + 1],
                                           s_diff[:, o:o + 1],
                                           s_mur[:, cc:cc + 1],
                                           s_prevc[:, cc:cc + 1],
                                           ALU.mult, ALU.add)

                if gi < NG - 1:
                    for cc in range(5):
                        v_gate(cc)
                        v_chunk(cc)
                    for cc in range(5, CC):
                        v_chunk(cc)
                    pad()
                    v.engine_nop().then_inc(sem["vmxr"], 1)
                    for cc in range(5, CC):
                        v_gate(cc)
                    v.wait_ge(sem["tB"], TB_WK[gi][CC - 1])
                    v_mixk(gi + 1)
                else:
                    for cc in range(CC):
                        v_gate(cc)
                if dbg and gi == 0:
                    pad()
                    v.engine_nop().then_inc(sem["vdbg"], 1)
                    v.wait_ge(sem["ddbg"], 64)
                v_ln_vals(TB_LN2S[gi], 3 * N, pb[3], pb[4])
            v_out(NG - 1)

    nc.finalize()
    return nc


_NC_CACHE = {}


def _get_nc():
    if "nc" not in _NC_CACHE:
        _NC_CACHE["nc"] = build_nc()
    return _NC_CACHE["nc"]


def _chunk_weight(w, kchunks):
    K, M = w.shape
    assert K == kchunks * 128
    return np.ascontiguousarray(
        w.reshape(kchunks, 128, M).transpose(1, 0, 2).reshape(128, kchunks * M))


def _col_tile(vec, ncol):
    return np.ascontiguousarray(vec.reshape(ncol, 128).T)


def stage_inputs(L0_attn_x, L0_attn_W_in, L0_attn_b_in, L0_attn_W_xp,
                 L0_attn_b_xp, L0_attn_W_dt, L0_attn_b_dt, L0_attn_W_out,
                 L0_attn_b_out, L0_attn_A_log, L0_attn_D, L0_ffn_W_r,
                 L0_ffn_W_k, L0_ffn_W_v, L0_ffn_mu_r, L0_ffn_mu_k, T=T_FULL):
    f = np.float32
    x = np.asarray(L0_attn_x, f)
    w_in = np.asarray(L0_attn_W_in, f)
    win = np.ascontiguousarray(
        w_in.reshape(CC, 128, JC, 128).transpose(1, 2, 0, 3).reshape(128, -1))
    wdt = _chunk_weight(np.asarray(L0_attn_W_dt, f), JI)
    wxp = _chunk_weight(np.asarray(L0_attn_W_xp, f), JI)
    wout = _chunk_weight(np.asarray(L0_attn_W_out, f), JI)
    wr = _chunk_weight(np.ascontiguousarray(np.asarray(L0_ffn_W_r, f).T), CC)
    wk = _chunk_weight(np.ascontiguousarray(np.asarray(L0_ffn_W_k, f).T), CC)
    wv = _chunk_weight(np.ascontiguousarray(np.asarray(L0_ffn_W_v, f).T), CC)
    bin_t = _col_tile(np.asarray(L0_attn_b_in, f), JC)
    bdt_t = _col_tile(np.asarray(L0_attn_b_dt, f), CC)
    bout_t = _col_tile(np.asarray(L0_attn_b_out, f), CC)
    bxp_t = np.ascontiguousarray(np.asarray(L0_attn_b_xp, f).reshape(DS, 1))
    alog_t = np.ascontiguousarray(np.asarray(L0_attn_A_log, f).reshape(DS, 1))
    dvec_t = _col_tile(np.asarray(L0_attn_D, f), CC)
    mur_t = _col_tile(np.asarray(L0_ffn_mu_r, f), CC)
    muk_t = _col_tile(np.asarray(L0_ffn_mu_k, f), CC)

    shared = dict(win=win, wdt=wdt, wxp=wxp, wout=wout, wr=wr, wk=wk, wv=wv,
                  bin=bin_t, bdt=bdt_t, bout=bout_t, bxp=bxp_t, alog=alog_t,
                  dvec=dvec_t, mur=mur_t, muk=muk_t)

    in_maps = []
    for c in range(NCORES):
        b, h = c // 2, c % 2
        xT = np.ascontiguousarray(x[b, h * T:(h + 1) * T, :].T)
        if h == 1:
            xbT = _col_tile(x[b, T - 1, :], CC)
            hmask = np.ones((128, 1), f)
            rmask = np.zeros((128, NCORES * 9), f)
            rmask[:, (c - 1) * 9:c * 9] = 1.0
        else:
            xbT = np.zeros((128, CC), f)
            hmask = np.zeros((128, 1), f)
            rmask = np.zeros((128, NCORES * 9), f)
        in_maps.append(dict(shared, xT=xT, xbT=xbT, hmask=hmask, rmask=rmask))
    return in_maps


def kernel(**inputs):
    in_maps = stage_inputs(**inputs)
    nc = _get_nc()
    res = run_bass_kernel_spmd(nc, in_maps, core_ids=list(range(NCORES)))
    f = np.float32
    out = np.empty((B_, S_, C), f)
    for c in range(NCORES):
        b, h = c // 2, c % 2
        out[b, h * T_FULL:(h + 1) * T_FULL, :] = np.asarray(
            res.results[c]["outT"], f).T
    return out
